# version 18
# baseline (speedup 1.0000x reference)
import numpy as np
from contextlib import ExitStack

import concourse.bass as bass
import concourse.mybir as mybir
from concourse.bass_utils import run_bass_kernel_spmd

F32 = mybir.dt.float32
F32R = mybir.dt.float32r
BF16 = mybir.dt.bfloat16
F8 = mybir.dt.float8e4
AF = mybir.ActivationFunctionType
ALU = mybir.AluOpType
AX = mybir.AxisListType

P = 128
D = 2048
TT = 16
DT = 16
H = 16
E = 128
NCH = 16
MS = 12288
NMS = 48
MSW = 256
HID = 8192
NS = 64
NQ = 4
QW = 512
TSUP = 4
LNEPS = 1e-5
RMSEPS = 1e-6


class Sem:
    def __init__(self, nc, es, name):
        self.h = es.enter_context(nc.semaphore(name))
        self.n = 0
        self.name = name


class Prog:
    ENGS = ("tensor", "vector", "scalar", "sync", "gpsimd")

    def __init__(self, nc):
        self.nc = nc
        self.q = {e: [] for e in self.ENGS}
        self.last_wait = {}

    def _wait(self, eng, sem, val):
        if val <= 0:
            return
        key = (eng, sem.name)
        if self.last_wait.get(key, 0) >= val:
            return
        self.last_wait[key] = val
        self.q[eng].append(lambda e, h=sem.h, v=val: e.wait_ge(h, v))

    def op(self, eng, fn, waits=(), inc=None, inc_k=1):
        for sem, val in waits:
            self._wait(eng, sem, val)
        if inc is not None:
            self.q[eng].append(lambda e, fn=fn, h=inc.h, k=inc_k: fn(e).then_inc(h, k))
            inc.n += inc_k
            return inc.n
        self.q[eng].append(fn)
        return None

    def barrier(self, sems):
        for eng in self.ENGS:
            for s in sems:
                self._wait(eng, s, s.n)

    def flush(self):
        with self.nc.Block() as block:
            for eng in self.ENGS:
                ops = self.q[eng]
                if ops:
                    getattr(block, eng)(lambda e, ops=ops: [f(e) for f in ops])
        self.q = {e: [] for e in self.ENGS}


def build(debug=False):
    nc = bass.Bass("TRN2", target_bir_lowering=False)
    import ml_dtypes

    x_d = nc.dram_tensor("x", [TT, P, D], F32, kind="ExternalInput").ap()
    wmod_d = nc.dram_tensor("wmod", [NMS, DT, P, MSW], F32, kind="ExternalInput").ap()
    ccol_d = nc.dram_tensor("ccol", [P, DT], F32, kind="ExternalInput").ap()
    wqkv_d = nc.dram_tensor("wqkv", [H, 3, DT, P, E], F32, kind="ExternalInput").ap()
    wf_d = nc.dram_tensor("wf", [DT, P, H], F32, kind="ExternalInput").ap()
    wo_d = nc.dram_tensor("wo", [DT, P, D], F32, kind="ExternalInput").ap()
    nw_d = nc.dram_tensor("nwcol", [P, DT], F32, kind="ExternalInput").ap()
    wg_d = nc.dram_tensor("wg", [NS, 2, DT, P, E], F32, kind="ExternalInput").ap()
    wl_d = nc.dram_tensor("wl", [NQ, NS, P, QW], F32, kind="ExternalInput").ap()
    y_d = nc.dram_tensor("y", [TT, P, D], F32, kind="ExternalOutput").ap()

    x1_hbm = nc.dram_tensor("x1_scratch", [TT, P, D], F32, kind="Internal").ap()
    g_hbm = nc.dram_tensor("g_scratch", [2, 1, D], BF16, kind="Internal").ap()
    act_hbm = nc.dram_tensor("act_scratch", [TSUP, NS // 2, P, 2 * QW], F8, kind="Internal").ap()

    eye_bf_d = nc.inline_tensor(np.eye(P, dtype=ml_dtypes.bfloat16), "eye_bf_c")
    eye16_d = nc.inline_tensor(np.eye(16, dtype=np.float32), "eye16_c")
    tri_np = np.triu(np.ones((P, P), dtype=np.float32))
    tri_d = nc.inline_tensor(tri_np, "tri_c")
    mask_d = nc.inline_tensor(tri_np.astype(ml_dtypes.bfloat16), "mask_c")
    ones_f_d = nc.inline_tensor(np.ones((1, P), dtype=np.float32), "ones_f_c")
    ones_b_d = nc.inline_tensor(np.ones((1, P), dtype=ml_dtypes.bfloat16), "ones_b_c")
    onescol_d = nc.inline_tensor(np.ones((P, 1), dtype=np.float32), "ones_col_c")
    onescol_b_d = nc.inline_tensor(np.ones((P, 1), dtype=ml_dtypes.bfloat16), "ones_colb_c")
    constvals = [LNEPS, RMSEPS]
    constv_d = nc.inline_tensor(
        np.broadcast_to(np.asarray(constvals, np.float32), (P, len(constvals))).copy(),
        "constv_c")

    dbg = {}
    es = ExitStack()

    def sbl(name, shape, dt):
        return es.enter_context(nc.sbuf_tensor(name, shape, dt, side="left"))

    eye_bf = sbl("eye_bf", [P, P], BF16)
    eye16 = sbl("eye16", [16, 16], F32)
    tri = sbl("tri", [P, P], F32)
    maskb = sbl("maskb", [P, P], BF16)
    ones_f = sbl("ones_f", [1, P], F32)
    ones_b = sbl("ones_b", [1, P], BF16)
    sacol = sbl("sacol", [P, DT], F32)
    cacol = sbl("cacol", [P, DT], F32)
    sbcol = sbl("sbcol", [P, DT], F32)
    cbcol = sbl("cbcol", [P, DT], F32)
    mean1 = sbl("mean1", [P, TT], F32)
    rstd1 = sbl("rstd1", [P, TT], F32)
    mean2 = sbl("mean2", [P, TT], F32)
    rstd2 = sbl("rstd2", [P, TT], F32)
    msum = sbl("msum", [P, TT], F32)
    sqsum = sbl("sqsum", [P, TT], F32)
    rmsr = sbl("rmsr", [P, TT], F32)

    e1t = sbl("e1t", [P, NCH * H], F32)
    e2t = sbl("e2t", [P, NCH * H], F32)
    e3t = sbl("e3t", [P, (NCH - 1) * H], F32)
    e4bc = sbl("e4bc", [P, (NCH - 1) * H], F32)

    Sbf = sbl("Sbf", [P, E], BF16)
    tmp1 = sbl("tmp1", [P, 2 * TT], F32)
    nwcol = sbl("nwcol_sb", [P, DT], F32)

    constv = sbl("constv", [P, 2], F32)
    onescol = sbl("onescol", [P, 1], F32)
    onescol_b = sbl("onescol_b", [P, 1], BF16)
    totrow = sbl("totrow", [1, NCH * H], F32)
    nc.const_aps.register(F32, LNEPS, constv[:, 0:1])
    nc.const_aps.register(F32, RMSEPS, constv[:, 1:2])

    sT = Sem(nc, es, "sT")
    sV = Sem(nc, es, "sV")
    sS = Sem(nc, es, "sS")
    dA = Sem(nc, es, "dA")
    dB = Sem(nc, es, "dB")
    dC = Sem(nc, es, "dC")
    dE = Sem(nc, es, "dE")
    dF = Sem(nc, es, "dF")
    dG = Sem(nc, es, "dG")
    dOut = Sem(nc, es, "dOut")
    dOut2 = Sem(nc, es, "dOut2")
    dH = Sem(nc, es, "dH")
    dI = Sem(nc, es, "dI")
    dJ = Sem(nc, es, "dJ")
    dK = Sem(nc, es, "dK")
    dL = Sem(nc, es, "dL")
    dM = Sem(nc, es, "dM")
    dN = Sem(nc, es, "dN")
    dO = Sem(nc, es, "dO")
    dP = Sem(nc, es, "dP")
    dQ = Sem(nc, es, "dQ")
    ALLSEMS = [sT, sV, sS, dA, dB, dC, dE, dF, dG, dOut, dOut2, dH, dI, dJ, dK,
               dL, dM, dN, dO, dP, dQ]

    pr = Prog(nc)

    def dma(qeng, dst, src, sem, waits=()):
        return pr.op(qeng, lambda e, d=dst, s=src: e.dma_start(out=d, in_=s),
                     waits=waits, inc=sem, inc_k=16)

    def BARRIER():
        pr.barrier(ALLSEMS)

    big1 = ExitStack()
    h1T = big1.enter_context(nc.sbuf_tensor("h1T", [P, DT * D], BF16, side="left"))
    oT = big1.enter_context(nc.sbuf_tensor("oT", [P, H * D], BF16, side="left"))

    _sc1 = ExitStack()
    wmbufs = [_sc1.enter_context(nc.sbuf_tensor(f"wmbuf{i}", [P, DT * MSW], BF16,
                                                side="right")) for i in range(4)]
    ccol = _sc1.enter_context(nc.sbuf_tensor("ccol_sb", [P, DT], F32, side="right"))
    silc = _sc1.enter_context(nc.sbuf_tensor("silc", [P, DT], F32, side="right"))
    silcb = _sc1.enter_context(nc.sbuf_tensor("silcb", [P, DT], BF16, side="right"))
    rowtA = _sc1.enter_context(nc.sbuf_tensor("rowtA", [1, MSW], F32, side="right"))
    rowtB = _sc1.enter_context(nc.sbuf_tensor("rowtB", [1, MSW], F32, side="right"))
    rowsgn = _sc1.enter_context(nc.sbuf_tensor("rowsgn", [1, MSW], F32, side="right"))
    rowabs = _sc1.enter_context(nc.sbuf_tensor("rowabs", [1, MSW], F32, side="right"))
    xrA = _sc1.enter_context(nc.sbuf_tensor("xrA", [P, D], F32, side="right"))
    xrB = _sc1.enter_context(nc.sbuf_tensor("xrB", [P, D], F32, side="right"))
    garow = _sc1.enter_context(nc.sbuf_tensor("garow", [1, D], BF16, side="right"))
    gbrow = _sc1.enter_context(nc.sbuf_tensor("gbrow", [1, D], BF16, side="right"))
    lnxb0 = _sc1.enter_context(nc.sbuf_tensor("lnxb0", [P, D], BF16, side="right"))
    lnxb1 = _sc1.enter_context(nc.sbuf_tensor("lnxb1", [P, D], BF16, side="right"))
    ps_m0 = _sc1.enter_context(nc.psum_tensor("ps_m0", [P, 512], F32))
    ps_m1 = _sc1.enter_context(nc.psum_tensor("ps_m1", [P, 512], F32))
    ps_col = _sc1.enter_context(nc.psum_tensor("ps_col", [P, 512], F32))
    ps_t0 = _sc1.enter_context(nc.psum_tensor("ps_t0", [P, 1024], BF16))
    ps_t1 = _sc1.enter_context(nc.psum_tensor("ps_t1", [P, 1024], BF16))
    ps_t2 = _sc1.enter_context(nc.psum_tensor("ps_t2", [P, 1024], BF16))
    ps_t3 = _sc1.enter_context(nc.psum_tensor("ps_t3", [P, 1024], BF16))
    if True:
        for dst, srcd in ((eye_bf, eye_bf_d), (eye16, eye16_d), (tri, tri_d),
                          (maskb, mask_d), (ones_f, ones_f_d), (ones_b, ones_b_d)):
            dma("sync", dst[:, :], srcd.ap(), dC)
        dma("sync", constv[:, :], constv_d.ap(), dC)
        dma("sync", onescol[:, :], onescol_d.ap(), dC)
        dma("sync", onescol_b[:, :], onescol_b_d.ap(), dC)
        dma("sync", ccol[:, :], ccol_d, dC)
        dma("sync", nwcol[:, :], nw_d, dC)
        cdone = dC.n

        sil_done = pr.op("scalar", lambda e: e.activation(
            out=silc[:, :], in_=ccol[:, :], func=AF.Silu),
            waits=[(dC, cdone)], inc=sS)
        silb_done = pr.op("vector", lambda e: e.tensor_copy(
            out=silcb[:, :], in_=silc[:, :]), waits=[(sS, sil_done)], inc=sV)

        SEGCOL = {0: (sacol, 0.0), 1: (cacol, 1.0), 3: (sbcol, 0.0), 4: (cbcol, 1.0)}
        mod_state = {
            "mv_done": [0, 0, 0, 0],
            "dma": {},
            "ev": [0, 0],
            "colv": 0,
            "clampv": 0,
            "clamps": 0,
        }

        def emit_mod_dma(ms):
            st = mod_state
            slot = ms % 4
            buf = wmbufs[slot]
            dsem = (dF, dG, dJ, dK)[slot]
            st["dma"][ms] = (dsem, dma(
                "gpsimd", buf[:, :].rearrange("p (dt c) -> p dt c", dt=DT),
                wmod_d[ms].rearrange("dt p c -> p dt c"), dsem,
                waits=[(sT, st["mv_done"][slot]), (dsem, dsem.n)]))

        def emit_mod_slice(ms):
            st = mod_state
            if ms not in st["dma"]:
                emit_mod_dma(ms)
            slot = ms % 4
            par = ms % 2
            buf = wmbufs[slot]
            dsem, cnt = st["dma"][ms]
            rowt = rowtA if par == 0 else rowtB
            psm = ps_m0 if par == 0 else ps_m1
            md = None
            for dt_ in range(DT):
                md = pr.op("tensor",
                           lambda e, dt_=dt_, buf=buf, psm=psm: e.matmul(
                               psm[0:1, 0:MSW], silcb[:, dt_:dt_ + 1],
                               buf[:, dt_ * MSW:(dt_ + 1) * MSW],
                               start=(dt_ == 0), stop=(dt_ == DT - 1)),
                           waits=([(dsem, cnt), (sV, silb_done),
                                   (sV, st["ev"][par])] if dt_ == 0 else []),
                           inc=(sT if dt_ == DT - 1 else None))
            st["mv_done"][slot] = md
            ev = pr.op("vector", lambda e, psm=psm, rowt=rowt: e.tensor_copy(
                out=rowt[0:1, :], in_=psm[0:1, 0:MSW]),
                waits=[(sT, md), (sV, st["clampv"]), (sV, st["colv"]),
                       (sS, st["clamps"])], inc=sV)
            st["ev"][par] = ev
            seg = ms // 8
            if seg in SEGCOL:
                dstcol, addone = SEGCOL[seg]
                cm = None
                for k in range(2):
                    dtile = (ms % 8) * 2 + k
                    cm = pr.op("tensor", lambda e, k=k, dtile=dtile, rowt=rowt:
                               e.matmul(ps_col[:, dtile:dtile + 1],
                                        rowt[0:1, k * P:(k + 1) * P],
                                        ones_f[0:1, 0:1], start=True, stop=True),
                               waits=[(sV, ev), (sV, st["colv"])],
                               inc=(sT if k == 1 else None))
                d0 = (ms % 8) * 2
                st["colv"] = pr.op("vector",
                                   lambda e, d0=d0, dstcol=dstcol, addone=addone:
                                   e.tensor_scalar(
                                       out=dstcol[:, d0:d0 + 2],
                                       in0=ps_col[:, d0:d0 + 2],
                                       scalar1=float(addone), scalar2=None,
                                       op0=ALU.add),
                                   waits=[(sT, cm)], inc=sV)
            else:
                target = garow if seg == 2 else gbrow
                off = (ms % 8) * MSW
                s1_ = pr.op("scalar", lambda e, rowt=rowt: e.activation(
                    out=rowsgn[0:1, :], in_=rowt[0:1, :], func=AF.Sign),
                    waits=[(sV, ev), (sV, st["clampv"])], inc=sS)
                s2_ = pr.op("scalar", lambda e, rowt=rowt: e.activation(
                    out=rowabs[0:1, :], in_=rowt[0:1, :], func=AF.Abs),
                    waits=[(sS, s1_)], inc=sS)
                st["clamps"] = s2_
                v1_ = pr.op("vector", lambda e: e.tensor_scalar_max(
                    out=rowabs[0:1, :], in0=rowabs[0:1, :], scalar1=0.25),
                    waits=[(sS, s2_)], inc=sV)
                st["clampv"] = pr.op("vector",
                                     lambda e, target=target, off=off:
                                     e.tensor_tensor(
                                         out=target[0:1, off:off + MSW],
                                         in0=rowsgn[0:1, :], in1=rowabs[0:1, :],
                                         op=ALU.mult),
                                     waits=[(sV, v1_)], inc=sV)

        xrP = (xrA, xrB)
        xsemP = (dC, dE)
        vdoneP = [0, 0]
        sdoneP = [0, 0]
        for t in range(TT):
            par = t % 2
            w = ([(sV, vdoneP[par]), (sS, sdoneP[par])] if t >= 2
                 else [(dC, cdone)])
            cnt = dma("sync", xrP[par][:, :], x_d[t], xsemP[par], waits=w)
            vdoneP[par] = pr.op("vector", lambda e, par=par, t=t: e.tensor_reduce(
                out=msum[:, t:t + 1], in_=xrP[par][:, :], axis=AX.X, op=ALU.add),
                waits=[(xsemP[par], cnt)], inc=sV)
            sdoneP[par] = pr.op("scalar", lambda e, par=par, t=t: e.activation(
                out=lnxb0[:, :], in_=xrP[par][:, :], func=AF.Square,
                accum_out=sqsum[:, t:t + 1]),
                waits=[(xsemP[par], cnt), (sS, max(sdoneP))], inc=sS)

        for ms in range(24):
            emit_mod_slice(ms)
        saca_done = mod_state["colv"]

        mean_done = pr.op("vector", lambda e: e.tensor_scalar(
            out=mean1[:, :], in0=msum[:, :], scalar1=1.0 / D, scalar2=None,
            op0=ALU.mult), waits=[(sV, max(vdoneP)), (sS, max(sdoneP))], inc=sV)
        c1 = pr.op("vector", lambda e: e.tensor_scalar(
            out=tmp1[:, 0:TT], in0=sqsum[:, :], scalar1=1.0 / D, scalar2=None,
            op0=ALU.mult), waits=[(sV, mean_done)], inc=sV)
        c2 = pr.op("vector", lambda e: e.tensor_tensor(
            out=tmp1[:, 16:32], in0=mean1[:, :], in1=mean1[:, :], op=ALU.mult),
            waits=[(sV, c1)], inc=sV)
        vstat = pr.op("vector", lambda e: e.tensor_tensor(
            out=tmp1[:, 0:TT], in0=tmp1[:, 0:TT], in1=tmp1[:, 16:32],
            op=ALU.subtract), waits=[(sV, c2)], inc=sV)
        ln1d = pr.op("scalar", lambda e: e.activation(
            out=tmp1[:, 0:TT], in_=tmp1[:, 0:TT], func=AF.Ln, bias=LNEPS,
            scale=1.0), waits=[(sV, vstat)], inc=sS)
        rstd_done = pr.op("scalar", lambda e: e.activation(
            out=rstd1[:, :], in_=tmp1[:, 0:TT], func=AF.Exp, scale=-0.5),
            waits=[(sS, ln1d)], inc=sS)

        xr = (xrA, xrB)
        xsem = (dC, dE)
        vdone = vdoneP
        ms_next = 24
        ms_dma = 24
        tile_tr_done = [0, 0]
        ev_done = [(sS, 0), (sS, 0)]
        pstx = (ps_t0, ps_t1, ps_t2, ps_t3)
        ev_done = [(sS, 0)] * 4
        for t in range(TT):
            par = t % 2
            lnxb = lnxb0 if par == 0 else lnxb1
            w = [(sV, vdone[par]), (sS, max(sdoneP)), (sS, rstd_done)]
            cnt = dma("sync", xr[par][:, :], x_d[t], xsem[par], waits=w)
            vdone[par] = pr.op("vector", lambda e, par=par, t=t, lnxb=lnxb:
                               e.tensor_scalar(
                                   out=lnxb[:, :], in0=xr[par][:, :],
                                   scalar1=mean1[:, t:t + 1],
                                   scalar2=rstd1[:, t:t + 1],
                                   op0=ALU.subtract, op1=ALU.mult),
                               waits=[(xsem[par], cnt), (sS, rstd_done),
                                      (sV, mean_done), (sT, tile_tr_done[par])],
                               inc=sV)
            for dt_ in range(DT):
                ppar = dt_ % 4
                ptr = pstx[ppar]
                td = pr.op("tensor", lambda e, dt_=dt_, ptr=ptr, lnxb=lnxb:
                           e.transpose(ptr[:, 0:P], lnxb[:, dt_ * P:(dt_ + 1) * P],
                                       eye_bf[:, :]),
                           waits=[(sV, vdone[par]), ev_done[ppar]], inc=sT)
                if dt_ % 2 == 0:
                    ev_done[ppar] = (sS, pr.op(
                        "scalar", lambda e, dt_=dt_, t=t, ptr=ptr: e.activation(
                            out=h1T[:, dt_ * D + t * P: dt_ * D + (t + 1) * P],
                            in_=ptr[:, 0:P], func=AF.Identity,
                            scale=cacol[:, dt_:dt_ + 1],
                            bias=sacol[:, dt_:dt_ + 1]),
                        waits=[(sT, td), (sV, saca_done)], inc=sS))
                else:
                    ev_done[ppar] = (sV, pr.op(
                        "vector", lambda e, dt_=dt_, t=t, ptr=ptr: e.tensor_scalar(
                            out=h1T[:, dt_ * D + t * P: dt_ * D + (t + 1) * P],
                            in0=ptr[:, 0:P],
                            scalar1=cacol[:, dt_:dt_ + 1],
                            scalar2=sacol[:, dt_:dt_ + 1],
                            op0=ALU.mult, op1=ALU.add),
                        waits=[(sT, td), (sV, saca_done)], inc=sV))
            tile_tr_done[par] = sT.n
            while ms_dma < NMS and ms_dma < 24 + (t + 1) * 2:
                emit_mod_dma(ms_dma)
                ms_dma += 1
            while ms_next < NMS and ms_next < 24 + t * 2:
                emit_mod_slice(ms_next)
                ms_next += 1
        while ms_next < NMS:
            emit_mod_slice(ms_next)
            ms_next += 1
        dma("gpsimd", g_hbm[0], garow[0:1, :], dOut,
            waits=[(sV, mod_state["clampv"])])
        dma("gpsimd", g_hbm[1], gbrow[0:1, :], dOut,
            waits=[(sV, mod_state["clampv"])])
        BARRIER()
        if debug:
            dbg["h1T"] = nc.dram_tensor("dbg_h1T", [P, DT * D], BF16,
                                        kind="ExternalOutput").ap()
            dma("gpsimd", dbg["h1T"], h1T[:, :], dOut)
            dbg["garow"] = nc.dram_tensor("dbg_garow", [1, D], BF16,
                                          kind="ExternalOutput").ap()
            dbg["sacol"] = nc.dram_tensor("dbg_sacol", [P, DT], F32,
                                          kind="ExternalOutput").ap()
            dbg["cacol"] = nc.dram_tensor("dbg_cacol", [P, DT], F32,
                                          kind="ExternalOutput").ap()
            dma("gpsimd", dbg["garow"], garow[0:1, :], dOut)
            dma("gpsimd", dbg["sacol"], sacol[:, :], dOut)
            dma("gpsimd", dbg["cacol"], cacol[:, :], dOut)
            BARRIER()
        pr.flush()
    _sc1.close()

    _sc3 = ExitStack()
    wfbuf = _sc3.enter_context(nc.sbuf_tensor("wfbuf", [P, DT * H], F32, side="right"))
    wfbf = _sc3.enter_context(nc.sbuf_tensor("wfbf", [P, DT * H], BF16, side="right"))
    ldT = _sc3.enter_context(nc.sbuf_tensor("ldT", [16, D], F32, side="right"))
    cumall = _sc3.enter_context(nc.sbuf_tensor("cumall", [P, NCH * H], F32, side="right"))
    ldtok = _sc3.enter_context(nc.sbuf_tensor("ldtok", [P, NCH * H], F32, side="right"))
    tmp4 = _sc3.enter_context(nc.sbuf_tensor("tmp4", [P, NCH * H], F32, side="right"))
    totbc = _sc3.enter_context(nc.sbuf_tensor("totbc", [P, NCH * H], F32, side="right"))
    ps_l0 = _sc3.enter_context(nc.psum_tensor("ps_l0", [16, 512], F32))
    ps_l1 = _sc3.enter_context(nc.psum_tensor("ps_l1", [16, 512], F32))
    ps_c0 = _sc3.enter_context(nc.psum_tensor("ps_c0", [P, 512], F32))
    ps_c1 = _sc3.enter_context(nc.psum_tensor("ps_c1", [P, 512], F32))
    if True:

        ps_c = (ps_c0, ps_c1)
        cnt = dma("sync", wfbuf[:, :].rearrange("p (dt h) -> p dt h", dt=DT), wf_d.rearrange("dt p h -> p dt h"), dA)
        wfc = pr.op("vector", lambda e: e.tensor_copy(out=wfbf[:, :], in_=wfbuf[:, :]),
                    waits=[(dA, cnt)], inc=sV)
        ld_ev = [0, 0]
        for ch in range(4):
            lp = ch % 2
            pld = ps_l0 if lp == 0 else ps_l1
            md = None
            for dt_ in range(DT):
                md = pr.op("tensor", lambda e, ch=ch, dt_=dt_, pld=pld: e.matmul(
                    pld[:, :], wfbf[:, dt_ * H:(dt_ + 1) * H],
                    h1T[:, dt_ * D + ch * 512: dt_ * D + (ch + 1) * 512],
                    start=(dt_ == 0), stop=(dt_ == DT - 1)),
                    waits=[(sV, wfc), (sS, ld_ev[lp])] if dt_ == 0 else [],
                    inc=(sT if dt_ == DT - 1 else None))
            ld_ev[lp] = pr.op("scalar", lambda e, ch=ch, pld=pld: e.activation(
                out=ldT[:, ch * 512:(ch + 1) * 512], in_=pld[:, :], func=AF.Sigmoid),
                waits=[(sT, md)], inc=sS)
        lds = pr.op("scalar", lambda e: e.activation(
            out=ldT[:, :], in_=ldT[:, :], func=AF.Ln),
            waits=[(sS, max(ld_ev))], inc=sS)

        pev = [0, 0]
        for c in range(NCH):
            cp = c % 2
            td = pr.op("tensor", lambda e, c=c, cp=cp: e.transpose(
                ps_c[cp][0:P, 0:16], ldT[:, c * P:(c + 1) * P], eye16[:, :]),
                waits=[(sS, lds), (sV, pev[cp])], inc=sT)
            pev[cp] = pr.op("vector", lambda e, c=c, cp=cp: e.tensor_copy(
                out=ldtok[:, c * H:(c + 1) * H], in_=ps_c[cp][0:P, 0:16]),
                waits=[(sT, td)], inc=sV)
        ld_all = sV.n
        for c in range(NCH):
            cp = c % 2
            pr.op("tensor", lambda e, c=c, cp=cp: e.matmul(
                ps_c[cp][:, 0:16], tri[:, :], ldtok[:, c * H:(c + 1) * H],
                start=True, stop=True),
                waits=[(sV, ld_all), (sV, pev[cp])])
            td = pr.op("tensor", lambda e, c=c, cp=cp: e.matmul(
                ps_c[cp][0:1, 16:32], onescol[:, :], ldtok[:, c * H:(c + 1) * H],
                start=True, stop=True), inc=sT)
            pr.op("vector", lambda e, c=c, cp=cp: e.tensor_copy(
                out=cumall[:, c * H:(c + 1) * H], in_=ps_c[cp][:, 0:16]),
                waits=[(sT, td)])
            pev[cp] = pr.op("vector", lambda e, c=c, cp=cp: e.tensor_copy(
                out=totrow[0:1, c * H:(c + 1) * H], in_=ps_c[cp][0:1, 16:32]),
                inc=sV)
        cum_done = sV.n

        nbc = NCH * H // P
        for i in range(nbc):
            cp = i % 2
            td = pr.op("tensor", lambda e, i=i, cp=cp: e.matmul(
                ps_c[cp][:, 0:P], ones_f[0:1, :], totrow[0:1, i * P:(i + 1) * P],
                start=True, stop=True), waits=[(sV, cum_done), (sV, pev[cp])], inc=sT)
            pev[cp] = pr.op("vector", lambda e, i=i, cp=cp: e.tensor_copy(
                out=totbc[:, i * P:(i + 1) * P], in_=ps_c[cp][:, 0:P]),
                waits=[(sT, td)], inc=sV)
        bc_ev = sV.n

        vt = pr.op("vector", lambda e: e.scalar_tensor_tensor(
            out=tmp4[:, :], in0=totbc[:, :], scalar=0.5, in1=cumall[:, :],
            op0=ALU.mult, op1=ALU.subtract), waits=[(sV, bc_ev)], inc=sV)
        pr.op("scalar", lambda e: e.activation(out=e1t[:, :], in_=tmp4[:, :],
                                               func=AF.Exp), waits=[(sV, vt)])
        e12 = pr.op("scalar", lambda e: e.activation(out=e2t[:, :], in_=tmp4[:, :],
                                                     func=AF.Exp, scale=-1.0), inc=sS)
        NE = (NCH - 1) * H
        p3 = pr.op("vector", lambda e: e.tensor_tensor(
            out=tmp4[:, 0:NE], in0=totbc[:, 0:NE], in1=cumall[:, 0:NE],
            op=ALU.subtract), waits=[(sS, e12)], inc=sV)
        vt3 = pr.op("vector", lambda e: e.scalar_tensor_tensor(
            out=tmp4[:, 0:NE], in0=totbc[:, H:H + NE], scalar=0.5, in1=tmp4[:, 0:NE],
            op0=ALU.mult, op1=ALU.add), waits=[(sV, p3)], inc=sV)
        e3d = pr.op("scalar", lambda e: e.activation(
            out=e3t[:, :], in_=tmp4[:, 0:NE], func=AF.Exp), waits=[(sV, vt3)], inc=sS)
        vt4 = pr.op("vector", lambda e: e.tensor_tensor(
            out=tmp4[0:1, 0:NE], in0=totrow[0:1, 0:NE],
            in1=totrow[0:1, H:H + NE], op=ALU.add), waits=[(sS, e3d)], inc=sV)
        e4r = pr.op("scalar", lambda e: e.activation(
            out=tmp4[0:1, 0:NE], in_=tmp4[0:1, 0:NE], func=AF.Exp, scale=0.5),
            waits=[(sV, vt4)], inc=sS)
        for i in range((NE + P - 1) // P):
            w_ = min(P, NE - i * P)
            cp = i % 2
            td = pr.op("tensor", lambda e, i=i, w_=w_, cp=cp: e.matmul(
                ps_c[cp][:, 0:w_], ones_f[0:1, :], tmp4[0:1, i * P:i * P + w_],
                start=True, stop=True), waits=[(sS, e4r), (sV, pev[cp])], inc=sT)
            pev[cp] = pr.op("vector", lambda e, i=i, w_=w_, cp=cp: e.tensor_copy(
                out=e4bc[:, i * P:i * P + w_], in_=ps_c[cp][:, 0:w_]),
                waits=[(sT, td)], inc=sV)
        BARRIER()
        if debug:
            for nm, tens, wdt in (("ldtok", ldtok, NCH * H), ("cumall", cumall, NCH * H),
                                  ("e1t", e1t, NCH * H), ("e2t", e2t, NCH * H),
                                  ("e3t", e3t, NE), ("e4bc", e4bc, NE)):
                dbg[nm] = nc.dram_tensor("dbg_" + nm, [P, wdt], F32,
                                         kind="ExternalOutput").ap()
                dma("gpsimd", dbg[nm], tens[:, :], dOut)
            BARRIER()
        pr.flush()
    _sc3.close()

    _sc4 = ExitStack()
    wqbf0 = _sc4.enter_context(nc.sbuf_tensor("wqbf0", [P, D], BF16, side="right"))
    wkbf0 = _sc4.enter_context(nc.sbuf_tensor("wkbf0", [P, D], BF16, side="right"))
    wvbf0 = _sc4.enter_context(nc.sbuf_tensor("wvbf0", [P, D], BF16, side="right"))
    wqbf1 = _sc4.enter_context(nc.sbuf_tensor("wqbf1", [P, D], BF16, side="right"))
    wkbf1 = _sc4.enter_context(nc.sbuf_tensor("wkbf1", [P, D], BF16, side="right"))
    wvbf1 = _sc4.enter_context(nc.sbuf_tensor("wvbf1", [P, D], BF16, side="right"))
    qT0 = _sc4.enter_context(nc.sbuf_tensor("qT0", [P, D], BF16, side="right"))
    qT1 = _sc4.enter_context(nc.sbuf_tensor("qT1", [P, D], BF16, side="right"))
    kT0 = _sc4.enter_context(nc.sbuf_tensor("kT0", [P, D], BF16, side="right"))
    kT1 = _sc4.enter_context(nc.sbuf_tensor("kT1", [P, D], BF16, side="right"))
    vT = _sc4.enter_context(nc.sbuf_tensor("vT", [P, D], BF16, side="right"))
    ktok0 = _sc4.enter_context(nc.sbuf_tensor("ktok0", [P, D], BF16, side="right"))
    ktok1 = _sc4.enter_context(nc.sbuf_tensor("ktok1", [P, D], BF16, side="right"))
    vtok0 = _sc4.enter_context(nc.sbuf_tensor("vtok0", [P, D], BF16, side="right"))
    vtok1 = _sc4.enter_context(nc.sbuf_tensor("vtok1", [P, D], BF16, side="right"))
    s3b = [[_sc4.enter_context(nc.sbuf_tensor(f"s3b{i}{j}", [P, P], BF16, side="right"))
            for j in range(2)] for i in range(2)]
    obb = [[_sc4.enter_context(nc.sbuf_tensor(f"obb{i}{j}", [P, P], BF16, side="right"))
            for j in range(2)] for i in range(2)]
    kdb = [[_sc4.enter_context(nc.sbuf_tensor(f"kdb{i}{j}", [P, P], BF16, side="right"))
            for j in range(2)] for i in range(2)]
    junk5 = _sc4.enter_context(nc.sbuf_tensor("junk5", [P, P], BF16, side="right"))
    Sbf1 = _sc4.enter_context(nc.sbuf_tensor("Sbf1", [P, E], BF16, side="right"))
    ps_p0 = _sc4.enter_context(nc.psum_tensor("ps_p0", [P, 512], F32))
    ps_p1 = _sc4.enter_context(nc.psum_tensor("ps_p1", [P, 512], F32))
    ps_s0 = _sc4.enter_context(nc.psum_tensor("ps_s0", [P, 512], F32))
    ps_s1 = _sc4.enter_context(nc.psum_tensor("ps_s1", [P, 512], F32))
    ps_S0 = _sc4.enter_context(nc.psum_tensor("ps_S0", [P, 512], F32))
    ps_S1 = _sc4.enter_context(nc.psum_tensor("ps_S1", [P, 512], F32))
    ps_x0 = _sc4.enter_context(nc.psum_tensor("ps_x0", [P, 1024], BF16))
    ps_x1 = _sc4.enter_context(nc.psum_tensor("ps_x1", [P, 1024], BF16))
    if True:
        wsem = (dA, dB)
        qTs = (qT0, qT1)
        kTs = (kT0, kT1)
        ktoks = (ktok0, ktok1)
        vtoks = (vtok0, vtok1)
        Sbfs = (Sbf, Sbf1)
        ps_ss = (ps_s0, ps_s1)
        ps_Ss = (ps_S0, ps_S1)
        ps_os = (ps_p0, ps_p1)
        ps_xs = (ps_x0, ps_x1)
        wbfs2 = ({0: wqbf0, 1: wkbf0, 2: wvbf0}, {0: wqbf1, 1: wkbf1, 2: wvbf1})

        bank_free = [(sS, 0), (sS, 0)]
        trx_free = [(sV, 0), (sV, 0)]
        wsem_i = 0
        cast_done = {}
        mat_last_mm = [[0, 0, 0], [0, 0, 0]]

        def weight_chain(h):
            nonlocal wsem_i
            for m in range(3):
                sem = (dF, dG)[wsem_i % 2]
                wsem_i += 1
                cnt = dma("gpsimd",
                          wbfs2[h % 2][m][:, :].rearrange("p (dt e) -> p dt e", dt=DT),
                          wqkv_d[h][m].rearrange("dt p e -> p dt e"), sem,
                          waits=[(sT, mat_last_mm[h % 2][m]), (sem, sem.n)])
                cast_done[(h, m)] = (sem, cnt)

        weight_chain(0)
        weight_chain(1)

        supd = [0, 0]
        m2m3 = [0, 0]
        m4d = [0, 0]
        oev = [[(sS, 0), (sS, 0)], [(sS, 0), (sS, 0)]]
        s1ev = [0, 0]
        kdv = [[0, 0], [0, 0]]
        otr = [[(sV, 0), (sV, 0)], [(sV, 0), (sV, 0)]]
        sq = [0, 0]
        sq_add = 0
        for p in range(H // 2):
            ha, hb = 2 * p, 2 * p + 1

            def proj(h, m, dstT, func):
                lastmm = 0
                for ch in range(4):
                    ppar = ch % 2
                    pp = ps_os[ppar]
                    wbf = wbfs2[h % 2][m]
                    md = None
                    for dt_ in range(DT):
                        md = pr.op("tensor",
                                   lambda e, ch=ch, dt_=dt_, pp=pp, wbf=wbf:
                                   e.matmul(pp[:, :], wbf[:, dt_ * P:(dt_ + 1) * P],
                                            h1T[:, dt_ * D + ch * 512: dt_ * D + (ch + 1) * 512],
                                            start=(dt_ == 0), stop=(dt_ == DT - 1)),
                                   waits=([cast_done[(h, m)], bank_free[ppar]]
                                          if dt_ == 0 else []),
                                   inc=(sT if dt_ == DT - 1 else None))
                    lastmm = md
                    if (ch + m) % 2 == 0:
                        bank_free[ppar] = (sS, pr.op(
                            "scalar", lambda e, ch=ch, pp=pp, dstT=dstT, func=func:
                            e.activation(out=dstT[:, ch * 512:(ch + 1) * 512],
                                         in_=pp[:, :], func=func),
                            waits=[(sT, md)], inc=sS))
                    else:
                        if func == AF.Silu:
                            bank_free[ppar] = (sS, pr.op(
                                "scalar", lambda e, ch=ch, pp=pp, dstT=dstT:
                                e.activation(out=dstT[:, ch * 512:(ch + 1) * 512],
                                             in_=pp[:, :], func=AF.Silu),
                                waits=[(sT, md)], inc=sS))
                        else:
                            bank_free[ppar] = (sV, pr.op(
                                "vector", lambda e, ch=ch, pp=pp, dstT=dstT:
                                e.tensor_copy(out=dstT[:, ch * 512:(ch + 1) * 512],
                                              in_=pp[:, :]),
                                waits=[(sT, md)], inc=sV))
                mat_last_mm[h % 2][m] = lastmm
                return bank_free

            def transp_all(srcT, dst, hx):
                for c in range(NCH):
                    xp = c % 2
                    pt = ps_xs[xp]
                    td = pr.op("tensor", lambda e, c=c, pt=pt, srcT=srcT: e.transpose(
                        pt[:, 0:P], srcT[:, c * P:(c + 1) * P], eye_bf[:, :]),
                        waits=[bank_free[0], bank_free[1], trx_free[xp]], inc=sT)
                    trx_free[xp] = (sV, pr.op(
                        "vector", lambda e, c=c, pt=pt, dst=dst: e.tensor_copy(
                            out=dst[:, c * P:(c + 1) * P], in_=pt[:, 0:P]),
                        waits=[(sT, td)], inc=sV))

            for h, hi in ((ha, 0), (hb, 1)):
                proj(h, 0, qTs[hi], AF.Silu)
                proj(h, 1, kTs[hi], AF.Silu)
                proj(h, 2, vT, AF.Identity)
                transp_all(kTs[hi], ktoks[hi], hi)
                transp_all(vT, vtoks[hi], hi)

            if p + 1 < H // 2:
                weight_chain(2 * p + 2)
                weight_chain(2 * p + 3)

            for hi in range(2):
                supd[hi] = pr.op("vector",
                                 lambda e, hi=hi: e.memset(Sbfs[hi][:, :], 0.0),
                                 waits=[(sT, m2m3[hi])], inc=sV)
            for c in range(NCH):
                cp = c % 2
                for h, hi in ((ha, 0), (hb, 1)):
                    col = c * H + h
                    qT = qTs[hi]
                    kT = kTs[hi]
                    ktok = ktoks[hi]
                    vtok = vtoks[hi]
                    s3buf = s3b[hi][cp]
                    obuf = obb[hi][cp]
                    kdbuf = kdb[hi][cp]
                    po = ps_os[hi]
                    pS = ps_Ss[hi]
                    pss = ps_ss[hi]
                    pt = ps_xs[hi]

                    if c < NCH - 1:
                        kdv[hi][cp] = pr.op(
                            "scalar", lambda e, c=c, h=h, kdbuf=kdbuf, ktok=ktok:
                            e.activation(out=kdbuf[:, :],
                                         in_=ktok[:, c * P:(c + 1) * P],
                                         func=AF.Identity,
                                         scale=e3t[:, c * H + h:c * H + h + 1]),
                            waits=[trx_free[hi], (sT, m4d[hi])], inc=sS)

                    m1 = pr.op("tensor", lambda e, c=c, kT=kT, qT=qT, pss=pss:
                               e.matmul(pss[:, 0:P], kT[:, c * P:(c + 1) * P],
                                        qT[:, c * P:(c + 1) * P],
                                        start=True, stop=True),
                               waits=[(sV, s1ev[hi]), trx_free[0], trx_free[1]],
                               inc=sT)
                    s1ev[hi] = pr.op("vector",
                                     lambda e, col=col, s3buf=s3buf, pss=pss:
                                     e.scalar_tensor_tensor(
                                         out=s3buf[:, :], in0=pss[:, 0:P],
                                         scalar=e1t[:, col:col + 1],
                                         in1=maskb[:, :],
                                         op0=ALU.mult, op1=ALU.mult),
                                     waits=[(sT, m1), (sT, m2m3[hi])], inc=sV)

                    m2m3[hi] = pr.op("tensor",
                                     lambda e, c=c, po=po, s3buf=s3buf, vtok=vtok:
                                     e.matmul(po[:, 0:P], s3buf[:, :],
                                              vtok[:, c * P:(c + 1) * P],
                                              start=True, stop=(c == 0)),
                                     waits=[(sV, s1ev[hi]),
                                            (bank_free[hi] if c == 0
                                             else oev[hi][cp ^ 1])],
                                     inc=(sT if c == 0 else None))
                    if c > 0:
                        m2m3[hi] = pr.op("tensor",
                                         lambda e, c=c, po=po, qT=qT, hi=hi:
                                         e.matmul(po[:, 0:P],
                                                  qT[:, c * P:(c + 1) * P],
                                                  Sbfs[hi][:, :],
                                                  start=False, stop=True),
                                         waits=[(sV, supd[hi])], inc=sT)

                    if c < NCH - 1:
                        m4d[hi] = pr.op("tensor",
                                        lambda e, c=c, kdbuf=kdbuf, vtok=vtok, pS=pS:
                                        e.matmul(pS[:, 0:P], kdbuf[:, :],
                                                 vtok[:, c * P:(c + 1) * P],
                                                 start=True, stop=True),
                                        waits=[(sS, kdv[hi][cp]), (sV, supd[hi])],
                                        inc=sT)
                        supd[hi] = pr.op("vector",
                                         lambda e, c=c, h=h, hi=hi, pS=pS:
                                         e.scalar_tensor_tensor(
                                             out=Sbfs[hi][:, :], in0=Sbfs[hi][:, :],
                                             scalar=e4bc[:, c * H + h:c * H + h + 1],
                                             in1=pS[:, 0:P],
                                             op0=ALU.mult, op1=ALU.add),
                                         waits=[(sT, m4d[hi]), (sT, m2m3[hi]),
                                                (sV, supd[hi])], inc=sV)

                    oev[hi][cp] = (sS, pr.op(
                        "scalar", lambda e, col=col, po=po, obuf=obuf:
                        e.activation(out=obuf[:, :], in_=po[:, 0:P],
                                     func=AF.Identity,
                                     scale=e2t[:, col:col + 1]),
                        waits=[(sT, m2m3[hi]), otr[hi][cp]], inc=sS))
                    otd = pr.op("tensor", lambda e, obuf=obuf, pt=pt: e.transpose(
                        pt[:, 0:P], obuf[:, :], eye_bf[:, :]),
                        waits=[oev[hi][cp], trx_free[hi]], inc=sT)
                    if hi == 0:
                        trx_free[hi] = (sS, pr.op(
                            "scalar", lambda e, c=c, h=h, pt=pt: e.activation(
                                out=oT[:, h * D + c * P: h * D + (c + 1) * P],
                                in_=pt[:, 0:P], func=AF.Identity),
                            waits=[(sT, otd)], inc=sS))
                    else:
                        trx_free[hi] = (sV, pr.op(
                            "vector", lambda e, c=c, h=h, pt=pt: e.tensor_copy(
                                out=oT[:, h * D + c * P: h * D + (c + 1) * P],
                                in_=pt[:, 0:P]),
                            waits=[(sT, otd)], inc=sV))
                    otr[hi][cp] = trx_free[hi]
            bank_free = [oev[0][(NCH - 1) % 2], oev[1][(NCH - 1) % 2]]
        BARRIER()
        if debug:
            for nm, tens in (("qT", qT1), ("kT", kT1), ("vtok", vtok1),
                             ("ktok", ktok1)):
                dbg[nm] = nc.dram_tensor("dbg_" + nm, [P, D], BF16,
                                         kind="ExternalOutput").ap()
                dma("gpsimd", dbg[nm], tens[:, :], dOut)
            dbg["oT"] = nc.dram_tensor("dbg_oT", [P, H * D], BF16,
                                       kind="ExternalOutput").ap()
            dma("gpsimd", dbg["oT"], oT[:, :], dOut)
            BARRIER()
        pr.flush()
    _sc4.close()

    wo_bf = h1T

    _sc5 = ExitStack()
    wob0 = _sc5.enter_context(nc.sbuf_tensor("wobuf0", [P, D], F32, side="right"))
    wob1 = _sc5.enter_context(nc.sbuf_tensor("wobuf1", [P, D], F32, side="right"))
    xrc0 = _sc5.enter_context(nc.sbuf_tensor("xrc0", [P, D], F32, side="right"))
    xrc1 = _sc5.enter_context(nc.sbuf_tensor("xrc1", [P, D], F32, side="right"))
    x1b0 = _sc5.enter_context(nc.sbuf_tensor("x1b0", [P, D], F32, side="right"))
    x1b1 = _sc5.enter_context(nc.sbuf_tensor("x1b1", [P, D], F32, side="right"))
    gaB = _sc5.enter_context(nc.sbuf_tensor("gaB", [P, D], BF16, side="right"))
    garow6 = _sc5.enter_context(nc.sbuf_tensor("garow6", [1, D], BF16, side="right"))
    junk6b = _sc5.enter_context(nc.sbuf_tensor("junk6b", [P, D], BF16, side="right"))
    rrow = _sc5.enter_context(nc.sbuf_tensor("rrow", [1, D], F32, side="right"))
    ps_a0 = _sc5.enter_context(nc.psum_tensor("ps_a0", [P, 512], F32))
    ps_a1 = _sc5.enter_context(nc.psum_tensor("ps_a1", [P, 512], F32))
    ps_a2 = _sc5.enter_context(nc.psum_tensor("ps_a2", [P, 512], F32))
    ps_a3 = _sc5.enter_context(nc.psum_tensor("ps_a3", [P, 512], F32))
    ps_gb = _sc5.enter_context(nc.psum_tensor("ps_gb", [P, 512], F32))
    if True:

        ps_a = (ps_a0, ps_a1, ps_a2, ps_a3)
        garc = dma("sync", garow6[0:1, :], g_hbm[0], dC)
        sqv = 0
        smm = 0
        for ft in range(DT):
            sqv = pr.op("vector", lambda e, ft=ft: e.tensor_tensor(
                out=junk6b[:, :], in0=oT[:, ft * D:(ft + 1) * D],
                in1=oT[:, ft * D:(ft + 1) * D], op=ALU.mult),
                waits=[(sT, smm)], inc=sV)
            for mc in range(4):
                smm = pr.op("tensor", lambda e, ft=ft, mc=mc: e.matmul(
                    ps_a[mc][0:1, :], onescol_b[:, 0:1],
                    junk6b[:, mc * 512:(mc + 1) * 512],
                    start=(ft == 0), stop=(ft == DT - 1)),
                    waits=[(sV, sqv)] if mc == 0 else [],
                    inc=(sT if mc == 3 else None))
        rrow_ev = 0
        for mc in range(4):
            rrow_ev = pr.op("vector", lambda e, mc=mc: e.tensor_copy(
                out=rrow[0:1, mc * 512:(mc + 1) * 512], in_=ps_a[mc][0:1, :]),
                waits=[(sT, smm)], inc=sV)
        rl = pr.op("scalar", lambda e: e.activation(
            out=rrow[0:1, :], in_=rrow[0:1, :], func=AF.Ln, bias=RMSEPS,
            scale=1.0 / D), waits=[(sV, rrow_ev)], inc=sS)
        rx = pr.op("scalar", lambda e: e.activation(
            out=rrow[0:1, :], in_=rrow[0:1, :], func=AF.Exp, scale=-0.5),
            waits=[(sS, rl)], inc=sS)
        rcm = 0
        for t in range(TT):
            rcm = pr.op("tensor", lambda e, t=t: e.matmul(
                ps_gb[:, t:t + 1], rrow[0:1, t * P:(t + 1) * P], ones_f[0:1, 0:1],
                start=True, stop=True),
                waits=[(sS, rx)], inc=(sT if t == TT - 1 else None))
        rms_done = pr.op("vector", lambda e: e.tensor_copy(
            out=rmsr[:, :], in_=ps_gb[:, 0:TT]), waits=[(sT, rcm)], inc=sV)
        rms_sem = sV

        gaB_done = 0
        gev = rms_done
        for i in range(4):
            td = pr.op("tensor", lambda e, i=i: e.matmul(
                ps_gb[:, :], ones_b[0:1, :], garow6[0:1, i * 512:(i + 1) * 512],
                start=True, stop=True), waits=[(sV, gev), (dC, garc)], inc=sT)
            gev = pr.op("vector", lambda e, i=i: e.tensor_copy(
                out=gaB[:, i * 512:(i + 1) * 512], in_=ps_gb[:, :]),
                waits=[(sT, td)], inc=sV)
        gaB_done = gev

        wsem = (dA, dB)
        wob = (wob0, wob1)
        wo_cast = [(sS, 0)] * DT
        ccount = [(sS, 0), (sS, 0)]
        for ft in range(DT):
            par = ft % 2
            w = [ccount[par]] if ft >= 2 else []
            cnt = dma("sync", wob[par][:, :], wo_d[ft], wsem[par], waits=w)
            if ft % 2 == 0:
                ccount[par] = (sS, pr.op("scalar", lambda e, ft=ft, par=par:
                                         e.activation(
                                             out=wo_bf[:, ft * D:(ft + 1) * D],
                                             in_=wob[par][:, :], func=AF.Identity,
                                             scale=nwcol[:, ft:ft + 1]),
                                         waits=[(wsem[par], cnt)], inc=sS))
            else:
                ccount[par] = (sV, pr.op("vector", lambda e, ft=ft, par=par:
                                         e.tensor_scalar(
                                             out=wo_bf[:, ft * D:(ft + 1) * D],
                                             in0=wob[par][:, :],
                                             scalar1=nwcol[:, ft:ft + 1],
                                             scalar2=None, op0=ALU.mult),
                                         waits=[(wsem[par], cnt)], inc=sV))
            wo_cast[ft] = ccount[par]

        xsem6 = (dC, dE)
        x1sem = (dF, dG)
        xcnt = [0, 0]
        x1w = [0, 0]
        vread = [0, 0]
        evs = [0, 0, 0, 0]
        stt_done = [0, 0, 0, 0]
        sst = 0
        for t in range(TT):
            par = t % 2
            xrc = xrc0 if par == 0 else xrc1
            x1buf = x1b0 if par == 0 else x1b1
            w = [(sV, vread[par])] if t >= 2 else [(dC, garc)]
            xcnt[par] = dma("sync", xrc[:, :], x_d[t], xsem6[par], waits=w)
            mmd = None
            for ft in range(DT):
                for mc in range(4):
                    mmd = pr.op("tensor", lambda e, t=t, mc=mc, ft=ft: e.matmul(
                        ps_a[mc][:, :], oT[:, ft * D + t * P:ft * D + (t + 1) * P],
                        wo_bf[:, ft * D + mc * 512: ft * D + (mc + 1) * 512],
                        start=(ft == 0), stop=(ft == DT - 1)),
                        waits=(([wo_cast[ft]] if t == 0 and mc == 0 else []) +
                               ([(sV, stt_done[mc])] if ft == 0 else [])),
                        inc=(sT if (ft == DT - 1 and mc == 3) else None))
            for mc in range(4):
                sttd = pr.op("vector", lambda e, t=t, mc=mc, x1buf=x1buf:
                             e.scalar_tensor_tensor(
                                 out=x1buf[:, mc * 512:(mc + 1) * 512],
                                 in0=ps_a[mc][:, :],
                                 scalar=rmsr[:, t:t + 1],
                                 in1=gaB[:, mc * 512:(mc + 1) * 512],
                                 op0=ALU.mult, op1=ALU.mult),
                             waits=[(sT, mmd), (sV, rms_done), (sV, gaB_done),
                                    (x1sem[par], x1w[par])], inc=sV)
                stt_done[mc] = sttd
                evs[mc] = pr.op("vector", lambda e, mc=mc, x1buf=x1buf, xrc=xrc:
                                e.tensor_tensor(
                                    out=x1buf[:, mc * 512:(mc + 1) * 512],
                                    in0=x1buf[:, mc * 512:(mc + 1) * 512],
                                    in1=xrc[:, mc * 512:(mc + 1) * 512],
                                    op=ALU.add),
                                waits=[(xsem6[par], xcnt[par]), (sV, sttd)], inc=sV)
            vread[par] = evs[3]
            vst = pr.op("vector", lambda e, t=t, x1buf=x1buf: e.tensor_reduce(
                out=msum[:, t:t + 1], in_=x1buf[:, :], axis=AX.X, op=ALU.add),
                waits=[(sV, evs[3])], inc=sV)
            sst = pr.op("scalar", lambda e, t=t, x1buf=x1buf: e.activation(
                out=junk6b[:, :], in_=x1buf[:, :], func=AF.Square,
                accum_out=sqsum[:, t:t + 1]),
                waits=[(sV, evs[3]), (sS, sst if t else 0)], inc=sS)
            x1w[par] = dma("gpsimd", x1_hbm[t], x1buf[:, :], x1sem[par],
                           waits=[(sV, vst), (sS, sst)])

        mean2_done = pr.op("vector", lambda e: e.tensor_scalar(
            out=mean2[:, :], in0=msum[:, :], scalar1=1.0 / D, scalar2=None,
            op0=ALU.mult), waits=[(sV, sV.n), (sS, sS.n)], inc=sV)
        c1b = pr.op("vector", lambda e: e.tensor_scalar(
            out=tmp1[:, 0:TT], in0=sqsum[:, :], scalar1=1.0 / D, scalar2=None,
            op0=ALU.mult), waits=[(sV, mean2_done)], inc=sV)
        c2b = pr.op("vector", lambda e: e.tensor_tensor(
            out=tmp1[:, 16:32], in0=mean2[:, :], in1=mean2[:, :], op=ALU.mult),
            waits=[(sV, c1b)], inc=sV)
        vst2 = pr.op("vector", lambda e: e.tensor_tensor(
            out=tmp1[:, 0:TT], in0=tmp1[:, 0:TT], in1=tmp1[:, 16:32],
            op=ALU.subtract), waits=[(sV, c2b)], inc=sV)
        ln2d = pr.op("scalar", lambda e: e.activation(
            out=tmp1[:, 0:TT], in_=tmp1[:, 0:TT], func=AF.Ln, bias=LNEPS, scale=1.0),
            waits=[(sV, vst2)], inc=sS)
        rstd2_done = pr.op("scalar", lambda e: e.activation(
            out=rstd2[:, :], in_=tmp1[:, 0:TT], func=AF.Exp, scale=-0.5),
            waits=[(sS, ln2d)], inc=sS)
        BARRIER()
        if debug:
            dbg["x1"] = nc.dram_tensor("dbg_x1", [TT, P, D], F32,
                                       kind="ExternalOutput").ap()
            for t in range(TT):
                dma("gpsimd", dbg["x1"][t], x1_hbm[t], dOut)
            BARRIER()
        pr.flush()
    _sc5.close()

    big1.close()

    big2 = ExitStack()
    h2T = big2.enter_context(nc.sbuf_tensor("h2T8", [P, DT * D], F8, side="left"))
    act2 = big2.enter_context(nc.sbuf_tensor("act2", [P, 2 * NS * QW], F8, side="left"))
    wl2 = big2.enter_context(nc.sbuf_tensor("wl2", [P, 2 * NS * QW], F8, side="left"))
    gbB = big2.enter_context(nc.sbuf_tensor("gbB", [P, D], BF16, side="left"))
    gbrow9 = big2.enter_context(nc.sbuf_tensor("gbrow9", [1, D], BF16, side="left"))

    _sc6 = ExitStack()
    x1r0 = _sc6.enter_context(nc.sbuf_tensor("x1r0", [P, D], F32, side="right"))
    x1r1 = _sc6.enter_context(nc.sbuf_tensor("x1r1", [P, D], F32, side="right"))
    lnxb20 = _sc6.enter_context(nc.sbuf_tensor("lnxb20", [P, D], BF16, side="right"))
    lnxb21 = _sc6.enter_context(nc.sbuf_tensor("lnxb21", [P, D], BF16, side="right"))
    ps_u0 = _sc6.enter_context(nc.psum_tensor("ps_u0", [P, 1024], BF16))
    ps_u1 = _sc6.enter_context(nc.psum_tensor("ps_u1", [P, 1024], BF16))
    ps_u2 = _sc6.enter_context(nc.psum_tensor("ps_u2", [P, 1024], BF16))
    ps_u3 = _sc6.enter_context(nc.psum_tensor("ps_u3", [P, 1024], BF16))
    if True:

        x1r = (x1r0, x1r1)
        xsem7 = (dA, dB)
        psux = (ps_u0, ps_u1, ps_u2, ps_u3)
        vdone = [0, 0]
        tile_tr_done = [0, 0]
        ev_done = [(sS, 0)] * 4
        for t in range(TT):
            par = t % 2
            lnxb2 = lnxb20 if par == 0 else lnxb21
            w = [(sV, vdone[par])] if t >= 2 else []
            cnt = dma("sync", x1r[par][:, :], x1_hbm[t], xsem7[par], waits=w)
            vdone[par] = pr.op("vector", lambda e, par=par, t=t, lnxb2=lnxb2:
                               e.tensor_scalar(
                                   out=lnxb2[:, :], in0=x1r[par][:, :],
                                   scalar1=mean2[:, t:t + 1],
                                   scalar2=rstd2[:, t:t + 1],
                                   op0=ALU.subtract, op1=ALU.mult),
                               waits=[(xsem7[par], cnt), (sS, rstd2_done),
                                      (sV, mean2_done),
                                      (sT, tile_tr_done[par])], inc=sV)
            for dt_ in range(DT):
                ppar = dt_ % 4
                ptr = psux[ppar]
                td = pr.op("tensor", lambda e, dt_=dt_, ptr=ptr, lnxb2=lnxb2: e.transpose(
                    ptr[:, 0:P], lnxb2[:, dt_ * P:(dt_ + 1) * P], eye_bf[:, :]),
                    waits=[(sV, vdone[par]), ev_done[ppar]], inc=sT)
                if dt_ % 2 == 0:
                    ev_done[ppar] = (sS, pr.op(
                        "scalar", lambda e, dt_=dt_, t=t, ptr=ptr: e.activation(
                            out=h2T[:, dt_ * D + t * P: dt_ * D + (t + 1) * P],
                            in_=ptr[:, 0:P], func=AF.Identity,
                            scale=cbcol[:, dt_:dt_ + 1],
                            bias=sbcol[:, dt_:dt_ + 1]),
                        waits=[(sT, td)], inc=sS))
                else:
                    ev_done[ppar] = (sV, pr.op(
                        "vector", lambda e, dt_=dt_, t=t, ptr=ptr: e.tensor_scalar(
                            out=h2T[:, dt_ * D + t * P: dt_ * D + (t + 1) * P],
                            in0=ptr[:, 0:P],
                            scalar1=cbcol[:, dt_:dt_ + 1],
                            scalar2=sbcol[:, dt_:dt_ + 1],
                            op0=ALU.mult, op1=ALU.add),
                        waits=[(sT, td)], inc=sV))
            tile_tr_done[par] = sT.n
        BARRIER()
        if debug:
            dbg["h2T"] = nc.dram_tensor("dbg_h2T", [P, DT * D], BF16,
                                        kind="ExternalOutput").ap()
            dma("gpsimd", dbg["h2T"], h2T[:, :], dOut)
            BARRIER()
        pr.flush()
    _sc6.close()

    _sc7 = ExitStack()
    wg0f8 = [_sc7.enter_context(nc.sbuf_tensor(f"wg0f8{i}", [P, DT * E], F8, side="right"))
             for i in range(2)]
    wg1f8 = [_sc7.enter_context(nc.sbuf_tensor(f"wg1f8{i}", [P, DT * E], F8, side="right"))
             for i in range(2)]
    gelbs = [_sc7.enter_context(nc.sbuf_tensor(f"gelb{i}", [P, 512], BF16, side="right"))
             for i in range(4)]
    actbs = [_sc7.enter_context(nc.sbuf_tensor(f"actb{i}", [P, 2 * QW], F8, side="right"))
             for i in range(4)]
    ps_gs = [_sc7.enter_context(nc.psum_tensor(f"ps_g{i}", [P, 512], F32))
             for i in range(4)]
    ps_fs = [_sc7.enter_context(nc.psum_tensor(f"ps_f{i}", [P, 512], F32))
             for i in range(4)]
    if True:
        wcsem = (dJ, dK)
        actsem = (dF, dG, dH, dI)
        gmm = 0
        fmm = 0
        gel_done = [0, 0, 0, 0]
        act_done = [0, 0, 0, 0]
        actw = [0, 0, 0, 0]
        g0cnt = [0, 0]
        g1cnt = [0, 0]
        NDP = DT // 2
        wlcast_q = [[None] * 8 for _ in range(NQ)]
        preload_acnts = [[None] * 4 for _ in range(2)]
        ACT_CSEM = (dC, dE, dN, dO)
        ACT_CQ = ("sync", "sync", "scalar", "scalar")

        def act_chunk_dma(bp, ats, ck, extra_waits):
            sem = ACT_CSEM[ck]
            cnt = dma(ACT_CQ[ck],
                      act2[:, bp * NS * QW + ck * 8 * 2 * QW:
                           bp * NS * QW + (ck + 1) * 8 * 2 * QW].rearrange(
                          "p (s c) -> p s c", s=8),
                      act_hbm[ats][ck * 8:(ck + 1) * 8].rearrange("s p c -> p s c"),
                      sem, waits=[(sem, sem.n)] + list(extra_waits))
            return (sem, cnt)

        def wl_group_dma(q, g, extra_waits):
            sem = dP if g % 2 == 0 else dQ
            cnt = dma("gpsimd",
                      wl2[:, (q % 2) * NS * QW + g * 8 * QW:
                           (q % 2) * NS * QW + (g + 1) * 8 * QW].rearrange(
                          "p (s c) -> p s c", s=8),
                      wl_d[q][g * 8:(g + 1) * 8].rearrange("s p c -> p s c"),
                      sem, waits=[(sem, sem.n)] + list(extra_waits))
            wlcast_q[q][g] = (sem, cnt)

        def wl_half_dma(q, g, half, extra_waits):
            sem = dP if g % 2 == 0 else dQ
            s0 = g * 8 + half * 4
            cnt = dma("gpsimd",
                      wl2[:, (q % 2) * NS * QW + s0 * QW:
                           (q % 2) * NS * QW + (s0 + 4) * QW].rearrange(
                          "p (s c) -> p s c", s=4),
                      wl_d[q][s0:s0 + 4].rearrange("s p c -> p s c"),
                      sem, waits=[(sem, sem.n)] + list(extra_waits))
            wlcast_q[q][g] = (sem, cnt)

        def h2pair(dtp, tc):
            return h2T[:, :].rearrange("p (dt c) -> p dt c", dt=DT)[
                :, 2 * dtp:2 * dtp + 2, tc * 512:(tc + 1) * 512]

        def wgpair(buf, dtp):
            return buf[:, :].rearrange("p (dt e) -> p dt e", dt=DT)[
                :, 2 * dtp:2 * dtp + 2, :]

        gbrc = dma("sync", gbrow9[0:1, :], g_hbm[1], dC, waits=[(dC, dC.n)])
        gbB_done = 0
        for i in range(4):
            td = pr.op("tensor", lambda e, i=i: e.matmul(
                ps_gs[i][:, :], ones_b[0:1, :],
                gbrow9[0:1, i * 512:(i + 1) * 512],
                start=True, stop=True), waits=[(dC, gbrc)], inc=sT)
            gbB_done = pr.op("vector", lambda e, i=i: e.tensor_copy(
                out=gbB[:, i * 512:(i + 1) * 512], in_=ps_gs[i][:, :]),
                waits=[(sT, td)], inc=sV)

        g0cnt[0] = dma("gpsimd", wg0f8[0][:, :].rearrange("p (dt e) -> p dt e", dt=DT),
                       wg_d[0][0].rearrange("dt p e -> p dt e"), dJ)
        g1cnt[0] = dma("gpsimd", wg1f8[0][:, :].rearrange("p (dt e) -> p dt e", dt=DT),
                       wg_d[0][1].rearrange("dt p e -> p dt e"), dK,
                       waits=[(dJ, g0cnt[0])])
        for s in range(NS):
            par = s % 2
            if s % 2 == 1:
                idx = (s - 1) // 2
                wl_half_dma(idx // 16, (idx % 16) // 2, idx % 2, [])
            if s + 1 < NS:
                g0cnt[1 - par] = dma("gpsimd",
                                     wg0f8[1 - par][:, :].rearrange(
                                         "p (dt e) -> p dt e", dt=DT),
                                     wg_d[s + 1][0].rearrange("dt p e -> p dt e"),
                                     dJ, waits=[(dJ, dJ.n), (sT, gmm)])
            for dtp in range(NDP):
                for tc in range(4):
                    gmm = pr.op("tensor", lambda e, tc=tc, dtp=dtp, par=par:
                                e.matmul(ps_gs[tc][:, :], wgpair(wg0f8[par], dtp),
                                         h2pair(dtp, tc),
                                         start=(dtp == 0), stop=(dtp == NDP - 1),
                                         perf_mode=mybir.MatmulPerfMode.DoubleRow),
                                waits=(([(wcsem[0], g0cnt[par])] +
                                        [(sS, g) for g in gel_done] +
                                        ([(sV, gbB_done)] if s == 0 else []))
                                       if (dtp == 0 and tc == 0) else []),
                                inc=(sT if (dtp == NDP - 1 and tc == 3) else None))
            for tc in range(4):
                gel_done[tc] = pr.op("scalar", lambda e, tc=tc: e.activation(
                    out=gelbs[tc][:, :], in_=ps_gs[tc][:, :],
                    func=AF.Gelu_apprx_tanh),
                    waits=[(sT, gmm), (sV, act_done[tc])], inc=sS)
            if s + 1 < NS:
                g1cnt[1 - par] = dma("gpsimd",
                                     wg1f8[1 - par][:, :].rearrange(
                                         "p (dt e) -> p dt e", dt=DT),
                                     wg_d[s + 1][1].rearrange("dt p e -> p dt e"),
                                     dK, waits=[(dK, dK.n), (sT, fmm)])
            for dtp in range(NDP):
                for tc in range(4):
                    fmm = pr.op("tensor", lambda e, tc=tc, dtp=dtp, par=par:
                                e.matmul(ps_fs[tc][:, :], wgpair(wg1f8[par], dtp),
                                         h2pair(dtp, tc),
                                         start=(dtp == 0), stop=(dtp == NDP - 1),
                                         perf_mode=mybir.MatmulPerfMode.DoubleRow),
                                waits=(([(wcsem[1], g1cnt[par])] +
                                        [(sV, a) for a in act_done])
                                       if (dtp == 0 and tc == 0) else []),
                                inc=(sT if (dtp == NDP - 1 and tc == 3) else None))
            half = s % 2
            for tc in range(4):
                act_done[tc] = pr.op("vector", lambda e, tc=tc, half=half:
                                     e.tensor_tensor(
                                         out=actbs[tc][:, :].rearrange(
                                             "p (tt two c) -> p tt two c",
                                             tt=4, two=2)[:, :, half, :],
                                         in0=gelbs[tc][:, :].rearrange(
                                             "p (tt c) -> p tt c", tt=4),
                                         in1=ps_fs[tc][:, :].rearrange(
                                             "p (tt c) -> p tt c", tt=4),
                                         op=ALU.mult),
                                     waits=[(sS, gel_done[tc]), (sT, fmm)] +
                                           ([(actsem[tc], actw[tc])] if half == 0
                                            else []), inc=sV)
                if half == 1:
                    actw[tc] = dma("gpsimd", act_hbm[tc][s // 2], actbs[tc][:, :],
                                   actsem[tc], waits=[(sV, act_done[tc])])
            if half == 1 and (s // 2 + 1) % 8 == 0:
                ck = (s // 2 + 1) // 8 - 1
                for ts_ in (0, 1):
                    preload_acnts[ts_][ck] = act_chunk_dma(
                        ts_, ts_, ck, [(actsem[ts_], actw[ts_])])
        BARRIER()
        pr.flush()
    _sc7.close()

    _sc8 = ExitStack()
    x1q0 = _sc8.enter_context(nc.sbuf_tensor("x1q0", [P, QW], F32, side="right"))
    x1q1 = _sc8.enter_context(nc.sbuf_tensor("x1q1", [P, QW], F32, side="right"))
    outq0 = _sc8.enter_context(nc.sbuf_tensor("outq0", [P, QW], F32, side="right"))
    outq1 = _sc8.enter_context(nc.sbuf_tensor("outq1", [P, QW], F32, side="right"))
    ps_y0 = _sc8.enter_context(nc.psum_tensor("ps_y0", [P, 512], F32))
    ps_y1 = _sc8.enter_context(nc.psum_tensor("ps_y1", [P, 512], F32))
    ps_y2 = _sc8.enter_context(nc.psum_tensor("ps_y2", [P, 512], F32))
    ps_y3 = _sc8.enter_context(nc.psum_tensor("ps_y3", [P, 512], F32))
    ps_y4 = _sc8.enter_context(nc.psum_tensor("ps_y4", [P, 512], F32))
    ps_y5 = _sc8.enter_context(nc.psum_tensor("ps_y5", [P, 512], F32))
    ps_y6 = _sc8.enter_context(nc.psum_tensor("ps_y6", [P, 512], F32))
    ps_y7 = _sc8.enter_context(nc.psum_tensor("ps_y7", [P, 512], F32))
    if True:
        ps_y_sets = ((ps_y0, ps_y1, ps_y2, ps_y3), (ps_y4, ps_y5, ps_y6, ps_y7))
        x1qsem = (dL, dM)
        outsem = (dOut, dOut2)
        evq = [0, 0]
        vq_done = [[0, 0, 0, 0], [0, 0, 0, 0]]
        x1cnt = [0, 0]
        outw = [0, 0]
        gts = 0
        SPC = NS // 4
        NSP = NS // 2
        NGTS = NQ * TSUP
        mm_end_of = [0] * NGTS
        q_end_mm = [0] * NQ
        buf_acnts = [preload_acnts[0], preload_acnts[1]]

        def acpair(bp, sp, tt_):
            base = bp * NS * QW + sp * 2 * QW + tt_ * 256
            return act2[:, base:base + 256].rearrange("p (two c) -> p two c", two=2)

        def wlpair(qp, sp):
            return wl2[:, qp * NS * QW:(qp + 1) * NS * QW].rearrange(
                "p (s c) -> p s c", s=NS)[:, 2 * sp:2 * sp + 2, :]

        for q in range(NQ):
            if 1 <= q < NQ - 1:
                for g in range(8):
                    wl_group_dma(q + 1, g, [(sT, q_end_mm[q - 1])])
            for ts in range(TSUP):
                bp = gts % 2
                ps_y = ps_y_sets[bp]
                if gts >= 1 and gts + 1 < NGTS:
                    nb = (gts + 1) % 2
                    for ck in range(4):
                        buf_acnts[nb][ck] = act_chunk_dma(
                            nb, (gts + 1) % TSUP, ck, [(sT, mm_end_of[gts - 1])])
                for tt_ in range(2):
                    t = ts * 4 + tt_
                    tp = tt_ % 2
                    w = [(sV, evq[tp])]
                    if q == 0 and ts == 0:
                        w += [(dF, dF.n), (dG, dG.n)]
                    x1cnt[tp] = dma("sync",
                                    (x1q0 if tp == 0 else x1q1)[:, :],
                                    x1_hbm[t][:, q * QW:(q + 1) * QW],
                                    x1qsem[tp], waits=w)
                mmd = 0
                for sp in range(NSP):
                    ck = (2 * sp) // SPC
                    for tt_ in range(4):
                        w = []
                        if (2 * sp) % SPC == 0 and tt_ == 0:
                            w += [buf_acnts[bp][ck]]
                        if (2 * sp) % 8 == 0 and tt_ == 0 and ts == 0:
                            w += [wlcast_q[q][(2 * sp) // 8]]
                        if sp == 0:
                            w += [(sV, vq_done[bp][tt_])]
                        do_inc = (tt_ == 3 and (sp % 4 == 3))
                        mmd = pr.op("tensor",
                                    lambda e, bp=bp, qp=q % 2, sp=sp, tt_=tt_,
                                    py=ps_y[tt_]:
                                    e.matmul(py[:, :], acpair(bp, sp, tt_),
                                             wlpair(qp, sp),
                                             start=(sp == 0), stop=(sp == NSP - 1),
                                             perf_mode=mybir.MatmulPerfMode.DoubleRow),
                                    waits=w,
                                    inc=(sT if do_inc else None))
                mm_end_of[gts] = mmd
                for tt_ in range(4):
                    t = ts * 4 + tt_
                    tp = tt_ % 2
                    x1q = x1q0 if tp == 0 else x1q1
                    outq = outq0 if tp == 0 else outq1
                    vq = pr.op("vector", lambda e, q=q, outq=outq, py=ps_y[tt_]:
                               e.tensor_tensor(
                                   out=outq[:, :], in0=py[:, :],
                                   in1=gbB[:, q * QW:(q + 1) * QW], op=ALU.mult),
                               waits=[(sT, mmd), (sV, gbB_done),
                                      (x1qsem[tp], x1cnt[tp]),
                                      (outsem[tp], outw[tp])], inc=sV)
                    vq_done[bp][tt_] = vq
                    evq[tp] = pr.op("vector", lambda e, outq=outq, x1q=x1q:
                                    e.tensor_tensor(out=outq[:, :], in0=outq[:, :],
                                                    in1=x1q[:, :], op=ALU.add),
                                    waits=[(sV, vq)], inc=sV)
                    outw[tp] = dma("scalar", y_d[t][:, q * QW:(q + 1) * QW],
                                   outq[:, :], outsem[tp], waits=[(sV, evq[tp])])
                    if tt_ == 1:
                        for tt2 in (2, 3):
                            t2 = ts * 4 + tt2
                            tp2 = tt2 % 2
                            x1cnt[tp2] = dma("sync",
                                             (x1q0 if tp2 == 0 else x1q1)[:, :],
                                             x1_hbm[t2][:, q * QW:(q + 1) * QW],
                                             x1qsem[tp2], waits=[(sV, evq[tp2])])
                gts += 1
            q_end_mm[q] = mmd
        pr.op("gpsimd", lambda e: e.engine_nop(),
              waits=[(dOut, dOut.n), (dOut2, dOut2.n)])
        pr.flush()
    _sc8.close()

    big2.close()
    es.close()
    return nc, dbg


_BUILT = None


def _host_tiles(inputs):
    f32 = np.float32
    Wq = np.asarray(inputs["L0_attn_Wq"], f32)
    Wk = np.asarray(inputs["L0_attn_Wk"], f32)
    Wv = np.asarray(inputs["L0_attn_Wv"], f32)
    wqkv = np.ascontiguousarray(np.stack([
        Wq.reshape(DT, P, H, E).transpose(2, 0, 1, 3),
        Wk.reshape(DT, P, H, E).transpose(2, 0, 1, 3),
        Wv.reshape(DT, P, H, E).transpose(2, 0, 1, 3),
    ], axis=1))
    wmod = np.ascontiguousarray(
        np.asarray(inputs["L0_wmod"], f32).reshape(DT, P, NMS, MSW).transpose(2, 0, 1, 3))
    ccol = np.ascontiguousarray(
        np.asarray(inputs["L0_c"], f32).reshape(8, DT, P).transpose(0, 2, 1))
    wf = np.ascontiguousarray(np.asarray(inputs["L0_attn_Wf"], f32).reshape(DT, P, H))
    wo = np.ascontiguousarray(np.asarray(inputs["L0_attn_Wo"], f32).reshape(DT, P, D))
    nwcol_ = np.ascontiguousarray(
        np.asarray(inputs["L0_attn_norm_w"], f32).reshape(DT, P).T)
    wg_in = np.asarray(inputs["L0_ffn_w_gating"], f32)
    wg = np.ascontiguousarray(
        np.stack([wg_in[0].reshape(DT, P, NS, E).transpose(2, 0, 1, 3),
                  wg_in[1].reshape(DT, P, NS, E).transpose(2, 0, 1, 3)],
                 axis=1))
    wl = np.ascontiguousarray(
        np.asarray(inputs["L0_ffn_w_linear"], f32).reshape(NS, P, NQ, QW).transpose(2, 0, 1, 3))
    return wqkv, wmod, ccol, wf, wo, nwcol_, wg, wl


def kernel(**inputs):
    global _BUILT
    if _BUILT is None:
        _BUILT = build(debug=False)
    nc, _ = _BUILT
    wqkv, wmod, ccol, wf, wo, nwcol_, wg, wl = _host_tiles(inputs)
    x = np.ascontiguousarray(np.asarray(inputs["L0_attn_x"], np.float32))
    in_maps = []
    for b in range(8):
        in_maps.append({
            "x": x[b].reshape(TT, P, D),
            "wmod": wmod, "ccol": np.ascontiguousarray(ccol[b]),
            "wqkv": wqkv, "wf": wf, "wo": wo, "nwcol": nwcol_,
            "wg": wg, "wl": wl,
        })
    res = run_bass_kernel_spmd(nc, in_maps, core_ids=list(range(8)))
    out = np.stack([r["y"].reshape(D, D) for r in res.results])
    return out


# version 20
# speedup vs baseline: 1.2745x; 1.2745x over previous
import numpy as np
from contextlib import ExitStack

import concourse.bass as bass
import concourse.mybir as mybir
from concourse.bass_utils import run_bass_kernel_spmd

F32 = mybir.dt.float32
F32R = mybir.dt.float32r
BF16 = mybir.dt.bfloat16
F8 = mybir.dt.float8e4
AF = mybir.ActivationFunctionType
ALU = mybir.AluOpType
AX = mybir.AxisListType

P = 128
D = 2048
TT = 16
DT = 16
H = 16
E = 128
NCH = 16
MS = 12288
NMS = 48
MSW = 256
HID = 8192
NS = 64
NQ = 4
QW = 512
TSUP = 4
LNEPS = 1e-5
RMSEPS = 1e-6


class Sem:
    def __init__(self, nc, es, name):
        self.h = es.enter_context(nc.semaphore(name))
        self.n = 0
        self.name = name


class Prog:
    ENGS = ("tensor", "vector", "scalar", "sync", "gpsimd")

    def __init__(self, nc):
        self.nc = nc
        self.q = {e: [] for e in self.ENGS}
        self.last_wait = {}

    def _wait(self, eng, sem, val):
        if val <= 0:
            return
        key = (eng, sem.name)
        if self.last_wait.get(key, 0) >= val:
            return
        self.last_wait[key] = val
        self.q[eng].append(lambda e, h=sem.h, v=val: e.wait_ge(h, v))

    def op(self, eng, fn, waits=(), inc=None, inc_k=1):
        for sem, val in waits:
            self._wait(eng, sem, val)
        if inc is not None:
            self.q[eng].append(lambda e, fn=fn, h=inc.h, k=inc_k: fn(e).then_inc(h, k))
            inc.n += inc_k
            return inc.n
        self.q[eng].append(fn)
        return None

    def barrier(self, sems):
        for eng in self.ENGS:
            for s in sems:
                self._wait(eng, s, s.n)

    def flush(self):
        with self.nc.Block() as block:
            for eng in self.ENGS:
                ops = self.q[eng]
                if ops:
                    getattr(block, eng)(lambda e, ops=ops: [f(e) for f in ops])
        self.q = {e: [] for e in self.ENGS}


def build(debug=False):
    nc = bass.Bass("TRN2", target_bir_lowering=False)
    import ml_dtypes

    x_d = nc.dram_tensor("x", [TT, P, D], F32, kind="ExternalInput").ap()
    wmod_d = nc.dram_tensor("wmod", [NMS, DT, P, MSW], F32, kind="ExternalInput").ap()
    ccol_d = nc.dram_tensor("ccol", [P, DT], F32, kind="ExternalInput").ap()
    wqkv_d = nc.dram_tensor("wqkv", [H, 3, DT, P, E], F32, kind="ExternalInput").ap()
    wf_d = nc.dram_tensor("wf", [DT, P, H], F32, kind="ExternalInput").ap()
    wo_d = nc.dram_tensor("wo", [DT, P, D], F32, kind="ExternalInput").ap()
    nw_d = nc.dram_tensor("nwcol", [P, DT], F32, kind="ExternalInput").ap()
    wg_d = nc.dram_tensor("wg", [NS, 2, DT, P, E], F32, kind="ExternalInput").ap()
    wl_d = nc.dram_tensor("wl", [NQ, NS, P, QW], F32, kind="ExternalInput").ap()
    y_d = nc.dram_tensor("y", [TT, P, D], F32, kind="ExternalOutput").ap()

    x1_hbm = nc.dram_tensor("x1_scratch", [TT, P, D], F32, kind="Internal").ap()
    g_hbm = nc.dram_tensor("g_scratch", [2, 1, D], BF16, kind="Internal").ap()
    act_hbm = nc.dram_tensor("act_scratch", [TSUP, NS // 2, P, 2 * QW], F8, kind="Internal").ap()

    eye_bf_d = nc.inline_tensor(np.eye(P, dtype=ml_dtypes.bfloat16), "eye_bf_c")
    eye16_d = nc.inline_tensor(np.eye(16, dtype=np.float32), "eye16_c")
    tri_np = np.triu(np.ones((P, P), dtype=np.float32))
    tri_d = nc.inline_tensor(tri_np, "tri_c")
    mask_d = nc.inline_tensor(tri_np.astype(ml_dtypes.bfloat16), "mask_c")
    ones_f_d = nc.inline_tensor(np.ones((1, P), dtype=np.float32), "ones_f_c")
    ones_b_d = nc.inline_tensor(np.ones((1, P), dtype=ml_dtypes.bfloat16), "ones_b_c")
    onescol_d = nc.inline_tensor(np.ones((P, 1), dtype=np.float32), "ones_col_c")
    onescol_b_d = nc.inline_tensor(np.ones((P, 1), dtype=ml_dtypes.bfloat16), "ones_colb_c")
    constvals = [LNEPS, RMSEPS]
    constv_d = nc.inline_tensor(
        np.broadcast_to(np.asarray(constvals, np.float32), (P, len(constvals))).copy(),
        "constv_c")

    dbg = {}
    es = ExitStack()

    def sbl(name, shape, dt):
        return es.enter_context(nc.sbuf_tensor(name, shape, dt, side="left"))

    eye_bf = sbl("eye_bf", [P, P], BF16)
    eye16 = sbl("eye16", [16, 16], F32)
    tri = sbl("tri", [P, P], F32)
    maskb = sbl("maskb", [P, P], BF16)
    ones_f = sbl("ones_f", [1, P], F32)
    ones_b = sbl("ones_b", [1, P], BF16)
    sacol = sbl("sacol", [P, DT], F32)
    cacol = sbl("cacol", [P, DT], F32)
    sbcol = sbl("sbcol", [P, DT], F32)
    cbcol = sbl("cbcol", [P, DT], F32)
    mean1 = sbl("mean1", [P, TT], F32)
    rstd1 = sbl("rstd1", [P, TT], F32)
    mean2 = sbl("mean2", [P, TT], F32)
    rstd2 = sbl("rstd2", [P, TT], F32)
    msum = sbl("msum", [P, TT], F32)
    sqsum = sbl("sqsum", [P, TT], F32)
    rmsr = sbl("rmsr", [P, TT], F32)

    e1t = sbl("e1t", [P, NCH * H], F32)
    e2t = sbl("e2t", [P, NCH * H], F32)
    e3t = sbl("e3t", [P, (NCH - 1) * H], F32)
    e4bc = sbl("e4bc", [P, (NCH - 1) * H], F32)

    Sbf = sbl("Sbf", [P, E], BF16)
    tmp1 = sbl("tmp1", [P, 2 * TT], F32)
    nwcol = sbl("nwcol_sb", [P, DT], F32)

    constv = sbl("constv", [P, 2], F32)
    onescol = sbl("onescol", [P, 1], F32)
    onescol_b = sbl("onescol_b", [P, 1], BF16)
    totrow = sbl("totrow", [1, NCH * H], F32)
    nc.const_aps.register(F32, LNEPS, constv[:, 0:1])
    nc.const_aps.register(F32, RMSEPS, constv[:, 1:2])

    sT = Sem(nc, es, "sT")
    sV = Sem(nc, es, "sV")
    sS = Sem(nc, es, "sS")
    dA = Sem(nc, es, "dA")
    dB = Sem(nc, es, "dB")
    dC = Sem(nc, es, "dC")
    dE = Sem(nc, es, "dE")
    dF = Sem(nc, es, "dF")
    dG = Sem(nc, es, "dG")
    dOut = Sem(nc, es, "dOut")
    dOut2 = Sem(nc, es, "dOut2")
    dH = Sem(nc, es, "dH")
    dI = Sem(nc, es, "dI")
    dJ = Sem(nc, es, "dJ")
    dK = Sem(nc, es, "dK")
    dL = Sem(nc, es, "dL")
    dM = Sem(nc, es, "dM")
    dN = Sem(nc, es, "dN")
    dO = Sem(nc, es, "dO")
    dP = Sem(nc, es, "dP")
    dQ = Sem(nc, es, "dQ")
    ALLSEMS = [sT, sV, sS, dA, dB, dC, dE, dF, dG, dOut, dOut2, dH, dI, dJ, dK,
               dL, dM, dN, dO, dP, dQ]

    pr = Prog(nc)

    def dma(qeng, dst, src, sem, waits=()):
        return pr.op(qeng, lambda e, d=dst, s=src: e.dma_start(out=d, in_=s),
                     waits=waits, inc=sem, inc_k=16)

    def BARRIER():
        pr.barrier(ALLSEMS)

    big1 = ExitStack()
    h1T = big1.enter_context(nc.sbuf_tensor("h1T", [P, DT * D], BF16, side="left"))
    oT = big1.enter_context(nc.sbuf_tensor("oT", [P, H * D], BF16, side="left"))

    _sc1 = ExitStack()
    wmbufs = [_sc1.enter_context(nc.sbuf_tensor(f"wmbuf{i}", [P, DT * MSW], BF16,
                                                side="right")) for i in range(4)]
    ccol = _sc1.enter_context(nc.sbuf_tensor("ccol_sb", [P, DT], F32, side="right"))
    silc = _sc1.enter_context(nc.sbuf_tensor("silc", [P, DT], F32, side="right"))
    silcb = _sc1.enter_context(nc.sbuf_tensor("silcb", [P, DT], BF16, side="right"))
    rowtA = _sc1.enter_context(nc.sbuf_tensor("rowtA", [1, MSW], F32, side="right"))
    rowtB = _sc1.enter_context(nc.sbuf_tensor("rowtB", [1, MSW], F32, side="right"))
    rowsgn = _sc1.enter_context(nc.sbuf_tensor("rowsgn", [1, MSW], F32, side="right"))
    rowabs = _sc1.enter_context(nc.sbuf_tensor("rowabs", [1, MSW], F32, side="right"))
    xrA = _sc1.enter_context(nc.sbuf_tensor("xrA", [P, D], F32, side="right"))
    xrB = _sc1.enter_context(nc.sbuf_tensor("xrB", [P, D], F32, side="right"))
    garow = _sc1.enter_context(nc.sbuf_tensor("garow", [1, D], BF16, side="right"))
    gbrow = _sc1.enter_context(nc.sbuf_tensor("gbrow", [1, D], BF16, side="right"))
    lnxb0 = _sc1.enter_context(nc.sbuf_tensor("lnxb0", [P, D], BF16, side="right"))
    lnxb1 = _sc1.enter_context(nc.sbuf_tensor("lnxb1", [P, D], BF16, side="right"))
    ps_m0 = _sc1.enter_context(nc.psum_tensor("ps_m0", [P, 512], F32))
    ps_m1 = _sc1.enter_context(nc.psum_tensor("ps_m1", [P, 512], F32))
    ps_col = _sc1.enter_context(nc.psum_tensor("ps_col", [P, 512], F32))
    ps_t0 = _sc1.enter_context(nc.psum_tensor("ps_t0", [P, 1024], BF16))
    ps_t1 = _sc1.enter_context(nc.psum_tensor("ps_t1", [P, 1024], BF16))
    ps_t2 = _sc1.enter_context(nc.psum_tensor("ps_t2", [P, 1024], BF16))
    ps_t3 = _sc1.enter_context(nc.psum_tensor("ps_t3", [P, 1024], BF16))
    if True:
        for dst, srcd in ((eye_bf, eye_bf_d), (eye16, eye16_d), (tri, tri_d),
                          (maskb, mask_d), (ones_f, ones_f_d), (ones_b, ones_b_d)):
            dma("sync", dst[:, :], srcd.ap(), dC)
        dma("sync", constv[:, :], constv_d.ap(), dC)
        dma("sync", onescol[:, :], onescol_d.ap(), dC)
        dma("sync", onescol_b[:, :], onescol_b_d.ap(), dC)
        dma("sync", ccol[:, :], ccol_d, dC)
        dma("sync", nwcol[:, :], nw_d, dC)
        cdone = dC.n

        sil_done = pr.op("scalar", lambda e: e.activation(
            out=silc[:, :], in_=ccol[:, :], func=AF.Silu),
            waits=[(dC, cdone)], inc=sS)
        silb_done = pr.op("vector", lambda e: e.tensor_copy(
            out=silcb[:, :], in_=silc[:, :]), waits=[(sS, sil_done)], inc=sV)

        SEGCOL = {0: (sacol, 0.0), 1: (cacol, 1.0), 3: (sbcol, 0.0), 4: (cbcol, 1.0)}
        mod_state = {
            "mv_done": [0, 0, 0, 0],
            "dma": {},
            "ev": [0, 0],
            "colv": 0,
            "clampv": 0,
            "clamps": 0,
        }

        def emit_mod_dma(ms):
            st = mod_state
            slot = ms % 4
            buf = wmbufs[slot]
            dsem = (dF, dG, dJ, dK)[slot]
            st["dma"][ms] = (dsem, dma(
                "gpsimd", buf[:, :].rearrange("p (dt c) -> p dt c", dt=DT),
                wmod_d[ms].rearrange("dt p c -> p dt c"), dsem,
                waits=[(sT, st["mv_done"][slot]), (dsem, dsem.n)]))

        def emit_mod_slice(ms):
            st = mod_state
            if ms not in st["dma"]:
                emit_mod_dma(ms)
            slot = ms % 4
            par = ms % 2
            buf = wmbufs[slot]
            dsem, cnt = st["dma"][ms]
            rowt = rowtA if par == 0 else rowtB
            psm = ps_m0 if par == 0 else ps_m1
            md = None
            for dt_ in range(DT):
                md = pr.op("tensor",
                           lambda e, dt_=dt_, buf=buf, psm=psm: e.matmul(
                               psm[0:1, 0:MSW], silcb[:, dt_:dt_ + 1],
                               buf[:, dt_ * MSW:(dt_ + 1) * MSW],
                               start=(dt_ == 0), stop=(dt_ == DT - 1)),
                           waits=([(dsem, cnt), (sV, silb_done),
                                   (sV, st["ev"][par])] if dt_ == 0 else []),
                           inc=(sT if dt_ == DT - 1 else None))
            st["mv_done"][slot] = md
            ev = pr.op("vector", lambda e, psm=psm, rowt=rowt: e.tensor_copy(
                out=rowt[0:1, :], in_=psm[0:1, 0:MSW]),
                waits=[(sT, md), (sV, st["clampv"]), (sV, st["colv"]),
                       (sS, st["clamps"])], inc=sV)
            st["ev"][par] = ev
            seg = ms // 8
            if seg in SEGCOL:
                dstcol, addone = SEGCOL[seg]
                cm = None
                for k in range(2):
                    dtile = (ms % 8) * 2 + k
                    cm = pr.op("tensor", lambda e, k=k, dtile=dtile, rowt=rowt:
                               e.matmul(ps_col[:, dtile:dtile + 1],
                                        rowt[0:1, k * P:(k + 1) * P],
                                        ones_f[0:1, 0:1], start=True, stop=True),
                               waits=[(sV, ev), (sV, st["colv"])],
                               inc=(sT if k == 1 else None))
                d0 = (ms % 8) * 2
                st["colv"] = pr.op("vector",
                                   lambda e, d0=d0, dstcol=dstcol, addone=addone:
                                   e.tensor_scalar(
                                       out=dstcol[:, d0:d0 + 2],
                                       in0=ps_col[:, d0:d0 + 2],
                                       scalar1=float(addone), scalar2=None,
                                       op0=ALU.add),
                                   waits=[(sT, cm)], inc=sV)
            else:
                target = garow if seg == 2 else gbrow
                off = (ms % 8) * MSW
                s1_ = pr.op("scalar", lambda e, rowt=rowt: e.activation(
                    out=rowsgn[0:1, :], in_=rowt[0:1, :], func=AF.Sign),
                    waits=[(sV, ev), (sV, st["clampv"])], inc=sS)
                s2_ = pr.op("scalar", lambda e, rowt=rowt: e.activation(
                    out=rowabs[0:1, :], in_=rowt[0:1, :], func=AF.Abs),
                    waits=[(sS, s1_)], inc=sS)
                st["clamps"] = s2_
                v1_ = pr.op("vector", lambda e: e.tensor_scalar_max(
                    out=rowabs[0:1, :], in0=rowabs[0:1, :], scalar1=0.25),
                    waits=[(sS, s2_)], inc=sV)
                st["clampv"] = pr.op("vector",
                                     lambda e, target=target, off=off:
                                     e.tensor_tensor(
                                         out=target[0:1, off:off + MSW],
                                         in0=rowsgn[0:1, :], in1=rowabs[0:1, :],
                                         op=ALU.mult),
                                     waits=[(sV, v1_)], inc=sV)

        xrP = (xrA, xrB)
        xsemP = (dC, dE)
        vdoneP = [0, 0]
        sdoneP = [0, 0]
        for t in range(TT):
            par = t % 2
            w = ([(sV, vdoneP[par]), (sS, sdoneP[par])] if t >= 2
                 else [(dC, cdone)])
            cnt = dma("sync", xrP[par][:, :], x_d[t], xsemP[par], waits=w)
            vdoneP[par] = pr.op("vector", lambda e, par=par, t=t: e.tensor_reduce(
                out=msum[:, t:t + 1], in_=xrP[par][:, :], axis=AX.X, op=ALU.add),
                waits=[(xsemP[par], cnt)], inc=sV)
            sdoneP[par] = pr.op("scalar", lambda e, par=par, t=t: e.activation(
                out=lnxb0[:, :], in_=xrP[par][:, :], func=AF.Square,
                accum_out=sqsum[:, t:t + 1]),
                waits=[(xsemP[par], cnt), (sS, max(sdoneP))], inc=sS)

        for ms in range(24):
            emit_mod_slice(ms)
        saca_done = mod_state["colv"]

        mean_done = pr.op("vector", lambda e: e.tensor_scalar(
            out=mean1[:, :], in0=msum[:, :], scalar1=1.0 / D, scalar2=None,
            op0=ALU.mult), waits=[(sV, max(vdoneP)), (sS, max(sdoneP))], inc=sV)
        c1 = pr.op("vector", lambda e: e.tensor_scalar(
            out=tmp1[:, 0:TT], in0=sqsum[:, :], scalar1=1.0 / D, scalar2=None,
            op0=ALU.mult), waits=[(sV, mean_done)], inc=sV)
        c2 = pr.op("vector", lambda e: e.tensor_tensor(
            out=tmp1[:, 16:32], in0=mean1[:, :], in1=mean1[:, :], op=ALU.mult),
            waits=[(sV, c1)], inc=sV)
        vstat = pr.op("vector", lambda e: e.tensor_tensor(
            out=tmp1[:, 0:TT], in0=tmp1[:, 0:TT], in1=tmp1[:, 16:32],
            op=ALU.subtract), waits=[(sV, c2)], inc=sV)
        ln1d = pr.op("scalar", lambda e: e.activation(
            out=tmp1[:, 0:TT], in_=tmp1[:, 0:TT], func=AF.Ln, bias=LNEPS,
            scale=1.0), waits=[(sV, vstat)], inc=sS)
        rstd_done = pr.op("scalar", lambda e: e.activation(
            out=rstd1[:, :], in_=tmp1[:, 0:TT], func=AF.Exp, scale=-0.5),
            waits=[(sS, ln1d)], inc=sS)

        xr = (xrA, xrB)
        xsem = (dC, dE)
        vdone = vdoneP
        ms_next = 24
        ms_dma = 24
        tile_tr_done = [0, 0]
        ev_done = [(sS, 0), (sS, 0)]
        pstx = (ps_t0, ps_t1, ps_t2, ps_t3)
        ev_done = [(sS, 0)] * 4
        for t in range(TT):
            par = t % 2
            lnxb = lnxb0 if par == 0 else lnxb1
            w = [(sV, vdone[par]), (sS, max(sdoneP)), (sS, rstd_done)]
            cnt = dma("sync", xr[par][:, :], x_d[t], xsem[par], waits=w)
            vdone[par] = pr.op("vector", lambda e, par=par, t=t, lnxb=lnxb:
                               e.tensor_scalar(
                                   out=lnxb[:, :], in0=xr[par][:, :],
                                   scalar1=mean1[:, t:t + 1],
                                   scalar2=rstd1[:, t:t + 1],
                                   op0=ALU.subtract, op1=ALU.mult),
                               waits=[(xsem[par], cnt), (sS, rstd_done),
                                      (sV, mean_done), (sT, tile_tr_done[par])],
                               inc=sV)
            for dt_ in range(DT):
                ppar = dt_ % 4
                ptr = pstx[ppar]
                td = pr.op("tensor", lambda e, dt_=dt_, ptr=ptr, lnxb=lnxb:
                           e.transpose(ptr[:, 0:P], lnxb[:, dt_ * P:(dt_ + 1) * P],
                                       eye_bf[:, :]),
                           waits=[(sV, vdone[par]), ev_done[ppar]], inc=sT)
                if dt_ % 2 == 0:
                    ev_done[ppar] = (sS, pr.op(
                        "scalar", lambda e, dt_=dt_, t=t, ptr=ptr: e.activation(
                            out=h1T[:, dt_ * D + t * P: dt_ * D + (t + 1) * P],
                            in_=ptr[:, 0:P], func=AF.Identity,
                            scale=cacol[:, dt_:dt_ + 1],
                            bias=sacol[:, dt_:dt_ + 1]),
                        waits=[(sT, td), (sV, saca_done)], inc=sS))
                else:
                    ev_done[ppar] = (sV, pr.op(
                        "vector", lambda e, dt_=dt_, t=t, ptr=ptr: e.tensor_scalar(
                            out=h1T[:, dt_ * D + t * P: dt_ * D + (t + 1) * P],
                            in0=ptr[:, 0:P],
                            scalar1=cacol[:, dt_:dt_ + 1],
                            scalar2=sacol[:, dt_:dt_ + 1],
                            op0=ALU.mult, op1=ALU.add),
                        waits=[(sT, td), (sV, saca_done)], inc=sV))
            tile_tr_done[par] = sT.n
            while ms_dma < NMS and ms_dma < 24 + (t + 1) * 2:
                emit_mod_dma(ms_dma)
                ms_dma += 1
            while ms_next < NMS and ms_next < 24 + t * 2:
                emit_mod_slice(ms_next)
                ms_next += 1
        while ms_next < NMS:
            emit_mod_slice(ms_next)
            ms_next += 1
        dma("gpsimd", g_hbm[0], garow[0:1, :], dOut,
            waits=[(sV, mod_state["clampv"])])
        dma("gpsimd", g_hbm[1], gbrow[0:1, :], dOut,
            waits=[(sV, mod_state["clampv"])])
        BARRIER()
        if debug:
            dbg["h1T"] = nc.dram_tensor("dbg_h1T", [P, DT * D], BF16,
                                        kind="ExternalOutput").ap()
            dma("gpsimd", dbg["h1T"], h1T[:, :], dOut)
            dbg["garow"] = nc.dram_tensor("dbg_garow", [1, D], BF16,
                                          kind="ExternalOutput").ap()
            dbg["sacol"] = nc.dram_tensor("dbg_sacol", [P, DT], F32,
                                          kind="ExternalOutput").ap()
            dbg["cacol"] = nc.dram_tensor("dbg_cacol", [P, DT], F32,
                                          kind="ExternalOutput").ap()
            dma("gpsimd", dbg["garow"], garow[0:1, :], dOut)
            dma("gpsimd", dbg["sacol"], sacol[:, :], dOut)
            dma("gpsimd", dbg["cacol"], cacol[:, :], dOut)
            BARRIER()
        pr.flush()
    _sc1.close()

    _sc3 = ExitStack()
    wfbuf = _sc3.enter_context(nc.sbuf_tensor("wfbuf", [P, DT * H], F32, side="right"))
    wfbf = _sc3.enter_context(nc.sbuf_tensor("wfbf", [P, DT * H], BF16, side="right"))
    ldT = _sc3.enter_context(nc.sbuf_tensor("ldT", [16, D], F32, side="right"))
    cumall = _sc3.enter_context(nc.sbuf_tensor("cumall", [P, NCH * H], F32, side="right"))
    ldtok = _sc3.enter_context(nc.sbuf_tensor("ldtok", [P, NCH * H], F32, side="right"))
    tmp4 = _sc3.enter_context(nc.sbuf_tensor("tmp4", [P, NCH * H], F32, side="right"))
    totbc = _sc3.enter_context(nc.sbuf_tensor("totbc", [P, NCH * H], F32, side="right"))
    ps_l0 = _sc3.enter_context(nc.psum_tensor("ps_l0", [16, 512], F32))
    ps_l1 = _sc3.enter_context(nc.psum_tensor("ps_l1", [16, 512], F32))
    ps_c0 = _sc3.enter_context(nc.psum_tensor("ps_c0", [P, 512], F32))
    ps_c1 = _sc3.enter_context(nc.psum_tensor("ps_c1", [P, 512], F32))
    if True:

        ps_c = (ps_c0, ps_c1)
        cnt = dma("sync", wfbuf[:, :].rearrange("p (dt h) -> p dt h", dt=DT), wf_d.rearrange("dt p h -> p dt h"), dA)
        wfc = pr.op("vector", lambda e: e.tensor_copy(out=wfbf[:, :], in_=wfbuf[:, :]),
                    waits=[(dA, cnt)], inc=sV)
        ld_ev = [0, 0]
        for ch in range(4):
            lp = ch % 2
            pld = ps_l0 if lp == 0 else ps_l1
            md = None
            for dt_ in range(DT):
                md = pr.op("tensor", lambda e, ch=ch, dt_=dt_, pld=pld: e.matmul(
                    pld[:, :], wfbf[:, dt_ * H:(dt_ + 1) * H],
                    h1T[:, dt_ * D + ch * 512: dt_ * D + (ch + 1) * 512],
                    start=(dt_ == 0), stop=(dt_ == DT - 1)),
                    waits=[(sV, wfc), (sS, ld_ev[lp])] if dt_ == 0 else [],
                    inc=(sT if dt_ == DT - 1 else None))
            ld_ev[lp] = pr.op("scalar", lambda e, ch=ch, pld=pld: e.activation(
                out=ldT[:, ch * 512:(ch + 1) * 512], in_=pld[:, :], func=AF.Sigmoid),
                waits=[(sT, md)], inc=sS)
        lds = pr.op("scalar", lambda e: e.activation(
            out=ldT[:, :], in_=ldT[:, :], func=AF.Ln),
            waits=[(sS, max(ld_ev))], inc=sS)

        pev = [0, 0]
        for c in range(NCH):
            cp = c % 2
            td = pr.op("tensor", lambda e, c=c, cp=cp: e.transpose(
                ps_c[cp][0:P, 0:16], ldT[:, c * P:(c + 1) * P], eye16[:, :]),
                waits=[(sS, lds), (sV, pev[cp])], inc=sT)
            pev[cp] = pr.op("vector", lambda e, c=c, cp=cp: e.tensor_copy(
                out=ldtok[:, c * H:(c + 1) * H], in_=ps_c[cp][0:P, 0:16]),
                waits=[(sT, td)], inc=sV)
        ld_all = sV.n
        for c in range(NCH):
            cp = c % 2
            pr.op("tensor", lambda e, c=c, cp=cp: e.matmul(
                ps_c[cp][:, 0:16], tri[:, :], ldtok[:, c * H:(c + 1) * H],
                start=True, stop=True),
                waits=[(sV, ld_all), (sV, pev[cp])])
            td = pr.op("tensor", lambda e, c=c, cp=cp: e.matmul(
                ps_c[cp][0:1, 16:32], onescol[:, :], ldtok[:, c * H:(c + 1) * H],
                start=True, stop=True), inc=sT)
            pr.op("vector", lambda e, c=c, cp=cp: e.tensor_copy(
                out=cumall[:, c * H:(c + 1) * H], in_=ps_c[cp][:, 0:16]),
                waits=[(sT, td)])
            pev[cp] = pr.op("vector", lambda e, c=c, cp=cp: e.tensor_copy(
                out=totrow[0:1, c * H:(c + 1) * H], in_=ps_c[cp][0:1, 16:32]),
                inc=sV)
        cum_done = sV.n

        nbc = NCH * H // P
        for i in range(nbc):
            cp = i % 2
            td = pr.op("tensor", lambda e, i=i, cp=cp: e.matmul(
                ps_c[cp][:, 0:P], ones_f[0:1, :], totrow[0:1, i * P:(i + 1) * P],
                start=True, stop=True), waits=[(sV, cum_done), (sV, pev[cp])], inc=sT)
            pev[cp] = pr.op("vector", lambda e, i=i, cp=cp: e.tensor_copy(
                out=totbc[:, i * P:(i + 1) * P], in_=ps_c[cp][:, 0:P]),
                waits=[(sT, td)], inc=sV)
        bc_ev = sV.n

        vt = pr.op("vector", lambda e: e.scalar_tensor_tensor(
            out=tmp4[:, :], in0=totbc[:, :], scalar=0.5, in1=cumall[:, :],
            op0=ALU.mult, op1=ALU.subtract), waits=[(sV, bc_ev)], inc=sV)
        pr.op("scalar", lambda e: e.activation(out=e1t[:, :], in_=tmp4[:, :],
                                               func=AF.Exp), waits=[(sV, vt)])
        e12 = pr.op("scalar", lambda e: e.activation(out=e2t[:, :], in_=tmp4[:, :],
                                                     func=AF.Exp, scale=-1.0), inc=sS)
        NE = (NCH - 1) * H
        p3 = pr.op("vector", lambda e: e.tensor_tensor(
            out=tmp4[:, 0:NE], in0=totbc[:, 0:NE], in1=cumall[:, 0:NE],
            op=ALU.subtract), waits=[(sS, e12)], inc=sV)
        vt3 = pr.op("vector", lambda e: e.scalar_tensor_tensor(
            out=tmp4[:, 0:NE], in0=totbc[:, H:H + NE], scalar=0.5, in1=tmp4[:, 0:NE],
            op0=ALU.mult, op1=ALU.add), waits=[(sV, p3)], inc=sV)
        e3d = pr.op("scalar", lambda e: e.activation(
            out=e3t[:, :], in_=tmp4[:, 0:NE], func=AF.Exp), waits=[(sV, vt3)], inc=sS)
        vt4 = pr.op("vector", lambda e: e.tensor_tensor(
            out=tmp4[0:1, 0:NE], in0=totrow[0:1, 0:NE],
            in1=totrow[0:1, H:H + NE], op=ALU.add), waits=[(sS, e3d)], inc=sV)
        e4r = pr.op("scalar", lambda e: e.activation(
            out=tmp4[0:1, 0:NE], in_=tmp4[0:1, 0:NE], func=AF.Exp, scale=0.5),
            waits=[(sV, vt4)], inc=sS)
        for i in range((NE + P - 1) // P):
            w_ = min(P, NE - i * P)
            cp = i % 2
            td = pr.op("tensor", lambda e, i=i, w_=w_, cp=cp: e.matmul(
                ps_c[cp][:, 0:w_], ones_f[0:1, :], tmp4[0:1, i * P:i * P + w_],
                start=True, stop=True), waits=[(sS, e4r), (sV, pev[cp])], inc=sT)
            pev[cp] = pr.op("vector", lambda e, i=i, w_=w_, cp=cp: e.tensor_copy(
                out=e4bc[:, i * P:i * P + w_], in_=ps_c[cp][:, 0:w_]),
                waits=[(sT, td)], inc=sV)
        BARRIER()
        if debug:
            for nm, tens, wdt in (("ldtok", ldtok, NCH * H), ("cumall", cumall, NCH * H),
                                  ("e1t", e1t, NCH * H), ("e2t", e2t, NCH * H),
                                  ("e3t", e3t, NE), ("e4bc", e4bc, NE)):
                dbg[nm] = nc.dram_tensor("dbg_" + nm, [P, wdt], F32,
                                         kind="ExternalOutput").ap()
                dma("gpsimd", dbg[nm], tens[:, :], dOut)
            BARRIER()
        pr.flush()
    _sc3.close()

    _sc4 = ExitStack()
    wqbf0 = _sc4.enter_context(nc.sbuf_tensor("wqbf0", [P, D], BF16, side="right"))
    wkbf0 = _sc4.enter_context(nc.sbuf_tensor("wkbf0", [P, D], BF16, side="right"))
    wvbf0 = _sc4.enter_context(nc.sbuf_tensor("wvbf0", [P, D], BF16, side="right"))
    wqbf1 = _sc4.enter_context(nc.sbuf_tensor("wqbf1", [P, D], BF16, side="right"))
    wkbf1 = _sc4.enter_context(nc.sbuf_tensor("wkbf1", [P, D], BF16, side="right"))
    wvbf1 = _sc4.enter_context(nc.sbuf_tensor("wvbf1", [P, D], BF16, side="right"))
    qT0 = _sc4.enter_context(nc.sbuf_tensor("qT0", [P, D], BF16, side="right"))
    qT1 = _sc4.enter_context(nc.sbuf_tensor("qT1", [P, D], BF16, side="right"))
    kT0 = _sc4.enter_context(nc.sbuf_tensor("kT0", [P, D], BF16, side="right"))
    kT1 = _sc4.enter_context(nc.sbuf_tensor("kT1", [P, D], BF16, side="right"))
    vT = _sc4.enter_context(nc.sbuf_tensor("vT", [P, D], BF16, side="right"))
    ktok0 = _sc4.enter_context(nc.sbuf_tensor("ktok0", [P, D], BF16, side="right"))
    ktok1 = _sc4.enter_context(nc.sbuf_tensor("ktok1", [P, D], BF16, side="right"))
    vtok0 = _sc4.enter_context(nc.sbuf_tensor("vtok0", [P, D], BF16, side="right"))
    vtok1 = _sc4.enter_context(nc.sbuf_tensor("vtok1", [P, D], BF16, side="right"))
    s3b = [[_sc4.enter_context(nc.sbuf_tensor(f"s3b{i}{j}", [P, P], BF16, side="right"))
            for j in range(2)] for i in range(2)]
    obb = [[_sc4.enter_context(nc.sbuf_tensor(f"obb{i}{j}", [P, P], BF16, side="right"))
            for j in range(2)] for i in range(2)]
    kdb = [[_sc4.enter_context(nc.sbuf_tensor(f"kdb{i}{j}", [P, P], BF16, side="right"))
            for j in range(2)] for i in range(2)]
    junk5 = _sc4.enter_context(nc.sbuf_tensor("junk5", [P, P], BF16, side="right"))
    Sbf1 = _sc4.enter_context(nc.sbuf_tensor("Sbf1", [P, E], BF16, side="right"))
    ps_p0 = _sc4.enter_context(nc.psum_tensor("ps_p0", [P, 512], F32))
    ps_p1 = _sc4.enter_context(nc.psum_tensor("ps_p1", [P, 512], F32))
    ps_s0 = _sc4.enter_context(nc.psum_tensor("ps_s0", [P, 512], F32))
    ps_s1 = _sc4.enter_context(nc.psum_tensor("ps_s1", [P, 512], F32))
    ps_S0 = _sc4.enter_context(nc.psum_tensor("ps_S0", [P, 512], F32))
    ps_S1 = _sc4.enter_context(nc.psum_tensor("ps_S1", [P, 512], F32))
    ps_x0 = _sc4.enter_context(nc.psum_tensor("ps_x0", [P, 1024], BF16))
    ps_x1 = _sc4.enter_context(nc.psum_tensor("ps_x1", [P, 1024], BF16))
    if True:
        wsem = (dA, dB)
        qTs = (qT0, qT1)
        kTs = (kT0, kT1)
        ktoks = (ktok0, ktok1)
        vtoks = (vtok0, vtok1)
        Sbfs = (Sbf, Sbf1)
        ps_ss = (ps_s0, ps_s1)
        ps_Ss = (ps_S0, ps_S1)
        ps_os = (ps_p0, ps_p1)
        ps_xs = (ps_x0, ps_x1)
        wbfs2 = ({0: wqbf0, 1: wkbf0, 2: wvbf0}, {0: wqbf1, 1: wkbf1, 2: wvbf1})

        bank_free = [(sS, 0), (sS, 0)]
        trx_free = [(sV, 0), (sV, 0)]
        wsem_i = 0
        cast_done = {}
        mat_last_mm = [[0, 0, 0], [0, 0, 0]]

        def weight_chain(h):
            nonlocal wsem_i
            for m in range(3):
                sem = (dF, dG)[wsem_i % 2]
                wsem_i += 1
                cnt = dma("gpsimd",
                          wbfs2[h % 2][m][:, :].rearrange("p (dt e) -> p dt e", dt=DT),
                          wqkv_d[h][m].rearrange("dt p e -> p dt e"), sem,
                          waits=[(sT, mat_last_mm[h % 2][m]), (sem, sem.n)])
                cast_done[(h, m)] = (sem, cnt)

        weight_chain(0)
        weight_chain(1)

        supd = [0, 0]
        m2m3 = [0, 0]
        m4d = [0, 0]
        oev = [[(sS, 0), (sS, 0)], [(sS, 0), (sS, 0)]]
        s1ev = [0, 0]
        kdv = [[0, 0], [0, 0]]
        otr = [[(sV, 0), (sV, 0)], [(sV, 0), (sV, 0)]]
        sq = [0, 0]
        sq_add = 0
        for p in range(H // 2):
            ha, hb = 2 * p, 2 * p + 1

            def proj(h, m, dstT, func):
                lastmm = 0
                for ch in range(4):
                    ppar = ch % 2
                    pp = ps_os[ppar]
                    wbf = wbfs2[h % 2][m]
                    md = None
                    for dt_ in range(DT):
                        md = pr.op("tensor",
                                   lambda e, ch=ch, dt_=dt_, pp=pp, wbf=wbf:
                                   e.matmul(pp[:, :], wbf[:, dt_ * P:(dt_ + 1) * P],
                                            h1T[:, dt_ * D + ch * 512: dt_ * D + (ch + 1) * 512],
                                            start=(dt_ == 0), stop=(dt_ == DT - 1)),
                                   waits=([cast_done[(h, m)], bank_free[ppar]]
                                          if dt_ == 0 else []),
                                   inc=(sT if dt_ == DT - 1 else None))
                    lastmm = md
                    if (ch + m) % 2 == 0:
                        bank_free[ppar] = (sS, pr.op(
                            "scalar", lambda e, ch=ch, pp=pp, dstT=dstT, func=func:
                            e.activation(out=dstT[:, ch * 512:(ch + 1) * 512],
                                         in_=pp[:, :], func=func),
                            waits=[(sT, md)], inc=sS))
                    else:
                        if func == AF.Silu:
                            bank_free[ppar] = (sS, pr.op(
                                "scalar", lambda e, ch=ch, pp=pp, dstT=dstT:
                                e.activation(out=dstT[:, ch * 512:(ch + 1) * 512],
                                             in_=pp[:, :], func=AF.Silu),
                                waits=[(sT, md)], inc=sS))
                        else:
                            bank_free[ppar] = (sV, pr.op(
                                "vector", lambda e, ch=ch, pp=pp, dstT=dstT:
                                e.tensor_copy(out=dstT[:, ch * 512:(ch + 1) * 512],
                                              in_=pp[:, :]),
                                waits=[(sT, md)], inc=sV))
                mat_last_mm[h % 2][m] = lastmm
                return bank_free

            def transp_all(srcT, dst, hx):
                for g in range(NCH // 4):
                    xp = g % 2
                    pt = ps_xs[xp]
                    td = 0
                    for j in range(4):
                        c = g * 4 + j
                        td = pr.op("tensor",
                                   lambda e, c=c, j=j, pt=pt, srcT=srcT: e.transpose(
                                       pt[:, j * P:(j + 1) * P],
                                       srcT[:, c * P:(c + 1) * P], eye_bf[:, :]),
                                   waits=([bank_free[0], bank_free[1], trx_free[xp]]
                                          if j == 0 else []),
                                   inc=(sT if j == 3 else None))
                    trx_free[xp] = (sV, pr.op(
                        "vector", lambda e, g=g, pt=pt, dst=dst: e.tensor_copy(
                            out=dst[:, g * 512:(g + 1) * 512], in_=pt[:, 0:512]),
                        waits=[(sT, td)], inc=sV))

            for h, hi in ((ha, 0), (hb, 1)):
                proj(h, 0, qTs[hi], AF.Silu)
                proj(h, 1, kTs[hi], AF.Silu)
                proj(h, 2, vT, AF.Identity)
                transp_all(kTs[hi], ktoks[hi], hi)
                transp_all(vT, vtoks[hi], hi)

            if p + 1 < H // 2:
                weight_chain(2 * p + 2)
                weight_chain(2 * p + 3)

            kvdone = (trx_free[0], trx_free[1])
            pend_o = [None, None]

            def emit_otd(hi):
                c0, h0, obuf0, oevt = pend_o[hi]
                pt = ps_xs[hi]
                otd = pr.op("tensor", lambda e, obuf0=obuf0, pt=pt: e.transpose(
                    pt[:, 0:P], obuf0[:, :], eye_bf[:, :]),
                    waits=[oevt, trx_free[hi]], inc=sT)
                if hi == 0:
                    trx_free[hi] = (sS, pr.op(
                        "scalar", lambda e, c0=c0, h0=h0, pt=pt: e.activation(
                            out=oT[:, h0 * D + c0 * P: h0 * D + (c0 + 1) * P],
                            in_=pt[:, 0:P], func=AF.Identity),
                        waits=[(sT, otd)], inc=sS))
                else:
                    trx_free[hi] = (sV, pr.op(
                        "vector", lambda e, c0=c0, h0=h0, pt=pt: e.tensor_copy(
                            out=oT[:, h0 * D + c0 * P: h0 * D + (c0 + 1) * P],
                            in_=pt[:, 0:P]),
                        waits=[(sT, otd)], inc=sV))
                otr[hi][c0 % 2] = trx_free[hi]
                pend_o[hi] = None

            def emit_m1(c, h, hi, first):
                qT = qTs[hi]
                kT = kTs[hi]
                pss = ps_ss[hi]
                s3buf = s3b[hi][c % 2]
                m1 = pr.op("tensor", lambda e, c=c, kT=kT, qT=qT, pss=pss:
                           e.matmul(pss[:, 0:P], kT[:, c * P:(c + 1) * P],
                                    qT[:, c * P:(c + 1) * P],
                                    start=True, stop=True),
                           waits=[(sV, s1ev[hi])] +
                                 ([kvdone[0], kvdone[1]] if first else []),
                           inc=sT)
                s1ev[hi] = pr.op("vector",
                                 lambda e, col=c * H + h, s3buf=s3buf, pss=pss:
                                 e.scalar_tensor_tensor(
                                     out=s3buf[:, :], in0=pss[:, 0:P],
                                     scalar=e1t[:, col:col + 1],
                                     in1=maskb[:, :],
                                     op0=ALU.mult, op1=ALU.mult),
                                 waits=[(sT, m1), (sT, m2m3[hi])], inc=sV)

            def emit_kdv(c, h, hi, first):
                kdbuf = kdb[hi][c % 2]
                ktok = ktoks[hi]
                kdv[hi][c % 2] = pr.op(
                    "scalar", lambda e, c=c, h=h, kdbuf=kdbuf, ktok=ktok:
                    e.activation(out=kdbuf[:, :],
                                 in_=ktok[:, c * P:(c + 1) * P],
                                 func=AF.Identity,
                                 scale=e3t[:, c * H + h:c * H + h + 1]),
                    waits=[(sT, m4d[hi])] +
                          ([kvdone[0], kvdone[1]] if first else []),
                    inc=sS)

            for hi in range(2):
                supd[hi] = pr.op("vector",
                                 lambda e, hi=hi: e.memset(Sbfs[hi][:, :], 0.0),
                                 waits=[(sT, m2m3[hi])], inc=sV)
            for h, hi in ((ha, 0), (hb, 1)):
                emit_m1(0, h, hi, True)
                emit_kdv(0, h, hi, True)
            for c in range(NCH):
                cp = c % 2
                for h, hi in ((ha, 0), (hb, 1)):
                    col = c * H + h
                    qT = qTs[hi]
                    vtok = vtoks[hi]
                    s3buf = s3b[hi][cp]
                    obuf = obb[hi][cp]
                    kdbuf = kdb[hi][cp]
                    po = ps_os[hi]
                    pS = ps_Ss[hi]

                    m2m3[hi] = pr.op("tensor",
                                     lambda e, c=c, po=po, s3buf=s3buf, vtok=vtok:
                                     e.matmul(po[:, 0:P], s3buf[:, :],
                                              vtok[:, c * P:(c + 1) * P],
                                              start=True, stop=(c == 0)),
                                     waits=[(sV, s1ev[hi]),
                                            (bank_free[hi] if c == 0
                                             else oev[hi][cp ^ 1])],
                                     inc=(sT if c == 0 else None))
                    if c > 0:
                        m2m3[hi] = pr.op("tensor",
                                         lambda e, c=c, po=po, qT=qT, hi=hi:
                                         e.matmul(po[:, 0:P],
                                                  qT[:, c * P:(c + 1) * P],
                                                  Sbfs[hi][:, :],
                                                  start=False, stop=True),
                                         waits=[(sV, supd[hi])], inc=sT)

                    if c < NCH - 1:
                        m4d[hi] = pr.op("tensor",
                                        lambda e, c=c, kdbuf=kdbuf, vtok=vtok, pS=pS:
                                        e.matmul(pS[:, 0:P], kdbuf[:, :],
                                                 vtok[:, c * P:(c + 1) * P],
                                                 start=True, stop=True),
                                        waits=[(sS, kdv[hi][cp]), (sV, supd[hi])],
                                        inc=sT)
                        supd[hi] = pr.op("vector",
                                         lambda e, c=c, h=h, hi=hi, pS=pS:
                                         e.scalar_tensor_tensor(
                                             out=Sbfs[hi][:, :], in0=Sbfs[hi][:, :],
                                             scalar=e4bc[:, c * H + h:c * H + h + 1],
                                             in1=pS[:, 0:P],
                                             op0=ALU.mult, op1=ALU.add),
                                         waits=[(sT, m4d[hi]), (sT, m2m3[hi]),
                                                (sV, supd[hi])], inc=sV)

                    if pend_o[hi] is not None:
                        emit_otd(hi)
                    oev[hi][cp] = (sS, pr.op(
                        "scalar", lambda e, col=col, po=po, obuf=obuf:
                        e.activation(out=obuf[:, :], in_=po[:, 0:P],
                                     func=AF.Identity,
                                     scale=e2t[:, col:col + 1]),
                        waits=[(sT, m2m3[hi]), otr[hi][cp]], inc=sS))
                    pend_o[hi] = (c, h, obuf, oev[hi][cp])
                if c + 1 < NCH:
                    for h, hi in ((ha, 0), (hb, 1)):
                        emit_m1(c + 1, h, hi, False)
                        if c + 1 < NCH - 1:
                            emit_kdv(c + 1, h, hi, False)
            for hi in range(2):
                emit_otd(hi)
            bank_free = [oev[0][(NCH - 1) % 2], oev[1][(NCH - 1) % 2]]
        BARRIER()
        if debug:
            for nm, tens in (("qT", qT1), ("kT", kT1), ("vtok", vtok1),
                             ("ktok", ktok1)):
                dbg[nm] = nc.dram_tensor("dbg_" + nm, [P, D], BF16,
                                         kind="ExternalOutput").ap()
                dma("gpsimd", dbg[nm], tens[:, :], dOut)
            dbg["oT"] = nc.dram_tensor("dbg_oT", [P, H * D], BF16,
                                       kind="ExternalOutput").ap()
            dma("gpsimd", dbg["oT"], oT[:, :], dOut)
            BARRIER()
        pr.flush()
    _sc4.close()

    wo_bf = h1T

    _sc5 = ExitStack()
    wob0 = _sc5.enter_context(nc.sbuf_tensor("wobuf0", [P, D], F32, side="right"))
    wob1 = _sc5.enter_context(nc.sbuf_tensor("wobuf1", [P, D], F32, side="right"))
    xrc0 = _sc5.enter_context(nc.sbuf_tensor("xrc0", [P, D], F32, side="right"))
    xrc1 = _sc5.enter_context(nc.sbuf_tensor("xrc1", [P, D], F32, side="right"))
    x1b0 = _sc5.enter_context(nc.sbuf_tensor("x1b0", [P, D], F32, side="right"))
    x1b1 = _sc5.enter_context(nc.sbuf_tensor("x1b1", [P, D], F32, side="right"))
    gaB = _sc5.enter_context(nc.sbuf_tensor("gaB", [P, D], BF16, side="right"))
    garow6 = _sc5.enter_context(nc.sbuf_tensor("garow6", [1, D], BF16, side="right"))
    junk6b = _sc5.enter_context(nc.sbuf_tensor("junk6b", [P, D], BF16, side="right"))
    rrow = _sc5.enter_context(nc.sbuf_tensor("rrow", [1, D], F32, side="right"))
    ps_a0 = _sc5.enter_context(nc.psum_tensor("ps_a0", [P, 512], F32))
    ps_a1 = _sc5.enter_context(nc.psum_tensor("ps_a1", [P, 512], F32))
    ps_a2 = _sc5.enter_context(nc.psum_tensor("ps_a2", [P, 512], F32))
    ps_a3 = _sc5.enter_context(nc.psum_tensor("ps_a3", [P, 512], F32))
    ps_gb = _sc5.enter_context(nc.psum_tensor("ps_gb", [P, 512], F32))
    if True:

        ps_a = (ps_a0, ps_a1, ps_a2, ps_a3)
        garc = dma("sync", garow6[0:1, :], g_hbm[0], dC)
        sqv = 0
        smm = 0
        for ft in range(DT):
            sqv = pr.op("vector", lambda e, ft=ft: e.tensor_tensor(
                out=junk6b[:, :], in0=oT[:, ft * D:(ft + 1) * D],
                in1=oT[:, ft * D:(ft + 1) * D], op=ALU.mult),
                waits=[(sT, smm)], inc=sV)
            for mc in range(4):
                smm = pr.op("tensor", lambda e, ft=ft, mc=mc: e.matmul(
                    ps_a[mc][0:1, :], onescol_b[:, 0:1],
                    junk6b[:, mc * 512:(mc + 1) * 512],
                    start=(ft == 0), stop=(ft == DT - 1)),
                    waits=[(sV, sqv)] if mc == 0 else [],
                    inc=(sT if mc == 3 else None))
        rrow_ev = 0
        for mc in range(4):
            rrow_ev = pr.op("vector", lambda e, mc=mc: e.tensor_copy(
                out=rrow[0:1, mc * 512:(mc + 1) * 512], in_=ps_a[mc][0:1, :]),
                waits=[(sT, smm)], inc=sV)
        rl = pr.op("scalar", lambda e: e.activation(
            out=rrow[0:1, :], in_=rrow[0:1, :], func=AF.Ln, bias=RMSEPS,
            scale=1.0 / D), waits=[(sV, rrow_ev)], inc=sS)
        rx = pr.op("scalar", lambda e: e.activation(
            out=rrow[0:1, :], in_=rrow[0:1, :], func=AF.Exp, scale=-0.5),
            waits=[(sS, rl)], inc=sS)
        rcm = 0
        for t in range(TT):
            rcm = pr.op("tensor", lambda e, t=t: e.matmul(
                ps_gb[:, t:t + 1], rrow[0:1, t * P:(t + 1) * P], ones_f[0:1, 0:1],
                start=True, stop=True),
                waits=[(sS, rx)], inc=(sT if t == TT - 1 else None))
        rms_done = pr.op("vector", lambda e: e.tensor_copy(
            out=rmsr[:, :], in_=ps_gb[:, 0:TT]), waits=[(sT, rcm)], inc=sV)
        rms_sem = sV

        gaB_done = 0
        gev = rms_done
        for i in range(4):
            td = pr.op("tensor", lambda e, i=i: e.matmul(
                ps_gb[:, :], ones_b[0:1, :], garow6[0:1, i * 512:(i + 1) * 512],
                start=True, stop=True), waits=[(sV, gev), (dC, garc)], inc=sT)
            gev = pr.op("vector", lambda e, i=i: e.tensor_copy(
                out=gaB[:, i * 512:(i + 1) * 512], in_=ps_gb[:, :]),
                waits=[(sT, td)], inc=sV)
        gaB_done = gev

        wsem = (dA, dB)
        wob = (wob0, wob1)
        wo_cast = [(sS, 0)] * DT
        ccount = [(sS, 0), (sS, 0)]
        for ft in range(DT):
            par = ft % 2
            w = [ccount[par]] if ft >= 2 else []
            cnt = dma("sync", wob[par][:, :], wo_d[ft], wsem[par], waits=w)
            if ft % 2 == 0:
                ccount[par] = (sS, pr.op("scalar", lambda e, ft=ft, par=par:
                                         e.activation(
                                             out=wo_bf[:, ft * D:(ft + 1) * D],
                                             in_=wob[par][:, :], func=AF.Identity,
                                             scale=nwcol[:, ft:ft + 1]),
                                         waits=[(wsem[par], cnt)], inc=sS))
            else:
                ccount[par] = (sV, pr.op("vector", lambda e, ft=ft, par=par:
                                         e.tensor_scalar(
                                             out=wo_bf[:, ft * D:(ft + 1) * D],
                                             in0=wob[par][:, :],
                                             scalar1=nwcol[:, ft:ft + 1],
                                             scalar2=None, op0=ALU.mult),
                                         waits=[(wsem[par], cnt)], inc=sV))
            wo_cast[ft] = ccount[par]

        xsem6 = (dC, dE)
        x1sem = (dF, dG)
        xcnt = [0, 0]
        x1w = [0, 0]
        vread = [0, 0]
        evs = [0, 0, 0, 0]
        stt_done = [0, 0, 0, 0]
        sst = 0
        for t in range(TT):
            par = t % 2
            xrc = xrc0 if par == 0 else xrc1
            x1buf = x1b0 if par == 0 else x1b1
            w = [(sV, vread[par])] if t >= 2 else [(dC, garc)]
            xcnt[par] = dma("sync", xrc[:, :], x_d[t], xsem6[par], waits=w)
            mmd = None
            for ft in range(DT):
                for mc in range(4):
                    mmd = pr.op("tensor", lambda e, t=t, mc=mc, ft=ft: e.matmul(
                        ps_a[mc][:, :], oT[:, ft * D + t * P:ft * D + (t + 1) * P],
                        wo_bf[:, ft * D + mc * 512: ft * D + (mc + 1) * 512],
                        start=(ft == 0), stop=(ft == DT - 1)),
                        waits=(([wo_cast[ft]] if t == 0 and mc == 0 else []) +
                               ([(sV, stt_done[mc])] if ft == 0 else [])),
                        inc=(sT if (ft == DT - 1 and mc == 3) else None))
            for mc in range(4):
                sttd = pr.op("vector", lambda e, t=t, mc=mc, x1buf=x1buf:
                             e.scalar_tensor_tensor(
                                 out=x1buf[:, mc * 512:(mc + 1) * 512],
                                 in0=ps_a[mc][:, :],
                                 scalar=rmsr[:, t:t + 1],
                                 in1=gaB[:, mc * 512:(mc + 1) * 512],
                                 op0=ALU.mult, op1=ALU.mult),
                             waits=[(sT, mmd), (sV, rms_done), (sV, gaB_done),
                                    (x1sem[par], x1w[par])], inc=sV)
                stt_done[mc] = sttd
                evs[mc] = pr.op("vector", lambda e, mc=mc, x1buf=x1buf, xrc=xrc:
                                e.tensor_tensor(
                                    out=x1buf[:, mc * 512:(mc + 1) * 512],
                                    in0=x1buf[:, mc * 512:(mc + 1) * 512],
                                    in1=xrc[:, mc * 512:(mc + 1) * 512],
                                    op=ALU.add),
                                waits=[(xsem6[par], xcnt[par]), (sV, sttd)], inc=sV)
            vread[par] = evs[3]
            vst = pr.op("vector", lambda e, t=t, x1buf=x1buf: e.tensor_reduce(
                out=msum[:, t:t + 1], in_=x1buf[:, :], axis=AX.X, op=ALU.add),
                waits=[(sV, evs[3])], inc=sV)
            sst = pr.op("scalar", lambda e, t=t, x1buf=x1buf: e.activation(
                out=junk6b[:, :], in_=x1buf[:, :], func=AF.Square,
                accum_out=sqsum[:, t:t + 1]),
                waits=[(sV, evs[3]), (sS, sst if t else 0)], inc=sS)
            x1w[par] = dma("gpsimd", x1_hbm[t], x1buf[:, :], x1sem[par],
                           waits=[(sV, vst), (sS, sst)])

        mean2_done = pr.op("vector", lambda e: e.tensor_scalar(
            out=mean2[:, :], in0=msum[:, :], scalar1=1.0 / D, scalar2=None,
            op0=ALU.mult), waits=[(sV, sV.n), (sS, sS.n)], inc=sV)
        c1b = pr.op("vector", lambda e: e.tensor_scalar(
            out=tmp1[:, 0:TT], in0=sqsum[:, :], scalar1=1.0 / D, scalar2=None,
            op0=ALU.mult), waits=[(sV, mean2_done)], inc=sV)
        c2b = pr.op("vector", lambda e: e.tensor_tensor(
            out=tmp1[:, 16:32], in0=mean2[:, :], in1=mean2[:, :], op=ALU.mult),
            waits=[(sV, c1b)], inc=sV)
        vst2 = pr.op("vector", lambda e: e.tensor_tensor(
            out=tmp1[:, 0:TT], in0=tmp1[:, 0:TT], in1=tmp1[:, 16:32],
            op=ALU.subtract), waits=[(sV, c2b)], inc=sV)
        ln2d = pr.op("scalar", lambda e: e.activation(
            out=tmp1[:, 0:TT], in_=tmp1[:, 0:TT], func=AF.Ln, bias=LNEPS, scale=1.0),
            waits=[(sV, vst2)], inc=sS)
        rstd2_done = pr.op("scalar", lambda e: e.activation(
            out=rstd2[:, :], in_=tmp1[:, 0:TT], func=AF.Exp, scale=-0.5),
            waits=[(sS, ln2d)], inc=sS)
        BARRIER()
        if debug:
            dbg["x1"] = nc.dram_tensor("dbg_x1", [TT, P, D], F32,
                                       kind="ExternalOutput").ap()
            for t in range(TT):
                dma("gpsimd", dbg["x1"][t], x1_hbm[t], dOut)
            BARRIER()
        pr.flush()
    _sc5.close()

    big1.close()

    big2 = ExitStack()
    h2T = big2.enter_context(nc.sbuf_tensor("h2T8", [P, DT * D], F8, side="left"))
    act2 = big2.enter_context(nc.sbuf_tensor("act2", [P, 2 * NS * QW], F8, side="left"))
    wl2 = big2.enter_context(nc.sbuf_tensor("wl2", [P, 2 * NS * QW], F8, side="left"))
    gbB = big2.enter_context(nc.sbuf_tensor("gbB", [P, D], BF16, side="left"))
    gbrow9 = big2.enter_context(nc.sbuf_tensor("gbrow9", [1, D], BF16, side="left"))

    _sc6 = ExitStack()
    x1r0 = _sc6.enter_context(nc.sbuf_tensor("x1r0", [P, D], F32, side="right"))
    x1r1 = _sc6.enter_context(nc.sbuf_tensor("x1r1", [P, D], F32, side="right"))
    lnxb20 = _sc6.enter_context(nc.sbuf_tensor("lnxb20", [P, D], BF16, side="right"))
    lnxb21 = _sc6.enter_context(nc.sbuf_tensor("lnxb21", [P, D], BF16, side="right"))
    ps_u0 = _sc6.enter_context(nc.psum_tensor("ps_u0", [P, 1024], BF16))
    ps_u1 = _sc6.enter_context(nc.psum_tensor("ps_u1", [P, 1024], BF16))
    ps_u2 = _sc6.enter_context(nc.psum_tensor("ps_u2", [P, 1024], BF16))
    ps_u3 = _sc6.enter_context(nc.psum_tensor("ps_u3", [P, 1024], BF16))
    if True:

        x1r = (x1r0, x1r1)
        xsem7 = (dA, dB)
        psux = (ps_u0, ps_u1, ps_u2, ps_u3)
        vdone = [0, 0]
        tile_tr_done = [0, 0]
        ev_done = [(sS, 0)] * 4
        for t in range(TT):
            par = t % 2
            lnxb2 = lnxb20 if par == 0 else lnxb21
            w = [(sV, vdone[par])] if t >= 2 else []
            cnt = dma("sync", x1r[par][:, :], x1_hbm[t], xsem7[par], waits=w)
            vdone[par] = pr.op("vector", lambda e, par=par, t=t, lnxb2=lnxb2:
                               e.tensor_scalar(
                                   out=lnxb2[:, :], in0=x1r[par][:, :],
                                   scalar1=mean2[:, t:t + 1],
                                   scalar2=rstd2[:, t:t + 1],
                                   op0=ALU.subtract, op1=ALU.mult),
                               waits=[(xsem7[par], cnt), (sS, rstd2_done),
                                      (sV, mean2_done),
                                      (sT, tile_tr_done[par])], inc=sV)
            for dt_ in range(DT):
                ppar = dt_ % 4
                ptr = psux[ppar]
                td = pr.op("tensor", lambda e, dt_=dt_, ptr=ptr, lnxb2=lnxb2: e.transpose(
                    ptr[:, 0:P], lnxb2[:, dt_ * P:(dt_ + 1) * P], eye_bf[:, :]),
                    waits=[(sV, vdone[par]), ev_done[ppar]], inc=sT)
                if dt_ % 2 == 0:
                    ev_done[ppar] = (sS, pr.op(
                        "scalar", lambda e, dt_=dt_, t=t, ptr=ptr: e.activation(
                            out=h2T[:, dt_ * D + t * P: dt_ * D + (t + 1) * P],
                            in_=ptr[:, 0:P], func=AF.Identity,
                            scale=cbcol[:, dt_:dt_ + 1],
                            bias=sbcol[:, dt_:dt_ + 1]),
                        waits=[(sT, td)], inc=sS))
                else:
                    ev_done[ppar] = (sV, pr.op(
                        "vector", lambda e, dt_=dt_, t=t, ptr=ptr: e.tensor_scalar(
                            out=h2T[:, dt_ * D + t * P: dt_ * D + (t + 1) * P],
                            in0=ptr[:, 0:P],
                            scalar1=cbcol[:, dt_:dt_ + 1],
                            scalar2=sbcol[:, dt_:dt_ + 1],
                            op0=ALU.mult, op1=ALU.add),
                        waits=[(sT, td)], inc=sV))
            tile_tr_done[par] = sT.n
        BARRIER()
        if debug:
            dbg["h2T"] = nc.dram_tensor("dbg_h2T", [P, DT * D], BF16,
                                        kind="ExternalOutput").ap()
            dma("gpsimd", dbg["h2T"], h2T[:, :], dOut)
            BARRIER()
        pr.flush()
    _sc6.close()

    _sc7 = ExitStack()
    wg0f8 = [_sc7.enter_context(nc.sbuf_tensor(f"wg0f8{i}", [P, DT * E], F8, side="right"))
             for i in range(2)]
    wg1f8 = [_sc7.enter_context(nc.sbuf_tensor(f"wg1f8{i}", [P, DT * E], F8, side="right"))
             for i in range(2)]
    gelbs = [_sc7.enter_context(nc.sbuf_tensor(f"gelb{i}", [P, 512], BF16, side="right"))
             for i in range(4)]
    actbs = [_sc7.enter_context(nc.sbuf_tensor(f"actb{i}", [P, 2 * QW], F8, side="right"))
             for i in range(4)]
    ps_gs = [_sc7.enter_context(nc.psum_tensor(f"ps_g{i}", [P, 512], F32))
             for i in range(4)]
    ps_fs = [_sc7.enter_context(nc.psum_tensor(f"ps_f{i}", [P, 512], F32))
             for i in range(4)]
    if True:
        wcsem = (dJ, dK)
        actsem = (dF, dG, dH, dI)
        gmm = 0
        fmm = 0
        gel_done = [0, 0, 0, 0]
        act_done = [0, 0, 0, 0]
        actw = [0, 0, 0, 0]
        g0cnt = [0, 0]
        g1cnt = [0, 0]
        NDP = DT // 2
        wlcast_q = [[None] * 8 for _ in range(NQ)]
        preload_acnts = [[None] * 4 for _ in range(2)]
        ACT_CSEM = (dC, dE, dN, dO)
        ACT_CQ = ("sync", "sync", "scalar", "scalar")

        def act_chunk_dma(bp, ats, ck, extra_waits):
            sem = ACT_CSEM[ck]
            cnt = dma(ACT_CQ[ck],
                      act2[:, bp * NS * QW + ck * 8 * 2 * QW:
                           bp * NS * QW + (ck + 1) * 8 * 2 * QW].rearrange(
                          "p (s c) -> p s c", s=8),
                      act_hbm[ats][ck * 8:(ck + 1) * 8].rearrange("s p c -> p s c"),
                      sem, waits=[(sem, sem.n)] + list(extra_waits))
            return (sem, cnt)

        def wl_group_dma(q, g, extra_waits):
            sem = dP if g % 2 == 0 else dQ
            cnt = dma("gpsimd",
                      wl2[:, (q % 2) * NS * QW + g * 8 * QW:
                           (q % 2) * NS * QW + (g + 1) * 8 * QW].rearrange(
                          "p (s c) -> p s c", s=8),
                      wl_d[q][g * 8:(g + 1) * 8].rearrange("s p c -> p s c"),
                      sem, waits=[(sem, sem.n)] + list(extra_waits))
            wlcast_q[q][g] = (sem, cnt)

        def wl_half_dma(q, g, half, extra_waits):
            sem = dP if g % 2 == 0 else dQ
            s0 = g * 8 + half * 4
            cnt = dma("gpsimd",
                      wl2[:, (q % 2) * NS * QW + s0 * QW:
                           (q % 2) * NS * QW + (s0 + 4) * QW].rearrange(
                          "p (s c) -> p s c", s=4),
                      wl_d[q][s0:s0 + 4].rearrange("s p c -> p s c"),
                      sem, waits=[(sem, sem.n)] + list(extra_waits))
            wlcast_q[q][g] = (sem, cnt)

        def h2pair(dtp, tc):
            return h2T[:, :].rearrange("p (dt c) -> p dt c", dt=DT)[
                :, 2 * dtp:2 * dtp + 2, tc * 512:(tc + 1) * 512]

        def wgpair(buf, dtp):
            return buf[:, :].rearrange("p (dt e) -> p dt e", dt=DT)[
                :, 2 * dtp:2 * dtp + 2, :]

        gbrc = dma("sync", gbrow9[0:1, :], g_hbm[1], dC, waits=[(dC, dC.n)])
        gbB_done = 0
        for i in range(4):
            td = pr.op("tensor", lambda e, i=i: e.matmul(
                ps_gs[i][:, :], ones_b[0:1, :],
                gbrow9[0:1, i * 512:(i + 1) * 512],
                start=True, stop=True), waits=[(dC, gbrc)], inc=sT)
            gbB_done = pr.op("vector", lambda e, i=i: e.tensor_copy(
                out=gbB[:, i * 512:(i + 1) * 512], in_=ps_gs[i][:, :]),
                waits=[(sT, td)], inc=sV)

        g0cnt[0] = dma("gpsimd", wg0f8[0][:, :].rearrange("p (dt e) -> p dt e", dt=DT),
                       wg_d[0][0].rearrange("dt p e -> p dt e"), dJ)
        g1cnt[0] = dma("gpsimd", wg1f8[0][:, :].rearrange("p (dt e) -> p dt e", dt=DT),
                       wg_d[0][1].rearrange("dt p e -> p dt e"), dK,
                       waits=[(dJ, g0cnt[0])])
        for s in range(NS):
            par = s % 2
            if s % 2 == 1:
                idx = (s - 1) // 2
                wl_half_dma(idx // 16, (idx % 16) // 2, idx % 2, [])
            if s + 1 < NS:
                g0cnt[1 - par] = dma("gpsimd",
                                     wg0f8[1 - par][:, :].rearrange(
                                         "p (dt e) -> p dt e", dt=DT),
                                     wg_d[s + 1][0].rearrange("dt p e -> p dt e"),
                                     dJ, waits=[(dJ, dJ.n), (sT, gmm)])
            for dtp in range(NDP):
                for tc in range(4):
                    gmm = pr.op("tensor", lambda e, tc=tc, dtp=dtp, par=par:
                                e.matmul(ps_gs[tc][:, :], wgpair(wg0f8[par], dtp),
                                         h2pair(dtp, tc),
                                         start=(dtp == 0), stop=(dtp == NDP - 1),
                                         perf_mode=mybir.MatmulPerfMode.DoubleRow),
                                waits=(([(wcsem[0], g0cnt[par])] +
                                        [(sS, g) for g in gel_done] +
                                        ([(sV, gbB_done)] if s == 0 else []))
                                       if (dtp == 0 and tc == 0) else []),
                                inc=(sT if (dtp == NDP - 1 and tc == 3) else None))
            for tc in range(4):
                gel_done[tc] = pr.op("scalar", lambda e, tc=tc: e.activation(
                    out=gelbs[tc][:, :], in_=ps_gs[tc][:, :],
                    func=AF.Gelu_apprx_tanh),
                    waits=[(sT, gmm), (sV, act_done[tc])], inc=sS)
            if s + 1 < NS:
                g1cnt[1 - par] = dma("gpsimd",
                                     wg1f8[1 - par][:, :].rearrange(
                                         "p (dt e) -> p dt e", dt=DT),
                                     wg_d[s + 1][1].rearrange("dt p e -> p dt e"),
                                     dK, waits=[(dK, dK.n), (sT, fmm)])
            for dtp in range(NDP):
                for tc in range(4):
                    fmm = pr.op("tensor", lambda e, tc=tc, dtp=dtp, par=par:
                                e.matmul(ps_fs[tc][:, :], wgpair(wg1f8[par], dtp),
                                         h2pair(dtp, tc),
                                         start=(dtp == 0), stop=(dtp == NDP - 1),
                                         perf_mode=mybir.MatmulPerfMode.DoubleRow),
                                waits=(([(wcsem[1], g1cnt[par])] +
                                        [(sV, a) for a in act_done])
                                       if (dtp == 0 and tc == 0) else []),
                                inc=(sT if (dtp == NDP - 1 and tc == 3) else None))
            half = s % 2
            for tc in range(4):
                act_done[tc] = pr.op("vector", lambda e, tc=tc, half=half:
                                     e.tensor_tensor(
                                         out=actbs[tc][:, :].rearrange(
                                             "p (tt two c) -> p tt two c",
                                             tt=4, two=2)[:, :, half, :],
                                         in0=gelbs[tc][:, :].rearrange(
                                             "p (tt c) -> p tt c", tt=4),
                                         in1=ps_fs[tc][:, :].rearrange(
                                             "p (tt c) -> p tt c", tt=4),
                                         op=ALU.mult),
                                     waits=[(sS, gel_done[tc]), (sT, fmm)] +
                                           ([(actsem[tc], actw[tc])] if half == 0
                                            else []), inc=sV)
                if half == 1:
                    actw[tc] = dma("gpsimd", act_hbm[tc][s // 2], actbs[tc][:, :],
                                   actsem[tc], waits=[(sV, act_done[tc])])
            if half == 1 and (s // 2 + 1) % 8 == 0:
                ck = (s // 2 + 1) // 8 - 1
                for ts_ in (0, 1):
                    preload_acnts[ts_][ck] = act_chunk_dma(
                        ts_, ts_, ck, [(actsem[ts_], actw[ts_])])
        BARRIER()
        pr.flush()
    _sc7.close()

    _sc8 = ExitStack()
    x1q0 = _sc8.enter_context(nc.sbuf_tensor("x1q0", [P, QW], F32, side="right"))
    x1q1 = _sc8.enter_context(nc.sbuf_tensor("x1q1", [P, QW], F32, side="right"))
    outq0 = _sc8.enter_context(nc.sbuf_tensor("outq0", [P, QW], F32, side="right"))
    outq1 = _sc8.enter_context(nc.sbuf_tensor("outq1", [P, QW], F32, side="right"))
    ps_y0 = _sc8.enter_context(nc.psum_tensor("ps_y0", [P, 512], F32))
    ps_y1 = _sc8.enter_context(nc.psum_tensor("ps_y1", [P, 512], F32))
    ps_y2 = _sc8.enter_context(nc.psum_tensor("ps_y2", [P, 512], F32))
    ps_y3 = _sc8.enter_context(nc.psum_tensor("ps_y3", [P, 512], F32))
    ps_y4 = _sc8.enter_context(nc.psum_tensor("ps_y4", [P, 512], F32))
    ps_y5 = _sc8.enter_context(nc.psum_tensor("ps_y5", [P, 512], F32))
    ps_y6 = _sc8.enter_context(nc.psum_tensor("ps_y6", [P, 512], F32))
    ps_y7 = _sc8.enter_context(nc.psum_tensor("ps_y7", [P, 512], F32))
    if True:
        ps_y_sets = ((ps_y0, ps_y1, ps_y2, ps_y3), (ps_y4, ps_y5, ps_y6, ps_y7))
        x1qsem = (dL, dM)
        outsem = (dOut, dOut2)
        evq = [0, 0]
        vq_done = [[0, 0, 0, 0], [0, 0, 0, 0]]
        x1cnt = [0, 0]
        outw = [0, 0]
        gts = 0
        SPC = NS // 4
        NSP = NS // 2
        NGTS = NQ * TSUP
        mm_end_of = [0] * NGTS
        q_end_mm = [0] * NQ
        buf_acnts = [preload_acnts[0], preload_acnts[1]]

        def acpair(bp, sp, tt_):
            base = bp * NS * QW + sp * 2 * QW + tt_ * 256
            return act2[:, base:base + 256].rearrange("p (two c) -> p two c", two=2)

        def wlpair(qp, sp):
            return wl2[:, qp * NS * QW:(qp + 1) * NS * QW].rearrange(
                "p (s c) -> p s c", s=NS)[:, 2 * sp:2 * sp + 2, :]

        for q in range(NQ):
            if 1 <= q < NQ - 1:
                for g in range(8):
                    wl_group_dma(q + 1, g, [(sT, q_end_mm[q - 1])])
            for ts in range(TSUP):
                bp = gts % 2
                ps_y = ps_y_sets[bp]
                if gts >= 1 and gts + 1 < NGTS:
                    nb = (gts + 1) % 2
                    for ck in range(4):
                        buf_acnts[nb][ck] = act_chunk_dma(
                            nb, (gts + 1) % TSUP, ck, [(sT, mm_end_of[gts - 1])])
                for tt_ in range(2):
                    t = ts * 4 + tt_
                    tp = tt_ % 2
                    w = [(sV, evq[tp])]
                    if q == 0 and ts == 0:
                        w += [(dF, dF.n), (dG, dG.n)]
                    x1cnt[tp] = dma("sync",
                                    (x1q0 if tp == 0 else x1q1)[:, :],
                                    x1_hbm[t][:, q * QW:(q + 1) * QW],
                                    x1qsem[tp], waits=w)
                mmd = 0
                for sp in range(NSP):
                    ck = (2 * sp) // SPC
                    for tt_ in range(4):
                        w = []
                        if (2 * sp) % SPC == 0 and tt_ == 0:
                            w += [buf_acnts[bp][ck]]
                        if (2 * sp) % 8 == 0 and tt_ == 0 and ts == 0:
                            w += [wlcast_q[q][(2 * sp) // 8]]
                        if sp == 0:
                            w += [(sV, vq_done[bp][tt_])]
                        do_inc = (tt_ == 3 and (sp % 4 == 3))
                        mmd = pr.op("tensor",
                                    lambda e, bp=bp, qp=q % 2, sp=sp, tt_=tt_,
                                    py=ps_y[tt_]:
                                    e.matmul(py[:, :], acpair(bp, sp, tt_),
                                             wlpair(qp, sp),
                                             start=(sp == 0), stop=(sp == NSP - 1),
                                             perf_mode=mybir.MatmulPerfMode.DoubleRow),
                                    waits=w,
                                    inc=(sT if do_inc else None))
                mm_end_of[gts] = mmd
                for tt_ in range(4):
                    t = ts * 4 + tt_
                    tp = tt_ % 2
                    x1q = x1q0 if tp == 0 else x1q1
                    outq = outq0 if tp == 0 else outq1
                    vq = pr.op("vector", lambda e, q=q, outq=outq, py=ps_y[tt_]:
                               e.tensor_tensor(
                                   out=outq[:, :], in0=py[:, :],
                                   in1=gbB[:, q * QW:(q + 1) * QW], op=ALU.mult),
                               waits=[(sT, mmd), (sV, gbB_done),
                                      (x1qsem[tp], x1cnt[tp]),
                                      (outsem[tp], outw[tp])], inc=sV)
                    vq_done[bp][tt_] = vq
                    evq[tp] = pr.op("vector", lambda e, outq=outq, x1q=x1q:
                                    e.tensor_tensor(out=outq[:, :], in0=outq[:, :],
                                                    in1=x1q[:, :], op=ALU.add),
                                    waits=[(sV, vq)], inc=sV)
                    outw[tp] = dma("scalar", y_d[t][:, q * QW:(q + 1) * QW],
                                   outq[:, :], outsem[tp], waits=[(sV, evq[tp])])
                    if tt_ == 1:
                        for tt2 in (2, 3):
                            t2 = ts * 4 + tt2
                            tp2 = tt2 % 2
                            x1cnt[tp2] = dma("sync",
                                             (x1q0 if tp2 == 0 else x1q1)[:, :],
                                             x1_hbm[t2][:, q * QW:(q + 1) * QW],
                                             x1qsem[tp2], waits=[(sV, evq[tp2])])
                gts += 1
            q_end_mm[q] = mmd
        pr.op("gpsimd", lambda e: e.engine_nop(),
              waits=[(dOut, dOut.n), (dOut2, dOut2.n)])
        pr.flush()
    _sc8.close()

    big2.close()
    es.close()
    return nc, dbg


_BUILT = None


def _host_tiles(inputs):
    f32 = np.float32
    Wq = np.asarray(inputs["L0_attn_Wq"], f32)
    Wk = np.asarray(inputs["L0_attn_Wk"], f32)
    Wv = np.asarray(inputs["L0_attn_Wv"], f32)
    wqkv = np.ascontiguousarray(np.stack([
        Wq.reshape(DT, P, H, E).transpose(2, 0, 1, 3),
        Wk.reshape(DT, P, H, E).transpose(2, 0, 1, 3),
        Wv.reshape(DT, P, H, E).transpose(2, 0, 1, 3),
    ], axis=1))
    wmod = np.ascontiguousarray(
        np.asarray(inputs["L0_wmod"], f32).reshape(DT, P, NMS, MSW).transpose(2, 0, 1, 3))
    ccol = np.ascontiguousarray(
        np.asarray(inputs["L0_c"], f32).reshape(8, DT, P).transpose(0, 2, 1))
    wf = np.ascontiguousarray(np.asarray(inputs["L0_attn_Wf"], f32).reshape(DT, P, H))
    wo = np.ascontiguousarray(np.asarray(inputs["L0_attn_Wo"], f32).reshape(DT, P, D))
    nwcol_ = np.ascontiguousarray(
        np.asarray(inputs["L0_attn_norm_w"], f32).reshape(DT, P).T)
    wg_in = np.asarray(inputs["L0_ffn_w_gating"], f32)
    wg = np.ascontiguousarray(
        np.stack([wg_in[0].reshape(DT, P, NS, E).transpose(2, 0, 1, 3),
                  wg_in[1].reshape(DT, P, NS, E).transpose(2, 0, 1, 3)],
                 axis=1))
    wl = np.ascontiguousarray(
        np.asarray(inputs["L0_ffn_w_linear"], f32).reshape(NS, P, NQ, QW).transpose(2, 0, 1, 3))
    return wqkv, wmod, ccol, wf, wo, nwcol_, wg, wl


def kernel(**inputs):
    global _BUILT
    if _BUILT is None:
        _BUILT = build(debug=False)
    nc, _ = _BUILT
    wqkv, wmod, ccol, wf, wo, nwcol_, wg, wl = _host_tiles(inputs)
    x = np.ascontiguousarray(np.asarray(inputs["L0_attn_x"], np.float32))
    in_maps = []
    for b in range(8):
        in_maps.append({
            "x": x[b].reshape(TT, P, D),
            "wmod": wmod, "ccol": np.ascontiguousarray(ccol[b]),
            "wqkv": wqkv, "wf": wf, "wo": wo, "nwcol": nwcol_,
            "wg": wg, "wl": wl,
        })
    res = run_bass_kernel_spmd(nc, in_maps, core_ids=list(range(8)))
    out = np.stack([r["y"].reshape(D, D) for r in res.results])
    return out


# version 21
# speedup vs baseline: 1.2777x; 1.0026x over previous
import numpy as np
from contextlib import ExitStack

import concourse.bass as bass
import concourse.mybir as mybir
from concourse.bass_utils import run_bass_kernel_spmd

F32 = mybir.dt.float32
F32R = mybir.dt.float32r
BF16 = mybir.dt.bfloat16
F8 = mybir.dt.float8e4
AF = mybir.ActivationFunctionType
ALU = mybir.AluOpType
AX = mybir.AxisListType

P = 128
D = 2048
TT = 16
DT = 16
H = 16
E = 128
NCH = 16
MS = 12288
NMS = 48
MSW = 256
HID = 8192
NS = 64
NQ = 4
QW = 512
TSUP = 4
LNEPS = 1e-5
RMSEPS = 1e-6


class Sem:
    def __init__(self, nc, es, name):
        self.h = es.enter_context(nc.semaphore(name))
        self.n = 0
        self.name = name


class Prog:
    ENGS = ("tensor", "vector", "scalar", "sync", "gpsimd")

    def __init__(self, nc):
        self.nc = nc
        self.q = {e: [] for e in self.ENGS}
        self.last_wait = {}

    def _wait(self, eng, sem, val):
        if val <= 0:
            return
        key = (eng, sem.name)
        if self.last_wait.get(key, 0) >= val:
            return
        self.last_wait[key] = val
        self.q[eng].append(lambda e, h=sem.h, v=val: e.wait_ge(h, v))

    def op(self, eng, fn, waits=(), inc=None, inc_k=1):
        for sem, val in waits:
            self._wait(eng, sem, val)
        if inc is not None:
            self.q[eng].append(lambda e, fn=fn, h=inc.h, k=inc_k: fn(e).then_inc(h, k))
            inc.n += inc_k
            return inc.n
        self.q[eng].append(fn)
        return None

    def barrier(self, sems):
        for eng in self.ENGS:
            for s in sems:
                self._wait(eng, s, s.n)

    def flush(self):
        with self.nc.Block() as block:
            for eng in self.ENGS:
                ops = self.q[eng]
                if ops:
                    getattr(block, eng)(lambda e, ops=ops: [f(e) for f in ops])
        self.q = {e: [] for e in self.ENGS}


def build(debug=False):
    nc = bass.Bass("TRN2", target_bir_lowering=False)
    import ml_dtypes

    x_d = nc.dram_tensor("x", [TT, P, D], F32, kind="ExternalInput").ap()
    wmod_d = nc.dram_tensor("wmod", [NMS, DT, P, MSW], F32, kind="ExternalInput").ap()
    ccol_d = nc.dram_tensor("ccol", [P, DT], F32, kind="ExternalInput").ap()
    wqkv_d = nc.dram_tensor("wqkv", [H, 3, DT, P, E], F32, kind="ExternalInput").ap()
    wf_d = nc.dram_tensor("wf", [DT, P, H], F32, kind="ExternalInput").ap()
    wo_d = nc.dram_tensor("wo", [DT, P, D], F32, kind="ExternalInput").ap()
    nw_d = nc.dram_tensor("nwcol", [P, DT], F32, kind="ExternalInput").ap()
    wg_d = nc.dram_tensor("wg", [NS, 2, DT, P, E], F32, kind="ExternalInput").ap()
    wl_d = nc.dram_tensor("wl", [NQ, NS, P, QW], F32, kind="ExternalInput").ap()
    y_d = nc.dram_tensor("y", [TT, P, D], F32, kind="ExternalOutput").ap()

    x1_hbm = nc.dram_tensor("x1_scratch", [TT, P, D], F32, kind="Internal").ap()
    g_hbm = nc.dram_tensor("g_scratch", [2, 1, D], BF16, kind="Internal").ap()
    act_hbm = nc.dram_tensor("act_scratch", [TSUP, NS // 2, P, 2 * QW], F8, kind="Internal").ap()

    eye_bf_d = nc.inline_tensor(np.eye(P, dtype=ml_dtypes.bfloat16), "eye_bf_c")
    eye16_d = nc.inline_tensor(np.eye(16, dtype=np.float32), "eye16_c")
    tri_np = np.triu(np.ones((P, P), dtype=np.float32))
    tri_d = nc.inline_tensor(tri_np, "tri_c")
    mask_d = nc.inline_tensor(tri_np.astype(ml_dtypes.bfloat16), "mask_c")
    ones_f_d = nc.inline_tensor(np.ones((1, P), dtype=np.float32), "ones_f_c")
    ones_b_d = nc.inline_tensor(np.ones((1, P), dtype=ml_dtypes.bfloat16), "ones_b_c")
    onescol_d = nc.inline_tensor(np.ones((P, 1), dtype=np.float32), "ones_col_c")
    onescol_b_d = nc.inline_tensor(np.ones((P, 1), dtype=ml_dtypes.bfloat16), "ones_colb_c")
    constvals = [LNEPS, RMSEPS]
    constv_d = nc.inline_tensor(
        np.broadcast_to(np.asarray(constvals, np.float32), (P, len(constvals))).copy(),
        "constv_c")

    dbg = {}
    es = ExitStack()

    def sbl(name, shape, dt):
        return es.enter_context(nc.sbuf_tensor(name, shape, dt, side="left"))

    eye_bf = sbl("eye_bf", [P, P], BF16)
    eye16 = sbl("eye16", [16, 16], F32)
    tri = sbl("tri", [P, P], F32)
    maskb = sbl("maskb", [P, P], BF16)
    ones_f = sbl("ones_f", [1, P], F32)
    ones_b = sbl("ones_b", [1, P], BF16)
    sacol = sbl("sacol", [P, DT], F32)
    cacol = sbl("cacol", [P, DT], F32)
    sbcol = sbl("sbcol", [P, DT], F32)
    cbcol = sbl("cbcol", [P, DT], F32)
    mean1 = sbl("mean1", [P, TT], F32)
    rstd1 = sbl("rstd1", [P, TT], F32)
    mean2 = sbl("mean2", [P, TT], F32)
    rstd2 = sbl("rstd2", [P, TT], F32)
    msum = sbl("msum", [P, TT], F32)
    sqsum = sbl("sqsum", [P, TT], F32)
    rmsr = sbl("rmsr", [P, TT], F32)

    e1t = sbl("e1t", [P, NCH * H], F32)
    e2t = sbl("e2t", [P, NCH * H], F32)
    e3t = sbl("e3t", [P, (NCH - 1) * H], F32)
    e4bc = sbl("e4bc", [P, (NCH - 1) * H], F32)

    Sbf = sbl("Sbf", [P, E], BF16)
    tmp1 = sbl("tmp1", [P, 2 * TT], F32)
    nwcol = sbl("nwcol_sb", [P, DT], F32)

    constv = sbl("constv", [P, 2], F32)
    onescol = sbl("onescol", [P, 1], F32)
    onescol_b = sbl("onescol_b", [P, 1], BF16)
    totrow = sbl("totrow", [1, NCH * H], F32)
    nc.const_aps.register(F32, LNEPS, constv[:, 0:1])
    nc.const_aps.register(F32, RMSEPS, constv[:, 1:2])

    sT = Sem(nc, es, "sT")
    sV = Sem(nc, es, "sV")
    sS = Sem(nc, es, "sS")
    dA = Sem(nc, es, "dA")
    dB = Sem(nc, es, "dB")
    dC = Sem(nc, es, "dC")
    dE = Sem(nc, es, "dE")
    dF = Sem(nc, es, "dF")
    dG = Sem(nc, es, "dG")
    dOut = Sem(nc, es, "dOut")
    dOut2 = Sem(nc, es, "dOut2")
    dH = Sem(nc, es, "dH")
    dI = Sem(nc, es, "dI")
    dJ = Sem(nc, es, "dJ")
    dK = Sem(nc, es, "dK")
    dL = Sem(nc, es, "dL")
    dM = Sem(nc, es, "dM")
    dN = Sem(nc, es, "dN")
    dO = Sem(nc, es, "dO")
    dP = Sem(nc, es, "dP")
    dQ = Sem(nc, es, "dQ")
    ALLSEMS = [sT, sV, sS, dA, dB, dC, dE, dF, dG, dOut, dOut2, dH, dI, dJ, dK,
               dL, dM, dN, dO, dP, dQ]

    pr = Prog(nc)

    def dma(qeng, dst, src, sem, waits=()):
        return pr.op(qeng, lambda e, d=dst, s=src: e.dma_start(out=d, in_=s),
                     waits=waits, inc=sem, inc_k=16)

    def BARRIER():
        pr.barrier(ALLSEMS)

    big1 = ExitStack()
    h1T = big1.enter_context(nc.sbuf_tensor("h1T", [P, DT * D], BF16, side="left"))
    oT = big1.enter_context(nc.sbuf_tensor("oT", [P, H * D], BF16, side="left"))

    _sc1 = ExitStack()
    wmbufs = [_sc1.enter_context(nc.sbuf_tensor(f"wmbuf{i}", [P, DT * MSW], BF16,
                                                side="right")) for i in range(4)]
    ccol = _sc1.enter_context(nc.sbuf_tensor("ccol_sb", [P, DT], F32, side="right"))
    silc = _sc1.enter_context(nc.sbuf_tensor("silc", [P, DT], F32, side="right"))
    silcb = _sc1.enter_context(nc.sbuf_tensor("silcb", [P, DT], BF16, side="right"))
    rowtA = _sc1.enter_context(nc.sbuf_tensor("rowtA", [1, MSW], F32, side="right"))
    rowtB = _sc1.enter_context(nc.sbuf_tensor("rowtB", [1, MSW], F32, side="right"))
    rowsgn = _sc1.enter_context(nc.sbuf_tensor("rowsgn", [1, MSW], F32, side="right"))
    rowabs = _sc1.enter_context(nc.sbuf_tensor("rowabs", [1, MSW], F32, side="right"))
    xrA = _sc1.enter_context(nc.sbuf_tensor("xrA", [P, D], F32, side="right"))
    xrB = _sc1.enter_context(nc.sbuf_tensor("xrB", [P, D], F32, side="right"))
    garow = _sc1.enter_context(nc.sbuf_tensor("garow", [1, D], BF16, side="right"))
    gbrow = _sc1.enter_context(nc.sbuf_tensor("gbrow", [1, D], BF16, side="right"))
    lnxb0 = _sc1.enter_context(nc.sbuf_tensor("lnxb0", [P, D], BF16, side="right"))
    lnxb1 = _sc1.enter_context(nc.sbuf_tensor("lnxb1", [P, D], BF16, side="right"))
    ps_m0 = _sc1.enter_context(nc.psum_tensor("ps_m0", [P, 512], F32))
    ps_m1 = _sc1.enter_context(nc.psum_tensor("ps_m1", [P, 512], F32))
    ps_col = _sc1.enter_context(nc.psum_tensor("ps_col", [P, 512], F32))
    ps_t0 = _sc1.enter_context(nc.psum_tensor("ps_t0", [P, 1024], BF16))
    ps_t1 = _sc1.enter_context(nc.psum_tensor("ps_t1", [P, 1024], BF16))
    ps_t2 = _sc1.enter_context(nc.psum_tensor("ps_t2", [P, 1024], BF16))
    ps_t3 = _sc1.enter_context(nc.psum_tensor("ps_t3", [P, 1024], BF16))
    if True:
        for dst, srcd in ((eye_bf, eye_bf_d), (eye16, eye16_d), (tri, tri_d),
                          (maskb, mask_d), (ones_f, ones_f_d), (ones_b, ones_b_d)):
            dma("sync", dst[:, :], srcd.ap(), dC)
        dma("sync", constv[:, :], constv_d.ap(), dC)
        dma("sync", onescol[:, :], onescol_d.ap(), dC)
        dma("sync", onescol_b[:, :], onescol_b_d.ap(), dC)
        dma("sync", ccol[:, :], ccol_d, dC)
        dma("sync", nwcol[:, :], nw_d, dC)
        cdone = dC.n

        sil_done = pr.op("scalar", lambda e: e.activation(
            out=silc[:, :], in_=ccol[:, :], func=AF.Silu),
            waits=[(dC, cdone)], inc=sS)
        silb_done = pr.op("vector", lambda e: e.tensor_copy(
            out=silcb[:, :], in_=silc[:, :]), waits=[(sS, sil_done)], inc=sV)

        SEGCOL = {0: (sacol, 0.0), 1: (cacol, 1.0), 3: (sbcol, 0.0), 4: (cbcol, 1.0)}
        mod_state = {
            "mv_done": [0, 0, 0, 0],
            "dma": {},
            "ev": [0, 0],
            "colv": 0,
            "clampv": 0,
            "clamps": 0,
        }

        def emit_mod_dma(ms):
            st = mod_state
            slot = ms % 4
            buf = wmbufs[slot]
            dsem = (dF, dG, dJ, dK)[slot]
            st["dma"][ms] = (dsem, dma(
                "gpsimd", buf[:, :].rearrange("p (dt c) -> p dt c", dt=DT),
                wmod_d[ms].rearrange("dt p c -> p dt c"), dsem,
                waits=[(sT, st["mv_done"][slot]), (dsem, dsem.n)]))

        def emit_mod_slice(ms):
            st = mod_state
            if ms not in st["dma"]:
                emit_mod_dma(ms)
            slot = ms % 4
            par = ms % 2
            buf = wmbufs[slot]
            dsem, cnt = st["dma"][ms]
            rowt = rowtA if par == 0 else rowtB
            psm = ps_m0 if par == 0 else ps_m1
            md = None
            for dt_ in range(DT):
                md = pr.op("tensor",
                           lambda e, dt_=dt_, buf=buf, psm=psm: e.matmul(
                               psm[0:1, 0:MSW], silcb[:, dt_:dt_ + 1],
                               buf[:, dt_ * MSW:(dt_ + 1) * MSW],
                               start=(dt_ == 0), stop=(dt_ == DT - 1)),
                           waits=([(dsem, cnt), (sV, silb_done),
                                   (sV, st["ev"][par])] if dt_ == 0 else []),
                           inc=(sT if dt_ == DT - 1 else None))
            st["mv_done"][slot] = md
            ev = pr.op("vector", lambda e, psm=psm, rowt=rowt: e.tensor_copy(
                out=rowt[0:1, :], in_=psm[0:1, 0:MSW]),
                waits=[(sT, md), (sV, st["clampv"]), (sV, st["colv"]),
                       (sS, st["clamps"])], inc=sV)
            st["ev"][par] = ev
            seg = ms // 8
            if seg in SEGCOL:
                dstcol, addone = SEGCOL[seg]
                cm = None
                for k in range(2):
                    dtile = (ms % 8) * 2 + k
                    cm = pr.op("tensor", lambda e, k=k, dtile=dtile, rowt=rowt:
                               e.matmul(ps_col[:, dtile:dtile + 1],
                                        rowt[0:1, k * P:(k + 1) * P],
                                        ones_f[0:1, 0:1], start=True, stop=True),
                               waits=[(sV, ev), (sV, st["colv"])],
                               inc=(sT if k == 1 else None))
                d0 = (ms % 8) * 2
                st["colv"] = pr.op("vector",
                                   lambda e, d0=d0, dstcol=dstcol, addone=addone:
                                   e.tensor_scalar(
                                       out=dstcol[:, d0:d0 + 2],
                                       in0=ps_col[:, d0:d0 + 2],
                                       scalar1=float(addone), scalar2=None,
                                       op0=ALU.add),
                                   waits=[(sT, cm)], inc=sV)
            else:
                target = garow if seg == 2 else gbrow
                off = (ms % 8) * MSW
                s1_ = pr.op("scalar", lambda e, rowt=rowt: e.activation(
                    out=rowsgn[0:1, :], in_=rowt[0:1, :], func=AF.Sign),
                    waits=[(sV, ev), (sV, st["clampv"])], inc=sS)
                s2_ = pr.op("scalar", lambda e, rowt=rowt: e.activation(
                    out=rowabs[0:1, :], in_=rowt[0:1, :], func=AF.Abs),
                    waits=[(sS, s1_)], inc=sS)
                st["clamps"] = s2_
                v1_ = pr.op("vector", lambda e: e.tensor_scalar_max(
                    out=rowabs[0:1, :], in0=rowabs[0:1, :], scalar1=0.25),
                    waits=[(sS, s2_)], inc=sV)
                st["clampv"] = pr.op("vector",
                                     lambda e, target=target, off=off:
                                     e.tensor_tensor(
                                         out=target[0:1, off:off + MSW],
                                         in0=rowsgn[0:1, :], in1=rowabs[0:1, :],
                                         op=ALU.mult),
                                     waits=[(sV, v1_)], inc=sV)

        xrP = (xrA, xrB)
        xsemP = (dC, dE)
        vdoneP = [0, 0]
        sdoneP = [0, 0]
        for t in range(TT):
            par = t % 2
            w = ([(sV, vdoneP[par]), (sS, sdoneP[par])] if t >= 2
                 else [(dC, cdone)])
            cnt = dma("sync", xrP[par][:, :], x_d[t], xsemP[par], waits=w)
            vdoneP[par] = pr.op("vector", lambda e, par=par, t=t: e.tensor_reduce(
                out=msum[:, t:t + 1], in_=xrP[par][:, :], axis=AX.X, op=ALU.add),
                waits=[(xsemP[par], cnt)], inc=sV)
            sdoneP[par] = pr.op("scalar", lambda e, par=par, t=t: e.activation(
                out=lnxb0[:, :], in_=xrP[par][:, :], func=AF.Square,
                accum_out=sqsum[:, t:t + 1]),
                waits=[(xsemP[par], cnt), (sS, max(sdoneP))], inc=sS)

        for ms in range(24):
            emit_mod_slice(ms)
        saca_done = mod_state["colv"]

        mean_done = pr.op("vector", lambda e: e.tensor_scalar(
            out=mean1[:, :], in0=msum[:, :], scalar1=1.0 / D, scalar2=None,
            op0=ALU.mult), waits=[(sV, max(vdoneP)), (sS, max(sdoneP))], inc=sV)
        c1 = pr.op("vector", lambda e: e.tensor_scalar(
            out=tmp1[:, 0:TT], in0=sqsum[:, :], scalar1=1.0 / D, scalar2=None,
            op0=ALU.mult), waits=[(sV, mean_done)], inc=sV)
        c2 = pr.op("vector", lambda e: e.tensor_tensor(
            out=tmp1[:, 16:32], in0=mean1[:, :], in1=mean1[:, :], op=ALU.mult),
            waits=[(sV, c1)], inc=sV)
        vstat = pr.op("vector", lambda e: e.tensor_tensor(
            out=tmp1[:, 0:TT], in0=tmp1[:, 0:TT], in1=tmp1[:, 16:32],
            op=ALU.subtract), waits=[(sV, c2)], inc=sV)
        ln1d = pr.op("scalar", lambda e: e.activation(
            out=tmp1[:, 0:TT], in_=tmp1[:, 0:TT], func=AF.Ln, bias=LNEPS,
            scale=1.0), waits=[(sV, vstat)], inc=sS)
        rstd_done = pr.op("scalar", lambda e: e.activation(
            out=rstd1[:, :], in_=tmp1[:, 0:TT], func=AF.Exp, scale=-0.5),
            waits=[(sS, ln1d)], inc=sS)

        xr = (xrA, xrB)
        xsem = (dC, dE)
        vdone = vdoneP
        ms_next = 24
        ms_dma = 24
        tile_tr_done = [0, 0]
        ev_done = [(sS, 0), (sS, 0)]
        pstx = (ps_t0, ps_t1, ps_t2, ps_t3)
        ev_done = [(sS, 0)] * 4
        for t in range(TT):
            par = t % 2
            lnxb = lnxb0 if par == 0 else lnxb1
            w = [(sV, vdone[par]), (sS, max(sdoneP)), (sS, rstd_done)]
            cnt = dma("sync", xr[par][:, :], x_d[t], xsem[par], waits=w)
            vh = [0, 0]
            for half in range(2):
                vh[half] = pr.op("vector", lambda e, par=par, t=t, lnxb=lnxb,
                                 half=half: e.tensor_scalar(
                                     out=lnxb[:, half * (D // 2):
                                              (half + 1) * (D // 2)],
                                     in0=xr[par][:, half * (D // 2):
                                                 (half + 1) * (D // 2)],
                                     scalar1=mean1[:, t:t + 1],
                                     scalar2=rstd1[:, t:t + 1],
                                     op0=ALU.subtract, op1=ALU.mult),
                                 waits=[(xsem[par], cnt), (sS, rstd_done),
                                        (sV, mean_done),
                                        (sT, tile_tr_done[par])],
                                 inc=sV)
            vdone[par] = vh[1]
            for dt_ in range(DT):
                ppar = dt_ % 4
                ptr = pstx[ppar]
                td = pr.op("tensor", lambda e, dt_=dt_, ptr=ptr, lnxb=lnxb:
                           e.transpose(ptr[:, 0:P], lnxb[:, dt_ * P:(dt_ + 1) * P],
                                       eye_bf[:, :]),
                           waits=[(sV, vh[0] if dt_ < DT // 2 else vh[1]),
                                  ev_done[ppar]], inc=sT)
                if dt_ % 2 == 0:
                    ev_done[ppar] = (sS, pr.op(
                        "scalar", lambda e, dt_=dt_, t=t, ptr=ptr: e.activation(
                            out=h1T[:, dt_ * D + t * P: dt_ * D + (t + 1) * P],
                            in_=ptr[:, 0:P], func=AF.Identity,
                            scale=cacol[:, dt_:dt_ + 1],
                            bias=sacol[:, dt_:dt_ + 1]),
                        waits=[(sT, td), (sV, saca_done)], inc=sS))
                else:
                    ev_done[ppar] = (sV, pr.op(
                        "vector", lambda e, dt_=dt_, t=t, ptr=ptr: e.tensor_scalar(
                            out=h1T[:, dt_ * D + t * P: dt_ * D + (t + 1) * P],
                            in0=ptr[:, 0:P],
                            scalar1=cacol[:, dt_:dt_ + 1],
                            scalar2=sacol[:, dt_:dt_ + 1],
                            op0=ALU.mult, op1=ALU.add),
                        waits=[(sT, td), (sV, saca_done)], inc=sV))
            tile_tr_done[par] = sT.n
            while ms_dma < NMS and ms_dma < 24 + (t + 1) * 2:
                emit_mod_dma(ms_dma)
                ms_dma += 1
            while ms_next < NMS and ms_next < 24 + t * 2:
                emit_mod_slice(ms_next)
                ms_next += 1
        while ms_next < NMS:
            emit_mod_slice(ms_next)
            ms_next += 1
        dma("gpsimd", g_hbm[0], garow[0:1, :], dOut,
            waits=[(sV, mod_state["clampv"])])
        dma("gpsimd", g_hbm[1], gbrow[0:1, :], dOut,
            waits=[(sV, mod_state["clampv"])])
        BARRIER()
        if debug:
            dbg["h1T"] = nc.dram_tensor("dbg_h1T", [P, DT * D], BF16,
                                        kind="ExternalOutput").ap()
            dma("gpsimd", dbg["h1T"], h1T[:, :], dOut)
            dbg["garow"] = nc.dram_tensor("dbg_garow", [1, D], BF16,
                                          kind="ExternalOutput").ap()
            dbg["sacol"] = nc.dram_tensor("dbg_sacol", [P, DT], F32,
                                          kind="ExternalOutput").ap()
            dbg["cacol"] = nc.dram_tensor("dbg_cacol", [P, DT], F32,
                                          kind="ExternalOutput").ap()
            dma("gpsimd", dbg["garow"], garow[0:1, :], dOut)
            dma("gpsimd", dbg["sacol"], sacol[:, :], dOut)
            dma("gpsimd", dbg["cacol"], cacol[:, :], dOut)
            BARRIER()
        pr.flush()
    _sc1.close()

    _sc3 = ExitStack()
    wfbuf = _sc3.enter_context(nc.sbuf_tensor("wfbuf", [P, DT * H], F32, side="right"))
    wfbf = _sc3.enter_context(nc.sbuf_tensor("wfbf", [P, DT * H], BF16, side="right"))
    ldT = _sc3.enter_context(nc.sbuf_tensor("ldT", [16, D], F32, side="right"))
    cumall = _sc3.enter_context(nc.sbuf_tensor("cumall", [P, NCH * H], F32, side="right"))
    ldtok = _sc3.enter_context(nc.sbuf_tensor("ldtok", [P, NCH * H], F32, side="right"))
    tmp4 = _sc3.enter_context(nc.sbuf_tensor("tmp4", [P, NCH * H], F32, side="right"))
    totbc = _sc3.enter_context(nc.sbuf_tensor("totbc", [P, NCH * H], F32, side="right"))
    ps_l0 = _sc3.enter_context(nc.psum_tensor("ps_l0", [16, 512], F32))
    ps_l1 = _sc3.enter_context(nc.psum_tensor("ps_l1", [16, 512], F32))
    ps_c0 = _sc3.enter_context(nc.psum_tensor("ps_c0", [P, 512], F32))
    ps_c1 = _sc3.enter_context(nc.psum_tensor("ps_c1", [P, 512], F32))
    if True:

        ps_c = (ps_c0, ps_c1)
        cnt = dma("sync", wfbuf[:, :].rearrange("p (dt h) -> p dt h", dt=DT), wf_d.rearrange("dt p h -> p dt h"), dA)
        wfc = pr.op("vector", lambda e: e.tensor_copy(out=wfbf[:, :], in_=wfbuf[:, :]),
                    waits=[(dA, cnt)], inc=sV)
        ld_ev = [0, 0]
        for ch in range(4):
            lp = ch % 2
            pld = ps_l0 if lp == 0 else ps_l1
            md = None
            for dt_ in range(DT):
                md = pr.op("tensor", lambda e, ch=ch, dt_=dt_, pld=pld: e.matmul(
                    pld[:, :], wfbf[:, dt_ * H:(dt_ + 1) * H],
                    h1T[:, dt_ * D + ch * 512: dt_ * D + (ch + 1) * 512],
                    start=(dt_ == 0), stop=(dt_ == DT - 1)),
                    waits=[(sV, wfc), (sS, ld_ev[lp])] if dt_ == 0 else [],
                    inc=(sT if dt_ == DT - 1 else None))
            ld_ev[lp] = pr.op("scalar", lambda e, ch=ch, pld=pld: e.activation(
                out=ldT[:, ch * 512:(ch + 1) * 512], in_=pld[:, :], func=AF.Sigmoid),
                waits=[(sT, md)], inc=sS)
        lds = pr.op("scalar", lambda e: e.activation(
            out=ldT[:, :], in_=ldT[:, :], func=AF.Ln),
            waits=[(sS, max(ld_ev))], inc=sS)

        pev = [0, 0]
        for c in range(NCH):
            cp = c % 2
            td = pr.op("tensor", lambda e, c=c, cp=cp: e.transpose(
                ps_c[cp][0:P, 0:16], ldT[:, c * P:(c + 1) * P], eye16[:, :]),
                waits=[(sS, lds), (sV, pev[cp])], inc=sT)
            pev[cp] = pr.op("vector", lambda e, c=c, cp=cp: e.tensor_copy(
                out=ldtok[:, c * H:(c + 1) * H], in_=ps_c[cp][0:P, 0:16]),
                waits=[(sT, td)], inc=sV)
        ld_all = sV.n
        for c in range(NCH):
            cp = c % 2
            pr.op("tensor", lambda e, c=c, cp=cp: e.matmul(
                ps_c[cp][:, 0:16], tri[:, :], ldtok[:, c * H:(c + 1) * H],
                start=True, stop=True),
                waits=[(sV, ld_all), (sV, pev[cp])])
            td = pr.op("tensor", lambda e, c=c, cp=cp: e.matmul(
                ps_c[cp][0:1, 16:32], onescol[:, :], ldtok[:, c * H:(c + 1) * H],
                start=True, stop=True), inc=sT)
            pr.op("vector", lambda e, c=c, cp=cp: e.tensor_copy(
                out=cumall[:, c * H:(c + 1) * H], in_=ps_c[cp][:, 0:16]),
                waits=[(sT, td)])
            pev[cp] = pr.op("vector", lambda e, c=c, cp=cp: e.tensor_copy(
                out=totrow[0:1, c * H:(c + 1) * H], in_=ps_c[cp][0:1, 16:32]),
                inc=sV)
        cum_done = sV.n

        nbc = NCH * H // P
        for i in range(nbc):
            cp = i % 2
            td = pr.op("tensor", lambda e, i=i, cp=cp: e.matmul(
                ps_c[cp][:, 0:P], ones_f[0:1, :], totrow[0:1, i * P:(i + 1) * P],
                start=True, stop=True), waits=[(sV, cum_done), (sV, pev[cp])], inc=sT)
            pev[cp] = pr.op("vector", lambda e, i=i, cp=cp: e.tensor_copy(
                out=totbc[:, i * P:(i + 1) * P], in_=ps_c[cp][:, 0:P]),
                waits=[(sT, td)], inc=sV)
        bc_ev = sV.n

        vt = pr.op("vector", lambda e: e.scalar_tensor_tensor(
            out=tmp4[:, :], in0=totbc[:, :], scalar=0.5, in1=cumall[:, :],
            op0=ALU.mult, op1=ALU.subtract), waits=[(sV, bc_ev)], inc=sV)
        pr.op("scalar", lambda e: e.activation(out=e1t[:, :], in_=tmp4[:, :],
                                               func=AF.Exp), waits=[(sV, vt)])
        e12 = pr.op("scalar", lambda e: e.activation(out=e2t[:, :], in_=tmp4[:, :],
                                                     func=AF.Exp, scale=-1.0), inc=sS)
        NE = (NCH - 1) * H
        p3 = pr.op("vector", lambda e: e.tensor_tensor(
            out=tmp4[:, 0:NE], in0=totbc[:, 0:NE], in1=cumall[:, 0:NE],
            op=ALU.subtract), waits=[(sS, e12)], inc=sV)
        vt3 = pr.op("vector", lambda e: e.scalar_tensor_tensor(
            out=tmp4[:, 0:NE], in0=totbc[:, H:H + NE], scalar=0.5, in1=tmp4[:, 0:NE],
            op0=ALU.mult, op1=ALU.add), waits=[(sV, p3)], inc=sV)
        e3d = pr.op("scalar", lambda e: e.activation(
            out=e3t[:, :], in_=tmp4[:, 0:NE], func=AF.Exp), waits=[(sV, vt3)], inc=sS)
        vt4 = pr.op("vector", lambda e: e.tensor_tensor(
            out=tmp4[0:1, 0:NE], in0=totrow[0:1, 0:NE],
            in1=totrow[0:1, H:H + NE], op=ALU.add), waits=[(sS, e3d)], inc=sV)
        e4r = pr.op("scalar", lambda e: e.activation(
            out=tmp4[0:1, 0:NE], in_=tmp4[0:1, 0:NE], func=AF.Exp, scale=0.5),
            waits=[(sV, vt4)], inc=sS)
        for i in range((NE + P - 1) // P):
            w_ = min(P, NE - i * P)
            cp = i % 2
            td = pr.op("tensor", lambda e, i=i, w_=w_, cp=cp: e.matmul(
                ps_c[cp][:, 0:w_], ones_f[0:1, :], tmp4[0:1, i * P:i * P + w_],
                start=True, stop=True), waits=[(sS, e4r), (sV, pev[cp])], inc=sT)
            pev[cp] = pr.op("vector", lambda e, i=i, w_=w_, cp=cp: e.tensor_copy(
                out=e4bc[:, i * P:i * P + w_], in_=ps_c[cp][:, 0:w_]),
                waits=[(sT, td)], inc=sV)
        BARRIER()
        if debug:
            for nm, tens, wdt in (("ldtok", ldtok, NCH * H), ("cumall", cumall, NCH * H),
                                  ("e1t", e1t, NCH * H), ("e2t", e2t, NCH * H),
                                  ("e3t", e3t, NE), ("e4bc", e4bc, NE)):
                dbg[nm] = nc.dram_tensor("dbg_" + nm, [P, wdt], F32,
                                         kind="ExternalOutput").ap()
                dma("gpsimd", dbg[nm], tens[:, :], dOut)
            BARRIER()
        pr.flush()
    _sc3.close()

    _sc4 = ExitStack()
    wqbf0 = _sc4.enter_context(nc.sbuf_tensor("wqbf0", [P, D], BF16, side="right"))
    wkbf0 = _sc4.enter_context(nc.sbuf_tensor("wkbf0", [P, D], BF16, side="right"))
    wvbf0 = _sc4.enter_context(nc.sbuf_tensor("wvbf0", [P, D], BF16, side="right"))
    wqbf1 = _sc4.enter_context(nc.sbuf_tensor("wqbf1", [P, D], BF16, side="right"))
    wkbf1 = _sc4.enter_context(nc.sbuf_tensor("wkbf1", [P, D], BF16, side="right"))
    wvbf1 = _sc4.enter_context(nc.sbuf_tensor("wvbf1", [P, D], BF16, side="right"))
    qT0 = _sc4.enter_context(nc.sbuf_tensor("qT0", [P, D], BF16, side="right"))
    qT1 = _sc4.enter_context(nc.sbuf_tensor("qT1", [P, D], BF16, side="right"))
    kT0 = _sc4.enter_context(nc.sbuf_tensor("kT0", [P, D], BF16, side="right"))
    kT1 = _sc4.enter_context(nc.sbuf_tensor("kT1", [P, D], BF16, side="right"))
    vT = _sc4.enter_context(nc.sbuf_tensor("vT", [P, D], BF16, side="right"))
    ktok0 = _sc4.enter_context(nc.sbuf_tensor("ktok0", [P, D], BF16, side="right"))
    ktok1 = _sc4.enter_context(nc.sbuf_tensor("ktok1", [P, D], BF16, side="right"))
    vtok0 = _sc4.enter_context(nc.sbuf_tensor("vtok0", [P, D], BF16, side="right"))
    vtok1 = _sc4.enter_context(nc.sbuf_tensor("vtok1", [P, D], BF16, side="right"))
    s3b = [[_sc4.enter_context(nc.sbuf_tensor(f"s3b{i}{j}", [P, P], BF16, side="right"))
            for j in range(2)] for i in range(2)]
    obb = [[_sc4.enter_context(nc.sbuf_tensor(f"obb{i}{j}", [P, P], BF16, side="right"))
            for j in range(2)] for i in range(2)]
    kdb = [[_sc4.enter_context(nc.sbuf_tensor(f"kdb{i}{j}", [P, P], BF16, side="right"))
            for j in range(2)] for i in range(2)]
    junk5 = _sc4.enter_context(nc.sbuf_tensor("junk5", [P, P], BF16, side="right"))
    Sbf1 = _sc4.enter_context(nc.sbuf_tensor("Sbf1", [P, E], BF16, side="right"))
    ps_p0 = _sc4.enter_context(nc.psum_tensor("ps_p0", [P, 512], F32))
    ps_p1 = _sc4.enter_context(nc.psum_tensor("ps_p1", [P, 512], F32))
    ps_s0 = _sc4.enter_context(nc.psum_tensor("ps_s0", [P, 512], F32))
    ps_s1 = _sc4.enter_context(nc.psum_tensor("ps_s1", [P, 512], F32))
    ps_S0 = _sc4.enter_context(nc.psum_tensor("ps_S0", [P, 512], F32))
    ps_S1 = _sc4.enter_context(nc.psum_tensor("ps_S1", [P, 512], F32))
    ps_x0 = _sc4.enter_context(nc.psum_tensor("ps_x0", [P, 1024], BF16))
    ps_x1 = _sc4.enter_context(nc.psum_tensor("ps_x1", [P, 1024], BF16))
    if True:
        wsem = (dA, dB)
        qTs = (qT0, qT1)
        kTs = (kT0, kT1)
        ktoks = (ktok0, ktok1)
        vtoks = (vtok0, vtok1)
        Sbfs = (Sbf, Sbf1)
        ps_ss = (ps_s0, ps_s1)
        ps_Ss = (ps_S0, ps_S1)
        ps_os = (ps_p0, ps_p1)
        ps_xs = (ps_x0, ps_x1)
        wbfs2 = ({0: wqbf0, 1: wkbf0, 2: wvbf0}, {0: wqbf1, 1: wkbf1, 2: wvbf1})

        bank_free = [(sS, 0), (sS, 0)]
        trx_free = [(sV, 0), (sV, 0)]
        wsem_i = 0
        cast_done = {}
        mat_last_mm = [[0, 0, 0], [0, 0, 0]]

        def weight_chain(h):
            nonlocal wsem_i
            for m in range(3):
                sem = (dF, dG)[wsem_i % 2]
                wsem_i += 1
                cnt = dma("gpsimd",
                          wbfs2[h % 2][m][:, :].rearrange("p (dt e) -> p dt e", dt=DT),
                          wqkv_d[h][m].rearrange("dt p e -> p dt e"), sem,
                          waits=[(sT, mat_last_mm[h % 2][m]), (sem, sem.n)])
                cast_done[(h, m)] = (sem, cnt)

        weight_chain(0)
        weight_chain(1)

        supd = [0, 0]
        m2m3 = [0, 0]
        m4d = [0, 0]
        oev = [[(sS, 0), (sS, 0)], [(sS, 0), (sS, 0)]]
        s1ev = [0, 0]
        kdv = [[0, 0], [0, 0]]
        otr = [[(sV, 0), (sV, 0)], [(sV, 0), (sV, 0)]]
        sq = [0, 0]
        sq_add = 0
        for p in range(H // 2):
            ha, hb = 2 * p, 2 * p + 1

            def proj(h, m, dstT, func):
                lastmm = 0
                for ch in range(4):
                    ppar = ch % 2
                    pp = ps_os[ppar]
                    wbf = wbfs2[h % 2][m]
                    md = None
                    for dt_ in range(DT):
                        md = pr.op("tensor",
                                   lambda e, ch=ch, dt_=dt_, pp=pp, wbf=wbf:
                                   e.matmul(pp[:, :], wbf[:, dt_ * P:(dt_ + 1) * P],
                                            h1T[:, dt_ * D + ch * 512: dt_ * D + (ch + 1) * 512],
                                            start=(dt_ == 0), stop=(dt_ == DT - 1)),
                                   waits=([cast_done[(h, m)], bank_free[ppar]]
                                          if dt_ == 0 else []),
                                   inc=(sT if dt_ == DT - 1 else None))
                    lastmm = md
                    if (ch + m) % 2 == 0:
                        bank_free[ppar] = (sS, pr.op(
                            "scalar", lambda e, ch=ch, pp=pp, dstT=dstT, func=func:
                            e.activation(out=dstT[:, ch * 512:(ch + 1) * 512],
                                         in_=pp[:, :], func=func),
                            waits=[(sT, md)], inc=sS))
                    else:
                        if func == AF.Silu:
                            bank_free[ppar] = (sS, pr.op(
                                "scalar", lambda e, ch=ch, pp=pp, dstT=dstT:
                                e.activation(out=dstT[:, ch * 512:(ch + 1) * 512],
                                             in_=pp[:, :], func=AF.Silu),
                                waits=[(sT, md)], inc=sS))
                        else:
                            bank_free[ppar] = (sV, pr.op(
                                "vector", lambda e, ch=ch, pp=pp, dstT=dstT:
                                e.tensor_copy(out=dstT[:, ch * 512:(ch + 1) * 512],
                                              in_=pp[:, :]),
                                waits=[(sT, md)], inc=sV))
                mat_last_mm[h % 2][m] = lastmm
                return bank_free

            def transp_all(srcT, dst, hx):
                for g in range(NCH // 4):
                    xp = g % 2
                    pt = ps_xs[xp]
                    td = 0
                    for j in range(4):
                        c = g * 4 + j
                        td = pr.op("tensor",
                                   lambda e, c=c, j=j, pt=pt, srcT=srcT: e.transpose(
                                       pt[:, j * P:(j + 1) * P],
                                       srcT[:, c * P:(c + 1) * P], eye_bf[:, :]),
                                   waits=([bank_free[0], bank_free[1], trx_free[xp]]
                                          if j == 0 else []),
                                   inc=(sT if j == 3 else None))
                    trx_free[xp] = (sV, pr.op(
                        "vector", lambda e, g=g, pt=pt, dst=dst: e.tensor_copy(
                            out=dst[:, g * 512:(g + 1) * 512], in_=pt[:, 0:512]),
                        waits=[(sT, td)], inc=sV))

            for h, hi in ((ha, 0), (hb, 1)):
                proj(h, 0, qTs[hi], AF.Silu)
                proj(h, 1, kTs[hi], AF.Silu)
                proj(h, 2, vT, AF.Identity)
                transp_all(kTs[hi], ktoks[hi], hi)
                transp_all(vT, vtoks[hi], hi)

            if p + 1 < H // 2:
                weight_chain(2 * p + 2)
                weight_chain(2 * p + 3)

            kvdone = (trx_free[0], trx_free[1])
            pend_o = [None, None]

            def emit_otd(hi):
                c0, h0, obuf0, oevt = pend_o[hi]
                pt = ps_xs[hi]
                otd = pr.op("tensor", lambda e, obuf0=obuf0, pt=pt: e.transpose(
                    pt[:, 0:P], obuf0[:, :], eye_bf[:, :]),
                    waits=[oevt, trx_free[hi]], inc=sT)
                trx_free[hi] = (sS, pr.op(
                    "scalar", lambda e, c0=c0, h0=h0, pt=pt: e.activation(
                        out=oT[:, h0 * D + c0 * P: h0 * D + (c0 + 1) * P],
                        in_=pt[:, 0:P], func=AF.Identity),
                    waits=[(sT, otd)], inc=sS))
                otr[hi][c0 % 2] = trx_free[hi]
                pend_o[hi] = None

            def emit_m1(c, h, hi, first):
                qT = qTs[hi]
                kT = kTs[hi]
                pss = ps_ss[hi]
                s3buf = s3b[hi][c % 2]
                m1 = pr.op("tensor", lambda e, c=c, kT=kT, qT=qT, pss=pss:
                           e.matmul(pss[:, 0:P], kT[:, c * P:(c + 1) * P],
                                    qT[:, c * P:(c + 1) * P],
                                    start=True, stop=True),
                           waits=[(sV, s1ev[hi])] +
                                 ([kvdone[0], kvdone[1]] if first else []),
                           inc=sT)
                s1ev[hi] = pr.op("vector",
                                 lambda e, col=c * H + h, s3buf=s3buf, pss=pss:
                                 e.scalar_tensor_tensor(
                                     out=s3buf[:, :], in0=pss[:, 0:P],
                                     scalar=e1t[:, col:col + 1],
                                     in1=maskb[:, :],
                                     op0=ALU.mult, op1=ALU.mult),
                                 waits=[(sT, m1), (sT, m2m3[hi])], inc=sV)

            def emit_kdv(c, h, hi, first):
                kdbuf = kdb[hi][c % 2]
                ktok = ktoks[hi]
                kdv[hi][c % 2] = pr.op(
                    "scalar", lambda e, c=c, h=h, kdbuf=kdbuf, ktok=ktok:
                    e.activation(out=kdbuf[:, :],
                                 in_=ktok[:, c * P:(c + 1) * P],
                                 func=AF.Identity,
                                 scale=e3t[:, c * H + h:c * H + h + 1]),
                    waits=[(sT, m4d[hi])] +
                          ([kvdone[0], kvdone[1]] if first else []),
                    inc=sS)

            for hi in range(2):
                supd[hi] = pr.op("vector",
                                 lambda e, hi=hi: e.memset(Sbfs[hi][:, :], 0.0),
                                 waits=[(sT, m2m3[hi])], inc=sV)
            for h, hi in ((ha, 0), (hb, 1)):
                emit_m1(0, h, hi, True)
                emit_kdv(0, h, hi, True)
            for c in range(NCH):
                cp = c % 2
                for h, hi in ((ha, 0), (hb, 1)):
                    col = c * H + h
                    qT = qTs[hi]
                    vtok = vtoks[hi]
                    s3buf = s3b[hi][cp]
                    obuf = obb[hi][cp]
                    kdbuf = kdb[hi][cp]
                    po = ps_os[hi]
                    pS = ps_Ss[hi]

                    m2m3[hi] = pr.op("tensor",
                                     lambda e, c=c, po=po, s3buf=s3buf, vtok=vtok:
                                     e.matmul(po[:, 0:P], s3buf[:, :],
                                              vtok[:, c * P:(c + 1) * P],
                                              start=True, stop=(c == 0)),
                                     waits=[(sV, s1ev[hi]),
                                            (bank_free[hi] if c == 0
                                             else oev[hi][cp ^ 1])],
                                     inc=(sT if c == 0 else None))
                    if c > 0:
                        m2m3[hi] = pr.op("tensor",
                                         lambda e, c=c, po=po, qT=qT, hi=hi:
                                         e.matmul(po[:, 0:P],
                                                  qT[:, c * P:(c + 1) * P],
                                                  Sbfs[hi][:, :],
                                                  start=False, stop=True),
                                         waits=[(sV, supd[hi])], inc=sT)

                    if c < NCH - 1:
                        m4d[hi] = pr.op("tensor",
                                        lambda e, c=c, kdbuf=kdbuf, vtok=vtok, pS=pS:
                                        e.matmul(pS[:, 0:P], kdbuf[:, :],
                                                 vtok[:, c * P:(c + 1) * P],
                                                 start=True, stop=True),
                                        waits=[(sS, kdv[hi][cp]), (sV, supd[hi])],
                                        inc=sT)
                        supd[hi] = pr.op("vector",
                                         lambda e, c=c, h=h, hi=hi, pS=pS:
                                         e.scalar_tensor_tensor(
                                             out=Sbfs[hi][:, :], in0=Sbfs[hi][:, :],
                                             scalar=e4bc[:, c * H + h:c * H + h + 1],
                                             in1=pS[:, 0:P],
                                             op0=ALU.mult, op1=ALU.add),
                                         waits=[(sT, m4d[hi]), (sT, m2m3[hi]),
                                                (sV, supd[hi])], inc=sV)

                    if pend_o[hi] is not None:
                        emit_otd(hi)
                    oev[hi][cp] = (sS, pr.op(
                        "scalar", lambda e, col=col, po=po, obuf=obuf:
                        e.activation(out=obuf[:, :], in_=po[:, 0:P],
                                     func=AF.Identity,
                                     scale=e2t[:, col:col + 1]),
                        waits=[(sT, m2m3[hi]), otr[hi][cp]], inc=sS))
                    pend_o[hi] = (c, h, obuf, oev[hi][cp])
                if c + 1 < NCH:
                    for h, hi in ((ha, 0), (hb, 1)):
                        emit_m1(c + 1, h, hi, False)
                        if c + 1 < NCH - 1:
                            emit_kdv(c + 1, h, hi, False)
            for hi in range(2):
                emit_otd(hi)
            bank_free = [oev[0][(NCH - 1) % 2], oev[1][(NCH - 1) % 2]]
        BARRIER()
        if debug:
            for nm, tens in (("qT", qT1), ("kT", kT1), ("vtok", vtok1),
                             ("ktok", ktok1)):
                dbg[nm] = nc.dram_tensor("dbg_" + nm, [P, D], BF16,
                                         kind="ExternalOutput").ap()
                dma("gpsimd", dbg[nm], tens[:, :], dOut)
            dbg["oT"] = nc.dram_tensor("dbg_oT", [P, H * D], BF16,
                                       kind="ExternalOutput").ap()
            dma("gpsimd", dbg["oT"], oT[:, :], dOut)
            BARRIER()
        pr.flush()
    _sc4.close()

    wo_bf = h1T

    _sc5 = ExitStack()
    wob0 = _sc5.enter_context(nc.sbuf_tensor("wobuf0", [P, D], F32, side="right"))
    wob1 = _sc5.enter_context(nc.sbuf_tensor("wobuf1", [P, D], F32, side="right"))
    xrc0 = _sc5.enter_context(nc.sbuf_tensor("xrc0", [P, D], F32, side="right"))
    xrc1 = _sc5.enter_context(nc.sbuf_tensor("xrc1", [P, D], F32, side="right"))
    x1b0 = _sc5.enter_context(nc.sbuf_tensor("x1b0", [P, D], F32, side="right"))
    x1b1 = _sc5.enter_context(nc.sbuf_tensor("x1b1", [P, D], F32, side="right"))
    gaB = _sc5.enter_context(nc.sbuf_tensor("gaB", [P, D], BF16, side="right"))
    garow6 = _sc5.enter_context(nc.sbuf_tensor("garow6", [1, D], BF16, side="right"))
    junk6b = _sc5.enter_context(nc.sbuf_tensor("junk6b", [P, D], BF16, side="right"))
    rrow = _sc5.enter_context(nc.sbuf_tensor("rrow", [1, D], F32, side="right"))
    ps_a0 = _sc5.enter_context(nc.psum_tensor("ps_a0", [P, 512], F32))
    ps_a1 = _sc5.enter_context(nc.psum_tensor("ps_a1", [P, 512], F32))
    ps_a2 = _sc5.enter_context(nc.psum_tensor("ps_a2", [P, 512], F32))
    ps_a3 = _sc5.enter_context(nc.psum_tensor("ps_a3", [P, 512], F32))
    ps_gb = _sc5.enter_context(nc.psum_tensor("ps_gb", [P, 512], F32))
    ps_b0 = _sc5.enter_context(nc.psum_tensor("ps_b0", [P, 512], F32))
    ps_b1 = _sc5.enter_context(nc.psum_tensor("ps_b1", [P, 512], F32))
    ps_b2 = _sc5.enter_context(nc.psum_tensor("ps_b2", [P, 512], F32))
    if True:

        ps_a = (ps_a0, ps_a1, ps_a2, ps_a3)
        garc = dma("sync", garow6[0:1, :], g_hbm[0], dC)
        sqv = 0
        smm = 0
        for ft in range(DT):
            sqv = pr.op("vector", lambda e, ft=ft: e.tensor_tensor(
                out=junk6b[:, :], in0=oT[:, ft * D:(ft + 1) * D],
                in1=oT[:, ft * D:(ft + 1) * D], op=ALU.mult),
                waits=[(sT, smm)], inc=sV)
            for mc in range(4):
                smm = pr.op("tensor", lambda e, ft=ft, mc=mc: e.matmul(
                    ps_a[mc][0:1, :], onescol_b[:, 0:1],
                    junk6b[:, mc * 512:(mc + 1) * 512],
                    start=(ft == 0), stop=(ft == DT - 1)),
                    waits=[(sV, sqv)] if mc == 0 else [],
                    inc=(sT if mc == 3 else None))
        rrow_ev = 0
        rrowv = [0, 0, 0, 0]
        for mc in range(4):
            rrow_ev = pr.op("vector", lambda e, mc=mc: e.tensor_copy(
                out=rrow[0:1, mc * 512:(mc + 1) * 512], in_=ps_a[mc][0:1, :]),
                waits=[(sT, smm)], inc=sV)
            rrowv[mc] = rrow_ev
        rl = pr.op("scalar", lambda e: e.activation(
            out=rrow[0:1, :], in_=rrow[0:1, :], func=AF.Ln, bias=RMSEPS,
            scale=1.0 / D), waits=[(sV, rrow_ev)], inc=sS)
        rx = pr.op("scalar", lambda e: e.activation(
            out=rrow[0:1, :], in_=rrow[0:1, :], func=AF.Exp, scale=-0.5),
            waits=[(sS, rl)], inc=sS)
        rcm = 0
        for t in range(TT):
            rcm = pr.op("tensor", lambda e, t=t: e.matmul(
                ps_gb[:, t:t + 1], rrow[0:1, t * P:(t + 1) * P], ones_f[0:1, 0:1],
                start=True, stop=True),
                waits=[(sS, rx)], inc=(sT if t == TT - 1 else None))
        rms_done = pr.op("vector", lambda e: e.tensor_copy(
            out=rmsr[:, :], in_=ps_gb[:, 0:TT]), waits=[(sT, rcm)], inc=sV)
        rms_sem = sV

        gaB_done = 0
        gev = rms_done
        for i in range(4):
            td = pr.op("tensor", lambda e, i=i: e.matmul(
                ps_gb[:, :], ones_b[0:1, :], garow6[0:1, i * 512:(i + 1) * 512],
                start=True, stop=True), waits=[(sV, gev), (dC, garc)], inc=sT)
            gev = pr.op("vector", lambda e, i=i: e.tensor_copy(
                out=gaB[:, i * 512:(i + 1) * 512], in_=ps_gb[:, :]),
                waits=[(sT, td)], inc=sV)
        gaB_done = gev

        wsem = (dA, dB)
        wob = (wob0, wob1)
        wo_cast = [(sS, 0)] * DT
        ccount = [(sS, 0), (sS, 0)]
        for ft in range(DT):
            par = ft % 2
            w = [ccount[par]] if ft >= 2 else []
            cnt = dma("sync", wob[par][:, :], wo_d[ft], wsem[par], waits=w)
            if ft % 2 == 0:
                ccount[par] = (sS, pr.op("scalar", lambda e, ft=ft, par=par:
                                         e.activation(
                                             out=wo_bf[:, ft * D:(ft + 1) * D],
                                             in_=wob[par][:, :], func=AF.Identity,
                                             scale=nwcol[:, ft:ft + 1]),
                                         waits=[(wsem[par], cnt)], inc=sS))
            else:
                ccount[par] = (sV, pr.op("vector", lambda e, ft=ft, par=par:
                                         e.tensor_scalar(
                                             out=wo_bf[:, ft * D:(ft + 1) * D],
                                             in0=wob[par][:, :],
                                             scalar1=nwcol[:, ft:ft + 1],
                                             scalar2=None, op0=ALU.mult),
                                         waits=[(wsem[par], cnt)], inc=sV))
            wo_cast[ft] = ccount[par]

        xsem6 = (dC, dE)
        x1sem = (dF, dG)
        xcnt = [0, 0]
        x1w = [0, 0]
        vread = [0, 0]
        evs = [0, 0, 0, 0]
        ps_sets6 = ((ps_a0, ps_a1, ps_a2, ps_a3), (ps_b0, ps_b1, ps_b2, ps_gb))
        stt_done = [[0, 0, 0, 0], [0, 0, 0, gaB_done]]
        sst = 0
        for t in range(TT):
            par = t % 2
            pa = ps_sets6[par]
            xrc = xrc0 if par == 0 else xrc1
            x1buf = x1b0 if par == 0 else x1b1
            w = [(sV, vread[par])] if t >= 2 else [(dC, garc)]
            xcnt[par] = dma("sync", xrc[:, :], x_d[t], xsem6[par], waits=w)
            mmd = None
            for ft in range(DT):
                for mc in range(4):
                    mmd = pr.op("tensor", lambda e, t=t, mc=mc, ft=ft, pa=pa: e.matmul(
                        pa[mc][:, :], oT[:, ft * D + t * P:ft * D + (t + 1) * P],
                        wo_bf[:, ft * D + mc * 512: ft * D + (mc + 1) * 512],
                        start=(ft == 0), stop=(ft == DT - 1)),
                        waits=(([wo_cast[ft]] if t == 0 and mc == 0 else []) +
                               ([(sV, stt_done[par][mc])] if ft == 0 else []) +
                               ([(sV, rrowv[mc])] if t == 0 and ft == 0 else [])),
                        inc=(sT if (ft == DT - 1 and mc == 3) else None))
            for mc in range(4):
                sttd = pr.op("vector", lambda e, t=t, mc=mc, x1buf=x1buf, pa=pa:
                             e.scalar_tensor_tensor(
                                 out=x1buf[:, mc * 512:(mc + 1) * 512],
                                 in0=pa[mc][:, :],
                                 scalar=rmsr[:, t:t + 1],
                                 in1=gaB[:, mc * 512:(mc + 1) * 512],
                                 op0=ALU.mult, op1=ALU.mult),
                             waits=[(sT, mmd), (sV, rms_done), (sV, gaB_done),
                                    (x1sem[par], x1w[par])], inc=sV)
                stt_done[par][mc] = sttd
                evs[mc] = pr.op("vector", lambda e, mc=mc, x1buf=x1buf, xrc=xrc:
                                e.tensor_tensor(
                                    out=x1buf[:, mc * 512:(mc + 1) * 512],
                                    in0=x1buf[:, mc * 512:(mc + 1) * 512],
                                    in1=xrc[:, mc * 512:(mc + 1) * 512],
                                    op=ALU.add),
                                waits=[(xsem6[par], xcnt[par]), (sV, sttd)], inc=sV)
            vread[par] = evs[3]
            vst = pr.op("vector", lambda e, t=t, x1buf=x1buf: e.tensor_reduce(
                out=msum[:, t:t + 1], in_=x1buf[:, :], axis=AX.X, op=ALU.add),
                waits=[(sV, evs[3])], inc=sV)
            sst = pr.op("scalar", lambda e, t=t, x1buf=x1buf: e.activation(
                out=junk6b[:, :], in_=x1buf[:, :], func=AF.Square,
                accum_out=sqsum[:, t:t + 1]),
                waits=[(sV, evs[3]), (sS, sst if t else 0)], inc=sS)
            x1w[par] = dma("gpsimd", x1_hbm[t], x1buf[:, :], x1sem[par],
                           waits=[(sV, vst), (sS, sst)])

        mean2_done = pr.op("vector", lambda e: e.tensor_scalar(
            out=mean2[:, :], in0=msum[:, :], scalar1=1.0 / D, scalar2=None,
            op0=ALU.mult), waits=[(sV, sV.n), (sS, sS.n)], inc=sV)
        c1b = pr.op("vector", lambda e: e.tensor_scalar(
            out=tmp1[:, 0:TT], in0=sqsum[:, :], scalar1=1.0 / D, scalar2=None,
            op0=ALU.mult), waits=[(sV, mean2_done)], inc=sV)
        c2b = pr.op("vector", lambda e: e.tensor_tensor(
            out=tmp1[:, 16:32], in0=mean2[:, :], in1=mean2[:, :], op=ALU.mult),
            waits=[(sV, c1b)], inc=sV)
        vst2 = pr.op("vector", lambda e: e.tensor_tensor(
            out=tmp1[:, 0:TT], in0=tmp1[:, 0:TT], in1=tmp1[:, 16:32],
            op=ALU.subtract), waits=[(sV, c2b)], inc=sV)
        ln2d = pr.op("scalar", lambda e: e.activation(
            out=tmp1[:, 0:TT], in_=tmp1[:, 0:TT], func=AF.Ln, bias=LNEPS, scale=1.0),
            waits=[(sV, vst2)], inc=sS)
        rstd2_done = pr.op("scalar", lambda e: e.activation(
            out=rstd2[:, :], in_=tmp1[:, 0:TT], func=AF.Exp, scale=-0.5),
            waits=[(sS, ln2d)], inc=sS)
        BARRIER()
        if debug:
            dbg["x1"] = nc.dram_tensor("dbg_x1", [TT, P, D], F32,
                                       kind="ExternalOutput").ap()
            for t in range(TT):
                dma("gpsimd", dbg["x1"][t], x1_hbm[t], dOut)
            BARRIER()
        pr.flush()
    _sc5.close()

    big1.close()

    big2 = ExitStack()
    h2T = big2.enter_context(nc.sbuf_tensor("h2T8", [P, DT * D], F8, side="left"))
    act2 = big2.enter_context(nc.sbuf_tensor("act2", [P, 2 * NS * QW], F8, side="left"))
    wl2 = big2.enter_context(nc.sbuf_tensor("wl2", [P, 2 * NS * QW], F8, side="left"))
    gbB = big2.enter_context(nc.sbuf_tensor("gbB", [P, D], BF16, side="left"))
    gbrow9 = big2.enter_context(nc.sbuf_tensor("gbrow9", [1, D], BF16, side="left"))

    _sc6 = ExitStack()
    x1r0 = _sc6.enter_context(nc.sbuf_tensor("x1r0", [P, D], F32, side="right"))
    x1r1 = _sc6.enter_context(nc.sbuf_tensor("x1r1", [P, D], F32, side="right"))
    lnxb20 = _sc6.enter_context(nc.sbuf_tensor("lnxb20", [P, D], BF16, side="right"))
    lnxb21 = _sc6.enter_context(nc.sbuf_tensor("lnxb21", [P, D], BF16, side="right"))
    ps_u0 = _sc6.enter_context(nc.psum_tensor("ps_u0", [P, 1024], BF16))
    ps_u1 = _sc6.enter_context(nc.psum_tensor("ps_u1", [P, 1024], BF16))
    ps_u2 = _sc6.enter_context(nc.psum_tensor("ps_u2", [P, 1024], BF16))
    ps_u3 = _sc6.enter_context(nc.psum_tensor("ps_u3", [P, 1024], BF16))
    if True:

        x1r = (x1r0, x1r1)
        xsem7 = (dA, dB)
        psux = (ps_u0, ps_u1, ps_u2, ps_u3)
        vdone = [0, 0]
        tile_tr_done = [0, 0]
        ev_done = [(sS, 0)] * 4
        for t in range(TT):
            par = t % 2
            lnxb2 = lnxb20 if par == 0 else lnxb21
            w = [(sV, vdone[par])] if t >= 2 else []
            cnt = dma("sync", x1r[par][:, :], x1_hbm[t], xsem7[par], waits=w)
            vh = [0, 0]
            for half in range(2):
                vh[half] = pr.op("vector", lambda e, par=par, t=t, lnxb2=lnxb2,
                                 half=half: e.tensor_scalar(
                                     out=lnxb2[:, half * (D // 2):
                                               (half + 1) * (D // 2)],
                                     in0=x1r[par][:, half * (D // 2):
                                                  (half + 1) * (D // 2)],
                                     scalar1=mean2[:, t:t + 1],
                                     scalar2=rstd2[:, t:t + 1],
                                     op0=ALU.subtract, op1=ALU.mult),
                                 waits=[(xsem7[par], cnt), (sS, rstd2_done),
                                        (sV, mean2_done),
                                        (sT, tile_tr_done[par])], inc=sV)
            vdone[par] = vh[1]
            for dt_ in range(DT):
                ppar = dt_ % 4
                ptr = psux[ppar]
                td = pr.op("tensor", lambda e, dt_=dt_, ptr=ptr, lnxb2=lnxb2: e.transpose(
                    ptr[:, 0:P], lnxb2[:, dt_ * P:(dt_ + 1) * P], eye_bf[:, :]),
                    waits=[(sV, vh[0] if dt_ < DT // 2 else vh[1]),
                           ev_done[ppar]], inc=sT)
                if dt_ % 2 == 0:
                    ev_done[ppar] = (sS, pr.op(
                        "scalar", lambda e, dt_=dt_, t=t, ptr=ptr: e.activation(
                            out=h2T[:, dt_ * D + t * P: dt_ * D + (t + 1) * P],
                            in_=ptr[:, 0:P], func=AF.Identity,
                            scale=cbcol[:, dt_:dt_ + 1],
                            bias=sbcol[:, dt_:dt_ + 1]),
                        waits=[(sT, td)], inc=sS))
                else:
                    ev_done[ppar] = (sV, pr.op(
                        "vector", lambda e, dt_=dt_, t=t, ptr=ptr: e.tensor_scalar(
                            out=h2T[:, dt_ * D + t * P: dt_ * D + (t + 1) * P],
                            in0=ptr[:, 0:P],
                            scalar1=cbcol[:, dt_:dt_ + 1],
                            scalar2=sbcol[:, dt_:dt_ + 1],
                            op0=ALU.mult, op1=ALU.add),
                        waits=[(sT, td)], inc=sV))
            tile_tr_done[par] = sT.n
        BARRIER()
        if debug:
            dbg["h2T"] = nc.dram_tensor("dbg_h2T", [P, DT * D], BF16,
                                        kind="ExternalOutput").ap()
            dma("gpsimd", dbg["h2T"], h2T[:, :], dOut)
            BARRIER()
        pr.flush()
    _sc6.close()

    _sc7 = ExitStack()
    wg0f8 = [_sc7.enter_context(nc.sbuf_tensor(f"wg0f8{i}", [P, DT * E], F8, side="right"))
             for i in range(2)]
    wg1f8 = [_sc7.enter_context(nc.sbuf_tensor(f"wg1f8{i}", [P, DT * E], F8, side="right"))
             for i in range(2)]
    gelbs = [_sc7.enter_context(nc.sbuf_tensor(f"gelb{i}", [P, 512], BF16, side="right"))
             for i in range(4)]
    actbs = [_sc7.enter_context(nc.sbuf_tensor(f"actb{i}", [P, 2 * QW], F8, side="right"))
             for i in range(4)]
    ps_gs = [_sc7.enter_context(nc.psum_tensor(f"ps_g{i}", [P, 512], F32))
             for i in range(4)]
    ps_fs = [_sc7.enter_context(nc.psum_tensor(f"ps_f{i}", [P, 512], F32))
             for i in range(4)]
    if True:
        wcsem = (dJ, dK)
        actsem = (dF, dG, dH, dI)
        gmm = 0
        fmm = 0
        gel_done = [0, 0, 0, 0]
        act_done = [0, 0, 0, 0]
        actw = [0, 0, 0, 0]
        g0cnt = [0, 0]
        g1cnt = [0, 0]
        NDP = DT // 2
        wlcast_q = [[None] * 8 for _ in range(NQ)]
        preload_acnts = [[None] * 4 for _ in range(2)]
        ACT_CSEM = (dC, dE, dN, dO)
        ACT_CQ = ("sync", "sync", "scalar", "scalar")

        def act_chunk_dma(bp, ats, ck, extra_waits):
            sem = ACT_CSEM[ck]
            cnt = dma(ACT_CQ[ck],
                      act2[:, bp * NS * QW + ck * 8 * 2 * QW:
                           bp * NS * QW + (ck + 1) * 8 * 2 * QW].rearrange(
                          "p (s c) -> p s c", s=8),
                      act_hbm[ats][ck * 8:(ck + 1) * 8].rearrange("s p c -> p s c"),
                      sem, waits=[(sem, sem.n)] + list(extra_waits))
            return (sem, cnt)

        def wl_group_dma(q, g, extra_waits):
            sem = dP if g % 2 == 0 else dQ
            cnt = dma("gpsimd",
                      wl2[:, (q % 2) * NS * QW + g * 8 * QW:
                           (q % 2) * NS * QW + (g + 1) * 8 * QW].rearrange(
                          "p (s c) -> p s c", s=8),
                      wl_d[q][g * 8:(g + 1) * 8].rearrange("s p c -> p s c"),
                      sem, waits=[(sem, sem.n)] + list(extra_waits))
            wlcast_q[q][g] = (sem, cnt)

        def wl_half_dma(q, g, half, extra_waits):
            sem = dP if g % 2 == 0 else dQ
            s0 = g * 8 + half * 4
            cnt = dma("gpsimd",
                      wl2[:, (q % 2) * NS * QW + s0 * QW:
                           (q % 2) * NS * QW + (s0 + 4) * QW].rearrange(
                          "p (s c) -> p s c", s=4),
                      wl_d[q][s0:s0 + 4].rearrange("s p c -> p s c"),
                      sem, waits=[(sem, sem.n)] + list(extra_waits))
            wlcast_q[q][g] = (sem, cnt)

        def h2pair(dtp, tc):
            return h2T[:, :].rearrange("p (dt c) -> p dt c", dt=DT)[
                :, 2 * dtp:2 * dtp + 2, tc * 512:(tc + 1) * 512]

        def wgpair(buf, dtp):
            return buf[:, :].rearrange("p (dt e) -> p dt e", dt=DT)[
                :, 2 * dtp:2 * dtp + 2, :]

        gbrc = dma("sync", gbrow9[0:1, :], g_hbm[1], dC, waits=[(dC, dC.n)])
        gbB_done = 0
        for i in range(4):
            td = pr.op("tensor", lambda e, i=i: e.matmul(
                ps_gs[i][:, :], ones_b[0:1, :],
                gbrow9[0:1, i * 512:(i + 1) * 512],
                start=True, stop=True), waits=[(dC, gbrc)], inc=sT)
            gbB_done = pr.op("vector", lambda e, i=i: e.tensor_copy(
                out=gbB[:, i * 512:(i + 1) * 512], in_=ps_gs[i][:, :]),
                waits=[(sT, td)], inc=sV)

        g0cnt[0] = dma("gpsimd", wg0f8[0][:, :].rearrange("p (dt e) -> p dt e", dt=DT),
                       wg_d[0][0].rearrange("dt p e -> p dt e"), dJ)
        g1cnt[0] = dma("gpsimd", wg1f8[0][:, :].rearrange("p (dt e) -> p dt e", dt=DT),
                       wg_d[0][1].rearrange("dt p e -> p dt e"), dK,
                       waits=[(dJ, g0cnt[0])])
        for s in range(NS):
            par = s % 2
            if s % 2 == 1:
                idx = (s - 1) // 2
                wl_half_dma(idx // 16, (idx % 16) // 2, idx % 2, [])
            if s + 1 < NS:
                g0cnt[1 - par] = dma("gpsimd",
                                     wg0f8[1 - par][:, :].rearrange(
                                         "p (dt e) -> p dt e", dt=DT),
                                     wg_d[s + 1][0].rearrange("dt p e -> p dt e"),
                                     dJ, waits=[(dJ, dJ.n), (sT, gmm)])
            for dtp in range(NDP):
                for tc in range(4):
                    gmm = pr.op("tensor", lambda e, tc=tc, dtp=dtp, par=par:
                                e.matmul(ps_gs[tc][:, :], wgpair(wg0f8[par], dtp),
                                         h2pair(dtp, tc),
                                         start=(dtp == 0), stop=(dtp == NDP - 1),
                                         perf_mode=mybir.MatmulPerfMode.DoubleRow),
                                waits=(([(wcsem[0], g0cnt[par])] +
                                        [(sS, g) for g in gel_done] +
                                        ([(sV, gbB_done)] if s == 0 else []))
                                       if (dtp == 0 and tc == 0) else []),
                                inc=(sT if (dtp == NDP - 1 and tc == 3) else None))
            for tc in range(4):
                gel_done[tc] = pr.op("scalar", lambda e, tc=tc: e.activation(
                    out=gelbs[tc][:, :], in_=ps_gs[tc][:, :],
                    func=AF.Gelu_apprx_tanh),
                    waits=[(sT, gmm), (sV, act_done[tc])], inc=sS)
            if s + 1 < NS:
                g1cnt[1 - par] = dma("gpsimd",
                                     wg1f8[1 - par][:, :].rearrange(
                                         "p (dt e) -> p dt e", dt=DT),
                                     wg_d[s + 1][1].rearrange("dt p e -> p dt e"),
                                     dK, waits=[(dK, dK.n), (sT, fmm)])
            for dtp in range(NDP):
                for tc in range(4):
                    fmm = pr.op("tensor", lambda e, tc=tc, dtp=dtp, par=par:
                                e.matmul(ps_fs[tc][:, :], wgpair(wg1f8[par], dtp),
                                         h2pair(dtp, tc),
                                         start=(dtp == 0), stop=(dtp == NDP - 1),
                                         perf_mode=mybir.MatmulPerfMode.DoubleRow),
                                waits=(([(wcsem[1], g1cnt[par])] +
                                        [(sV, a) for a in act_done])
                                       if (dtp == 0 and tc == 0) else []),
                                inc=(sT if (dtp == NDP - 1 and tc == 3) else None))
            half = s % 2
            for tc in range(4):
                act_done[tc] = pr.op("vector", lambda e, tc=tc, half=half:
                                     e.tensor_tensor(
                                         out=actbs[tc][:, :].rearrange(
                                             "p (tt two c) -> p tt two c",
                                             tt=4, two=2)[:, :, half, :],
                                         in0=gelbs[tc][:, :].rearrange(
                                             "p (tt c) -> p tt c", tt=4),
                                         in1=ps_fs[tc][:, :].rearrange(
                                             "p (tt c) -> p tt c", tt=4),
                                         op=ALU.mult),
                                     waits=[(sS, gel_done[tc]), (sT, fmm)] +
                                           ([(actsem[tc], actw[tc])] if half == 0
                                            else []), inc=sV)
                if half == 1:
                    actw[tc] = dma("gpsimd", act_hbm[tc][s // 2], actbs[tc][:, :],
                                   actsem[tc], waits=[(sV, act_done[tc])])
            if half == 1 and (s // 2 + 1) % 8 == 0:
                ck = (s // 2 + 1) // 8 - 1
                for ts_ in (0, 1):
                    preload_acnts[ts_][ck] = act_chunk_dma(
                        ts_, ts_, ck, [(actsem[ts_], actw[ts_])])
        BARRIER()
        pr.flush()
    _sc7.close()

    _sc8 = ExitStack()
    x1q0 = _sc8.enter_context(nc.sbuf_tensor("x1q0", [P, QW], F32, side="right"))
    x1q1 = _sc8.enter_context(nc.sbuf_tensor("x1q1", [P, QW], F32, side="right"))
    outq0 = _sc8.enter_context(nc.sbuf_tensor("outq0", [P, QW], F32, side="right"))
    outq1 = _sc8.enter_context(nc.sbuf_tensor("outq1", [P, QW], F32, side="right"))
    ps_y0 = _sc8.enter_context(nc.psum_tensor("ps_y0", [P, 512], F32))
    ps_y1 = _sc8.enter_context(nc.psum_tensor("ps_y1", [P, 512], F32))
    ps_y2 = _sc8.enter_context(nc.psum_tensor("ps_y2", [P, 512], F32))
    ps_y3 = _sc8.enter_context(nc.psum_tensor("ps_y3", [P, 512], F32))
    ps_y4 = _sc8.enter_context(nc.psum_tensor("ps_y4", [P, 512], F32))
    ps_y5 = _sc8.enter_context(nc.psum_tensor("ps_y5", [P, 512], F32))
    ps_y6 = _sc8.enter_context(nc.psum_tensor("ps_y6", [P, 512], F32))
    ps_y7 = _sc8.enter_context(nc.psum_tensor("ps_y7", [P, 512], F32))
    if True:
        ps_y_sets = ((ps_y0, ps_y1, ps_y2, ps_y3), (ps_y4, ps_y5, ps_y6, ps_y7))
        x1qsem = (dL, dM)
        outsem = (dOut, dOut2)
        evq = [0, 0]
        vq_done = [[0, 0, 0, 0], [0, 0, 0, 0]]
        x1cnt = [0, 0]
        outw = [0, 0]
        gts = 0
        SPC = NS // 4
        NSP = NS // 2
        NGTS = NQ * TSUP
        mm_end_of = [0] * NGTS
        q_end_mm = [0] * NQ
        buf_acnts = [preload_acnts[0], preload_acnts[1]]

        def acpair(bp, sp, tt_):
            base = bp * NS * QW + sp * 2 * QW + tt_ * 256
            return act2[:, base:base + 256].rearrange("p (two c) -> p two c", two=2)

        def wlpair(qp, sp):
            return wl2[:, qp * NS * QW:(qp + 1) * NS * QW].rearrange(
                "p (s c) -> p s c", s=NS)[:, 2 * sp:2 * sp + 2, :]

        for q in range(NQ):
            if 1 <= q < NQ - 1:
                for g in range(8):
                    wl_group_dma(q + 1, g, [(sT, q_end_mm[q - 1])])
            for ts in range(TSUP):
                bp = gts % 2
                ps_y = ps_y_sets[bp]
                if gts >= 1 and gts + 1 < NGTS:
                    nb = (gts + 1) % 2
                    for ck in range(4):
                        buf_acnts[nb][ck] = act_chunk_dma(
                            nb, (gts + 1) % TSUP, ck, [(sT, mm_end_of[gts - 1])])
                for tt_ in range(2):
                    t = ts * 4 + tt_
                    tp = tt_ % 2
                    w = [(sV, evq[tp])]
                    if q == 0 and ts == 0:
                        w += [(dF, dF.n), (dG, dG.n)]
                    x1cnt[tp] = dma("sync",
                                    (x1q0 if tp == 0 else x1q1)[:, :],
                                    x1_hbm[t][:, q * QW:(q + 1) * QW],
                                    x1qsem[tp], waits=w)
                mmd = 0
                for sp in range(NSP):
                    ck = (2 * sp) // SPC
                    for tt_ in range(4):
                        w = []
                        if (2 * sp) % SPC == 0 and tt_ == 0:
                            w += [buf_acnts[bp][ck]]
                        if (2 * sp) % 8 == 0 and tt_ == 0 and ts == 0:
                            w += [wlcast_q[q][(2 * sp) // 8]]
                        if sp == 0:
                            w += [(sV, vq_done[bp][tt_])]
                        do_inc = (tt_ == 3 and (sp % 4 == 3))
                        mmd = pr.op("tensor",
                                    lambda e, bp=bp, qp=q % 2, sp=sp, tt_=tt_,
                                    py=ps_y[tt_]:
                                    e.matmul(py[:, :], acpair(bp, sp, tt_),
                                             wlpair(qp, sp),
                                             start=(sp == 0), stop=(sp == NSP - 1),
                                             perf_mode=mybir.MatmulPerfMode.DoubleRow),
                                    waits=w,
                                    inc=(sT if do_inc else None))
                mm_end_of[gts] = mmd
                for tt_ in range(4):
                    t = ts * 4 + tt_
                    tp = tt_ % 2
                    x1q = x1q0 if tp == 0 else x1q1
                    outq = outq0 if tp == 0 else outq1
                    vq = pr.op("vector", lambda e, q=q, outq=outq, py=ps_y[tt_]:
                               e.tensor_tensor(
                                   out=outq[:, :], in0=py[:, :],
                                   in1=gbB[:, q * QW:(q + 1) * QW], op=ALU.mult),
                               waits=[(sT, mmd), (sV, gbB_done),
                                      (x1qsem[tp], x1cnt[tp]),
                                      (outsem[tp], outw[tp])], inc=sV)
                    vq_done[bp][tt_] = vq
                    evq[tp] = pr.op("vector", lambda e, outq=outq, x1q=x1q:
                                    e.tensor_tensor(out=outq[:, :], in0=outq[:, :],
                                                    in1=x1q[:, :], op=ALU.add),
                                    waits=[(sV, vq)], inc=sV)
                    outw[tp] = dma("scalar", y_d[t][:, q * QW:(q + 1) * QW],
                                   outq[:, :], outsem[tp], waits=[(sV, evq[tp])])
                    if tt_ == 1:
                        for tt2 in (2, 3):
                            t2 = ts * 4 + tt2
                            tp2 = tt2 % 2
                            x1cnt[tp2] = dma("sync",
                                             (x1q0 if tp2 == 0 else x1q1)[:, :],
                                             x1_hbm[t2][:, q * QW:(q + 1) * QW],
                                             x1qsem[tp2], waits=[(sV, evq[tp2])])
                gts += 1
            q_end_mm[q] = mmd
        pr.op("gpsimd", lambda e: e.engine_nop(),
              waits=[(dOut, dOut.n), (dOut2, dOut2.n)])
        pr.flush()
    _sc8.close()

    big2.close()
    es.close()
    return nc, dbg


_BUILT = None


def _host_tiles(inputs):
    f32 = np.float32
    Wq = np.asarray(inputs["L0_attn_Wq"], f32)
    Wk = np.asarray(inputs["L0_attn_Wk"], f32)
    Wv = np.asarray(inputs["L0_attn_Wv"], f32)
    wqkv = np.ascontiguousarray(np.stack([
        Wq.reshape(DT, P, H, E).transpose(2, 0, 1, 3),
        Wk.reshape(DT, P, H, E).transpose(2, 0, 1, 3),
        Wv.reshape(DT, P, H, E).transpose(2, 0, 1, 3),
    ], axis=1))
    wmod = np.ascontiguousarray(
        np.asarray(inputs["L0_wmod"], f32).reshape(DT, P, NMS, MSW).transpose(2, 0, 1, 3))
    ccol = np.ascontiguousarray(
        np.asarray(inputs["L0_c"], f32).reshape(8, DT, P).transpose(0, 2, 1))
    wf = np.ascontiguousarray(np.asarray(inputs["L0_attn_Wf"], f32).reshape(DT, P, H))
    wo = np.ascontiguousarray(np.asarray(inputs["L0_attn_Wo"], f32).reshape(DT, P, D))
    nwcol_ = np.ascontiguousarray(
        np.asarray(inputs["L0_attn_norm_w"], f32).reshape(DT, P).T)
    wg_in = np.asarray(inputs["L0_ffn_w_gating"], f32)
    wg = np.ascontiguousarray(
        np.stack([wg_in[0].reshape(DT, P, NS, E).transpose(2, 0, 1, 3),
                  wg_in[1].reshape(DT, P, NS, E).transpose(2, 0, 1, 3)],
                 axis=1))
    wl = np.ascontiguousarray(
        np.asarray(inputs["L0_ffn_w_linear"], f32).reshape(NS, P, NQ, QW).transpose(2, 0, 1, 3))
    return wqkv, wmod, ccol, wf, wo, nwcol_, wg, wl


def kernel(**inputs):
    global _BUILT
    if _BUILT is None:
        _BUILT = build(debug=False)
    nc, _ = _BUILT
    wqkv, wmod, ccol, wf, wo, nwcol_, wg, wl = _host_tiles(inputs)
    x = np.ascontiguousarray(np.asarray(inputs["L0_attn_x"], np.float32))
    in_maps = []
    for b in range(8):
        in_maps.append({
            "x": x[b].reshape(TT, P, D),
            "wmod": wmod, "ccol": np.ascontiguousarray(ccol[b]),
            "wqkv": wqkv, "wf": wf, "wo": wo, "nwcol": nwcol_,
            "wg": wg, "wl": wl,
        })
    res = run_bass_kernel_spmd(nc, in_maps, core_ids=list(range(8)))
    out = np.stack([r["y"].reshape(D, D) for r in res.results])
    return out


# version 22 (final)
# speedup vs baseline: 1.2938x; 1.0126x over previous
import numpy as np
from contextlib import ExitStack

import concourse.bass as bass
import concourse.mybir as mybir
from concourse.bass_utils import run_bass_kernel_spmd

F32 = mybir.dt.float32
F32R = mybir.dt.float32r
BF16 = mybir.dt.bfloat16
F8 = mybir.dt.float8e4
AF = mybir.ActivationFunctionType
ALU = mybir.AluOpType
AX = mybir.AxisListType

P = 128
D = 2048
TT = 16
DT = 16
H = 16
E = 128
NCH = 16
MS = 12288
NMS = 48
MSW = 256
HID = 8192
NS = 64
NQ = 4
QW = 512
TSUP = 4
LNEPS = 1e-5
RMSEPS = 1e-6


class Sem:
    def __init__(self, nc, es, name):
        self.h = es.enter_context(nc.semaphore(name))
        self.n = 0
        self.name = name


class Prog:
    ENGS = ("tensor", "vector", "scalar", "sync", "gpsimd")

    def __init__(self, nc):
        self.nc = nc
        self.q = {e: [] for e in self.ENGS}
        self.last_wait = {}

    def _wait(self, eng, sem, val):
        if val <= 0:
            return
        key = (eng, sem.name)
        if self.last_wait.get(key, 0) >= val:
            return
        self.last_wait[key] = val
        self.q[eng].append(lambda e, h=sem.h, v=val: e.wait_ge(h, v))

    def op(self, eng, fn, waits=(), inc=None, inc_k=1):
        for sem, val in waits:
            self._wait(eng, sem, val)
        if inc is not None:
            self.q[eng].append(lambda e, fn=fn, h=inc.h, k=inc_k: fn(e).then_inc(h, k))
            inc.n += inc_k
            return inc.n
        self.q[eng].append(fn)
        return None

    def barrier(self, sems):
        for eng in self.ENGS:
            for s in sems:
                self._wait(eng, s, s.n)

    def flush(self):
        with self.nc.Block() as block:
            for eng in self.ENGS:
                ops = self.q[eng]
                if ops:
                    getattr(block, eng)(lambda e, ops=ops: [f(e) for f in ops])
        self.q = {e: [] for e in self.ENGS}


def build(debug=False):
    nc = bass.Bass("TRN2", target_bir_lowering=False)
    import ml_dtypes

    x_d = nc.dram_tensor("x", [TT, P, D], F32, kind="ExternalInput").ap()
    wmod_d = nc.dram_tensor("wmod", [NMS, DT, P, MSW], F32, kind="ExternalInput").ap()
    ccol_d = nc.dram_tensor("ccol", [P, DT], F32, kind="ExternalInput").ap()
    wqkv_d = nc.dram_tensor("wqkv", [H, 3, DT, P, E], F32, kind="ExternalInput").ap()
    wf_d = nc.dram_tensor("wf", [DT, P, H], F32, kind="ExternalInput").ap()
    wo_d = nc.dram_tensor("wo", [DT, P, D], F32, kind="ExternalInput").ap()
    nw_d = nc.dram_tensor("nwcol", [P, DT], F32, kind="ExternalInput").ap()
    wg_d = nc.dram_tensor("wg", [NS, 2, DT, P, E], F32, kind="ExternalInput").ap()
    wl_d = nc.dram_tensor("wl", [NQ, NS, P, QW], F32, kind="ExternalInput").ap()
    y_d = nc.dram_tensor("y", [TT, P, D], F32, kind="ExternalOutput").ap()

    x1_hbm = nc.dram_tensor("x1_scratch", [TT, P, D], F32, kind="Internal").ap()
    g_hbm = nc.dram_tensor("g_scratch", [2, 1, D], BF16, kind="Internal").ap()
    act_hbm = nc.dram_tensor("act_scratch", [TSUP, NS // 2, P, 2 * QW], F8, kind="Internal").ap()

    eye_bf_d = nc.inline_tensor(np.eye(P, dtype=ml_dtypes.bfloat16), "eye_bf_c")
    eye16_d = nc.inline_tensor(np.eye(16, dtype=np.float32), "eye16_c")
    tri_np = np.triu(np.ones((P, P), dtype=np.float32))
    tri_d = nc.inline_tensor(tri_np, "tri_c")
    mask_d = nc.inline_tensor(tri_np.astype(ml_dtypes.bfloat16), "mask_c")
    ones_f_d = nc.inline_tensor(np.ones((1, P), dtype=np.float32), "ones_f_c")
    ones_b_d = nc.inline_tensor(np.ones((1, P), dtype=ml_dtypes.bfloat16), "ones_b_c")
    onescol_d = nc.inline_tensor(np.ones((P, 1), dtype=np.float32), "ones_col_c")
    onescol_b_d = nc.inline_tensor(np.ones((P, 1), dtype=ml_dtypes.bfloat16), "ones_colb_c")
    constvals = [LNEPS, RMSEPS]
    constv_d = nc.inline_tensor(
        np.broadcast_to(np.asarray(constvals, np.float32), (P, len(constvals))).copy(),
        "constv_c")

    dbg = {}
    es = ExitStack()

    def sbl(name, shape, dt):
        return es.enter_context(nc.sbuf_tensor(name, shape, dt, side="left"))

    eye_bf = sbl("eye_bf", [P, P], BF16)
    eye16 = sbl("eye16", [16, 16], F32)
    tri = sbl("tri", [P, P], F32)
    maskb = sbl("maskb", [P, P], BF16)
    ones_f = sbl("ones_f", [1, P], F32)
    ones_b = sbl("ones_b", [1, P], BF16)
    sacol = sbl("sacol", [P, DT], F32)
    cacol = sbl("cacol", [P, DT], F32)
    sbcol = sbl("sbcol", [P, DT], F32)
    cbcol = sbl("cbcol", [P, DT], F32)
    mean1 = sbl("mean1", [P, TT], F32)
    rstd1 = sbl("rstd1", [P, TT], F32)
    mean2 = sbl("mean2", [P, TT], F32)
    rstd2 = sbl("rstd2", [P, TT], F32)
    msum = sbl("msum", [P, TT], F32)
    sqsum = sbl("sqsum", [P, TT], F32)
    rmsr = sbl("rmsr", [P, TT], F32)

    e1t = sbl("e1t", [P, NCH * H], F32)
    e2t = sbl("e2t", [P, NCH * H], F32)
    e3t = sbl("e3t", [P, (NCH - 1) * H], F32)
    e4bc = sbl("e4bc", [P, (NCH - 1) * H], F32)

    Sbf = sbl("Sbf", [P, E], BF16)
    tmp1 = sbl("tmp1", [P, 2 * TT], F32)
    nwcol = sbl("nwcol_sb", [P, DT], F32)

    constv = sbl("constv", [P, 2], F32)
    onescol = sbl("onescol", [P, 1], F32)
    onescol_b = sbl("onescol_b", [P, 1], BF16)
    totrow = sbl("totrow", [1, NCH * H], F32)
    nc.const_aps.register(F32, LNEPS, constv[:, 0:1])
    nc.const_aps.register(F32, RMSEPS, constv[:, 1:2])

    sT = Sem(nc, es, "sT")
    sV = Sem(nc, es, "sV")
    sS = Sem(nc, es, "sS")
    dA = Sem(nc, es, "dA")
    dB = Sem(nc, es, "dB")
    dC = Sem(nc, es, "dC")
    dE = Sem(nc, es, "dE")
    dF = Sem(nc, es, "dF")
    dG = Sem(nc, es, "dG")
    dOut = Sem(nc, es, "dOut")
    dOut2 = Sem(nc, es, "dOut2")
    dH = Sem(nc, es, "dH")
    dI = Sem(nc, es, "dI")
    dJ = Sem(nc, es, "dJ")
    dK = Sem(nc, es, "dK")
    dL = Sem(nc, es, "dL")
    dM = Sem(nc, es, "dM")
    dN = Sem(nc, es, "dN")
    dO = Sem(nc, es, "dO")
    dP = Sem(nc, es, "dP")
    dQ = Sem(nc, es, "dQ")
    ALLSEMS = [sT, sV, sS, dA, dB, dC, dE, dF, dG, dOut, dOut2, dH, dI, dJ, dK,
               dL, dM, dN, dO, dP, dQ]

    pr = Prog(nc)

    def dma(qeng, dst, src, sem, waits=()):
        return pr.op(qeng, lambda e, d=dst, s=src: e.dma_start(out=d, in_=s),
                     waits=waits, inc=sem, inc_k=16)

    def BARRIER():
        pr.barrier(ALLSEMS)

    big1 = ExitStack()
    h1T = big1.enter_context(nc.sbuf_tensor("h1T", [P, DT * D], BF16, side="left"))
    oT = big1.enter_context(nc.sbuf_tensor("oT", [P, H * D], BF16, side="left"))

    _sc1 = ExitStack()
    wmbufs = [_sc1.enter_context(nc.sbuf_tensor(f"wmbuf{i}", [P, DT * MSW], BF16,
                                                side="right")) for i in range(4)]
    ccol = _sc1.enter_context(nc.sbuf_tensor("ccol_sb", [P, DT], F32, side="right"))
    silc = _sc1.enter_context(nc.sbuf_tensor("silc", [P, DT], F32, side="right"))
    silcb = _sc1.enter_context(nc.sbuf_tensor("silcb", [P, DT], BF16, side="right"))
    rowtA = _sc1.enter_context(nc.sbuf_tensor("rowtA", [1, MSW], F32, side="right"))
    rowtB = _sc1.enter_context(nc.sbuf_tensor("rowtB", [1, MSW], F32, side="right"))
    rowsgn = _sc1.enter_context(nc.sbuf_tensor("rowsgn", [1, MSW], F32, side="right"))
    rowabs = _sc1.enter_context(nc.sbuf_tensor("rowabs", [1, MSW], F32, side="right"))
    xrA = _sc1.enter_context(nc.sbuf_tensor("xrA", [P, D], F32, side="right"))
    xrB = _sc1.enter_context(nc.sbuf_tensor("xrB", [P, D], F32, side="right"))
    garow = _sc1.enter_context(nc.sbuf_tensor("garow", [1, D], BF16, side="right"))
    gbrow = _sc1.enter_context(nc.sbuf_tensor("gbrow", [1, D], BF16, side="right"))
    lnxb0 = _sc1.enter_context(nc.sbuf_tensor("lnxb0", [P, D], BF16, side="right"))
    lnxb1 = _sc1.enter_context(nc.sbuf_tensor("lnxb1", [P, D], BF16, side="right"))
    ps_m0 = _sc1.enter_context(nc.psum_tensor("ps_m0", [P, 512], F32))
    ps_m1 = _sc1.enter_context(nc.psum_tensor("ps_m1", [P, 512], F32))
    ps_col = _sc1.enter_context(nc.psum_tensor("ps_col", [P, 512], F32))
    ps_t0 = _sc1.enter_context(nc.psum_tensor("ps_t0", [P, 1024], BF16))
    ps_t1 = _sc1.enter_context(nc.psum_tensor("ps_t1", [P, 1024], BF16))
    ps_t2 = _sc1.enter_context(nc.psum_tensor("ps_t2", [P, 1024], BF16))
    ps_t3 = _sc1.enter_context(nc.psum_tensor("ps_t3", [P, 1024], BF16))
    if True:
        for dst, srcd in ((eye_bf, eye_bf_d), (eye16, eye16_d), (tri, tri_d),
                          (maskb, mask_d), (ones_f, ones_f_d), (ones_b, ones_b_d)):
            dma("sync", dst[:, :], srcd.ap(), dC)
        dma("sync", constv[:, :], constv_d.ap(), dC)
        dma("sync", onescol[:, :], onescol_d.ap(), dC)
        dma("sync", onescol_b[:, :], onescol_b_d.ap(), dC)
        dma("sync", ccol[:, :], ccol_d, dC)
        dma("sync", nwcol[:, :], nw_d, dC)
        cdone = dC.n

        sil_done = pr.op("scalar", lambda e: e.activation(
            out=silc[:, :], in_=ccol[:, :], func=AF.Silu),
            waits=[(dC, cdone)], inc=sS)
        silb_done = pr.op("vector", lambda e: e.tensor_copy(
            out=silcb[:, :], in_=silc[:, :]), waits=[(sS, sil_done)], inc=sV)

        SEGCOL = {0: (sacol, 0.0), 1: (cacol, 1.0), 3: (sbcol, 0.0), 4: (cbcol, 1.0)}
        mod_state = {
            "mv_done": [0, 0, 0, 0],
            "dma": {},
            "ev": [0, 0],
            "colv": 0,
            "clampv": 0,
            "clamps": 0,
        }

        def emit_mod_dma(ms):
            st = mod_state
            slot = ms % 4
            buf = wmbufs[slot]
            dsem = (dF, dG, dJ, dK)[slot]
            st["dma"][ms] = (dsem, dma(
                "gpsimd", buf[:, :].rearrange("p (dt c) -> p dt c", dt=DT),
                wmod_d[ms].rearrange("dt p c -> p dt c"), dsem,
                waits=[(sT, st["mv_done"][slot]), (dsem, dsem.n)]))

        mod_state["evof"] = {}

        def emit_mod_matvec(ms):
            st = mod_state
            if ms not in st["dma"]:
                emit_mod_dma(ms)
            slot = ms % 4
            par = ms % 2
            buf = wmbufs[slot]
            dsem, cnt = st["dma"][ms]
            rowt = rowtA if par == 0 else rowtB
            psm = ps_m0 if par == 0 else ps_m1
            md = None
            for dt_ in range(DT):
                md = pr.op("tensor",
                           lambda e, dt_=dt_, buf=buf, psm=psm: e.matmul(
                               psm[0:1, 0:MSW], silcb[:, dt_:dt_ + 1],
                               buf[:, dt_ * MSW:(dt_ + 1) * MSW],
                               start=(dt_ == 0), stop=(dt_ == DT - 1)),
                           waits=([(dsem, cnt), (sV, silb_done),
                                   (sV, st["ev"][par])] if dt_ == 0 else []),
                           inc=(sT if dt_ == DT - 1 else None))
            st["mv_done"][slot] = md
            ev = pr.op("vector", lambda e, psm=psm, rowt=rowt: e.tensor_copy(
                out=rowt[0:1, :], in_=psm[0:1, 0:MSW]),
                waits=[(sT, md), (sV, st["clampv"]), (sV, st["colv"]),
                       (sS, st["clamps"])], inc=sV)
            st["ev"][par] = ev
            st["evof"][ms] = ev

        def emit_mod_post(ms):
            st = mod_state
            par = ms % 2
            rowt = rowtA if par == 0 else rowtB
            ev = st["evof"][ms]
            seg = ms // 8
            if seg in SEGCOL:
                dstcol, addone = SEGCOL[seg]
                cm = None
                for k in range(2):
                    dtile = (ms % 8) * 2 + k
                    cm = pr.op("tensor", lambda e, k=k, dtile=dtile, rowt=rowt:
                               e.matmul(ps_col[:, dtile:dtile + 1],
                                        rowt[0:1, k * P:(k + 1) * P],
                                        ones_f[0:1, 0:1], start=True, stop=True),
                               waits=[(sV, ev), (sV, st["colv"])],
                               inc=(sT if k == 1 else None))
                d0 = (ms % 8) * 2
                st["colv"] = pr.op("vector",
                                   lambda e, d0=d0, dstcol=dstcol, addone=addone:
                                   e.tensor_scalar(
                                       out=dstcol[:, d0:d0 + 2],
                                       in0=ps_col[:, d0:d0 + 2],
                                       scalar1=float(addone), scalar2=None,
                                       op0=ALU.add),
                                   waits=[(sT, cm)], inc=sV)
            else:
                target = garow if seg == 2 else gbrow
                off = (ms % 8) * MSW
                s1_ = pr.op("scalar", lambda e, rowt=rowt: e.activation(
                    out=rowsgn[0:1, :], in_=rowt[0:1, :], func=AF.Sign),
                    waits=[(sV, ev), (sV, st["clampv"])], inc=sS)
                s2_ = pr.op("scalar", lambda e, rowt=rowt: e.activation(
                    out=rowabs[0:1, :], in_=rowt[0:1, :], func=AF.Abs),
                    waits=[(sS, s1_)], inc=sS)
                st["clamps"] = s2_
                v1_ = pr.op("vector", lambda e: e.tensor_scalar_max(
                    out=rowabs[0:1, :], in0=rowabs[0:1, :], scalar1=0.25),
                    waits=[(sS, s2_)], inc=sV)
                st["clampv"] = pr.op("vector",
                                     lambda e, target=target, off=off:
                                     e.tensor_tensor(
                                         out=target[0:1, off:off + MSW],
                                         in0=rowsgn[0:1, :], in1=rowabs[0:1, :],
                                         op=ALU.mult),
                                     waits=[(sV, v1_)], inc=sV)

        xrP = (xrA, xrB)
        xsemP = (dC, dE)
        vdoneP = [0, 0]
        sdoneP = [0, 0]
        for t in range(TT):
            par = t % 2
            w = ([(sV, vdoneP[par]), (sS, sdoneP[par])] if t >= 2
                 else [(dC, cdone)])
            cnt = dma("sync", xrP[par][:, :], x_d[t], xsemP[par], waits=w)
            vdoneP[par] = pr.op("vector", lambda e, par=par, t=t: e.tensor_reduce(
                out=msum[:, t:t + 1], in_=xrP[par][:, :], axis=AX.X, op=ALU.add),
                waits=[(xsemP[par], cnt)], inc=sV)
            sdoneP[par] = pr.op("scalar", lambda e, par=par, t=t: e.activation(
                out=lnxb0[:, :], in_=xrP[par][:, :], func=AF.Square,
                accum_out=sqsum[:, t:t + 1]),
                waits=[(xsemP[par], cnt), (sS, max(sdoneP))], inc=sS)

        for ms in range(24):
            emit_mod_matvec(ms)
            if ms > 0:
                emit_mod_post(ms - 1)
        emit_mod_post(23)
        saca_done = mod_state["colv"]

        mean_done = pr.op("vector", lambda e: e.tensor_scalar(
            out=mean1[:, :], in0=msum[:, :], scalar1=1.0 / D, scalar2=None,
            op0=ALU.mult), waits=[(sV, max(vdoneP)), (sS, max(sdoneP))], inc=sV)
        c1 = pr.op("vector", lambda e: e.tensor_scalar(
            out=tmp1[:, 0:TT], in0=sqsum[:, :], scalar1=1.0 / D, scalar2=None,
            op0=ALU.mult), waits=[(sV, mean_done)], inc=sV)
        c2 = pr.op("vector", lambda e: e.tensor_tensor(
            out=tmp1[:, 16:32], in0=mean1[:, :], in1=mean1[:, :], op=ALU.mult),
            waits=[(sV, c1)], inc=sV)
        vstat = pr.op("vector", lambda e: e.tensor_tensor(
            out=tmp1[:, 0:TT], in0=tmp1[:, 0:TT], in1=tmp1[:, 16:32],
            op=ALU.subtract), waits=[(sV, c2)], inc=sV)
        ln1d = pr.op("scalar", lambda e: e.activation(
            out=tmp1[:, 0:TT], in_=tmp1[:, 0:TT], func=AF.Ln, bias=LNEPS,
            scale=1.0), waits=[(sV, vstat)], inc=sS)
        rstd_done = pr.op("scalar", lambda e: e.activation(
            out=rstd1[:, :], in_=tmp1[:, 0:TT], func=AF.Exp, scale=-0.5),
            waits=[(sS, ln1d)], inc=sS)

        xr = (xrA, xrB)
        xsem = (dC, dE)
        vdone = vdoneP
        ms_next = 24
        ms_dma = 24
        tile_tr_done = [0, 0]
        ev_done = [(sS, 0), (sS, 0)]
        pstx = (ps_t0, ps_t1, ps_t2, ps_t3)
        ev_done = [(sS, 0)] * 4
        for t in range(TT):
            par = t % 2
            lnxb = lnxb0 if par == 0 else lnxb1
            w = [(sV, vdone[par]), (sS, max(sdoneP)), (sS, rstd_done)]
            cnt = dma("sync", xr[par][:, :], x_d[t], xsem[par], waits=w)
            vh = [0, 0]
            for half in range(2):
                vh[half] = pr.op("vector", lambda e, par=par, t=t, lnxb=lnxb,
                                 half=half: e.tensor_scalar(
                                     out=lnxb[:, half * (D // 2):
                                              (half + 1) * (D // 2)],
                                     in0=xr[par][:, half * (D // 2):
                                                 (half + 1) * (D // 2)],
                                     scalar1=mean1[:, t:t + 1],
                                     scalar2=rstd1[:, t:t + 1],
                                     op0=ALU.subtract, op1=ALU.mult),
                                 waits=[(xsem[par], cnt), (sS, rstd_done),
                                        (sV, mean_done),
                                        (sT, tile_tr_done[par])],
                                 inc=sV)
            vdone[par] = vh[1]
            for dt_ in range(DT):
                ppar = dt_ % 4
                ptr = pstx[ppar]
                td = pr.op("tensor", lambda e, dt_=dt_, ptr=ptr, lnxb=lnxb:
                           e.transpose(ptr[:, 0:P], lnxb[:, dt_ * P:(dt_ + 1) * P],
                                       eye_bf[:, :]),
                           waits=[(sV, vh[0] if dt_ < DT // 2 else vh[1]),
                                  ev_done[ppar]], inc=sT)
                if dt_ % 2 == 0:
                    ev_done[ppar] = (sS, pr.op(
                        "scalar", lambda e, dt_=dt_, t=t, ptr=ptr: e.activation(
                            out=h1T[:, dt_ * D + t * P: dt_ * D + (t + 1) * P],
                            in_=ptr[:, 0:P], func=AF.Identity,
                            scale=cacol[:, dt_:dt_ + 1],
                            bias=sacol[:, dt_:dt_ + 1]),
                        waits=[(sT, td), (sV, saca_done)], inc=sS))
                else:
                    ev_done[ppar] = (sV, pr.op(
                        "vector", lambda e, dt_=dt_, t=t, ptr=ptr: e.tensor_scalar(
                            out=h1T[:, dt_ * D + t * P: dt_ * D + (t + 1) * P],
                            in0=ptr[:, 0:P],
                            scalar1=cacol[:, dt_:dt_ + 1],
                            scalar2=sacol[:, dt_:dt_ + 1],
                            op0=ALU.mult, op1=ALU.add),
                        waits=[(sT, td), (sV, saca_done)], inc=sV))
            tile_tr_done[par] = sT.n
            while ms_dma < NMS and ms_dma < 24 + (t + 1) * 2:
                emit_mod_dma(ms_dma)
                ms_dma += 1
            while ms_next < NMS and ms_next < 24 + t * 2:
                emit_mod_matvec(ms_next)
                if ms_next > 24:
                    emit_mod_post(ms_next - 1)
                ms_next += 1
        while ms_next < NMS:
            emit_mod_matvec(ms_next)
            if ms_next > 24:
                emit_mod_post(ms_next - 1)
            ms_next += 1
        emit_mod_post(NMS - 1)
        dma("gpsimd", g_hbm[0], garow[0:1, :], dOut,
            waits=[(sV, mod_state["clampv"])])
        dma("gpsimd", g_hbm[1], gbrow[0:1, :], dOut,
            waits=[(sV, mod_state["clampv"])])
        BARRIER()
        if debug:
            dbg["h1T"] = nc.dram_tensor("dbg_h1T", [P, DT * D], BF16,
                                        kind="ExternalOutput").ap()
            dma("gpsimd", dbg["h1T"], h1T[:, :], dOut)
            dbg["garow"] = nc.dram_tensor("dbg_garow", [1, D], BF16,
                                          kind="ExternalOutput").ap()
            dbg["sacol"] = nc.dram_tensor("dbg_sacol", [P, DT], F32,
                                          kind="ExternalOutput").ap()
            dbg["cacol"] = nc.dram_tensor("dbg_cacol", [P, DT], F32,
                                          kind="ExternalOutput").ap()
            dma("gpsimd", dbg["garow"], garow[0:1, :], dOut)
            dma("gpsimd", dbg["sacol"], sacol[:, :], dOut)
            dma("gpsimd", dbg["cacol"], cacol[:, :], dOut)
            BARRIER()
        pr.flush()
    _sc1.close()

    _sc3 = ExitStack()
    wfbuf = _sc3.enter_context(nc.sbuf_tensor("wfbuf", [P, DT * H], F32, side="right"))
    wfbf = _sc3.enter_context(nc.sbuf_tensor("wfbf", [P, DT * H], BF16, side="right"))
    ldT = _sc3.enter_context(nc.sbuf_tensor("ldT", [16, D], F32, side="right"))
    cumall = _sc3.enter_context(nc.sbuf_tensor("cumall", [P, NCH * H], F32, side="right"))
    ldtok = _sc3.enter_context(nc.sbuf_tensor("ldtok", [P, NCH * H], F32, side="right"))
    tmp4 = _sc3.enter_context(nc.sbuf_tensor("tmp4", [P, NCH * H], F32, side="right"))
    totbc = _sc3.enter_context(nc.sbuf_tensor("totbc", [P, NCH * H], F32, side="right"))
    ps_l0 = _sc3.enter_context(nc.psum_tensor("ps_l0", [16, 512], F32))
    ps_l1 = _sc3.enter_context(nc.psum_tensor("ps_l1", [16, 512], F32))
    ps_c0 = _sc3.enter_context(nc.psum_tensor("ps_c0", [P, 512], F32))
    ps_c1 = _sc3.enter_context(nc.psum_tensor("ps_c1", [P, 512], F32))
    if True:

        ps_c = (ps_c0, ps_c1)
        cnt = dma("sync", wfbuf[:, :].rearrange("p (dt h) -> p dt h", dt=DT), wf_d.rearrange("dt p h -> p dt h"), dA)
        wfc = pr.op("vector", lambda e: e.tensor_copy(out=wfbf[:, :], in_=wfbuf[:, :]),
                    waits=[(dA, cnt)], inc=sV)
        ld_ev = [0, 0]
        for ch in range(4):
            lp = ch % 2
            pld = ps_l0 if lp == 0 else ps_l1
            md = None
            for dt_ in range(DT):
                md = pr.op("tensor", lambda e, ch=ch, dt_=dt_, pld=pld: e.matmul(
                    pld[:, :], wfbf[:, dt_ * H:(dt_ + 1) * H],
                    h1T[:, dt_ * D + ch * 512: dt_ * D + (ch + 1) * 512],
                    start=(dt_ == 0), stop=(dt_ == DT - 1)),
                    waits=[(sV, wfc), (sS, ld_ev[lp])] if dt_ == 0 else [],
                    inc=(sT if dt_ == DT - 1 else None))
            ld_ev[lp] = pr.op("scalar", lambda e, ch=ch, pld=pld: e.activation(
                out=ldT[:, ch * 512:(ch + 1) * 512], in_=pld[:, :], func=AF.Sigmoid),
                waits=[(sT, md)], inc=sS)
        lds = pr.op("scalar", lambda e: e.activation(
            out=ldT[:, :], in_=ldT[:, :], func=AF.Ln),
            waits=[(sS, max(ld_ev))], inc=sS)

        pev = [0, 0]
        for c in range(NCH):
            cp = c % 2
            td = pr.op("tensor", lambda e, c=c, cp=cp: e.transpose(
                ps_c[cp][0:P, 0:16], ldT[:, c * P:(c + 1) * P], eye16[:, :]),
                waits=[(sS, lds), (sV, pev[cp])], inc=sT)
            pev[cp] = pr.op("vector", lambda e, c=c, cp=cp: e.tensor_copy(
                out=ldtok[:, c * H:(c + 1) * H], in_=ps_c[cp][0:P, 0:16]),
                waits=[(sT, td)], inc=sV)
        ld_all = sV.n
        for c in range(NCH):
            cp = c % 2
            pr.op("tensor", lambda e, c=c, cp=cp: e.matmul(
                ps_c[cp][:, 0:16], tri[:, :], ldtok[:, c * H:(c + 1) * H],
                start=True, stop=True),
                waits=[(sV, ld_all), (sV, pev[cp])])
            td = pr.op("tensor", lambda e, c=c, cp=cp: e.matmul(
                ps_c[cp][0:1, 16:32], onescol[:, :], ldtok[:, c * H:(c + 1) * H],
                start=True, stop=True), inc=sT)
            pr.op("vector", lambda e, c=c, cp=cp: e.tensor_copy(
                out=cumall[:, c * H:(c + 1) * H], in_=ps_c[cp][:, 0:16]),
                waits=[(sT, td)])
            pev[cp] = pr.op("vector", lambda e, c=c, cp=cp: e.tensor_copy(
                out=totrow[0:1, c * H:(c + 1) * H], in_=ps_c[cp][0:1, 16:32]),
                inc=sV)
        cum_done = sV.n

        nbc = NCH * H // P
        for i in range(nbc):
            cp = i % 2
            td = pr.op("tensor", lambda e, i=i, cp=cp: e.matmul(
                ps_c[cp][:, 0:P], ones_f[0:1, :], totrow[0:1, i * P:(i + 1) * P],
                start=True, stop=True), waits=[(sV, cum_done), (sV, pev[cp])], inc=sT)
            pev[cp] = pr.op("vector", lambda e, i=i, cp=cp: e.tensor_copy(
                out=totbc[:, i * P:(i + 1) * P], in_=ps_c[cp][:, 0:P]),
                waits=[(sT, td)], inc=sV)
        bc_ev = sV.n

        vt = pr.op("vector", lambda e: e.scalar_tensor_tensor(
            out=tmp4[:, :], in0=totbc[:, :], scalar=0.5, in1=cumall[:, :],
            op0=ALU.mult, op1=ALU.subtract), waits=[(sV, bc_ev)], inc=sV)
        pr.op("scalar", lambda e: e.activation(out=e1t[:, :], in_=tmp4[:, :],
                                               func=AF.Exp), waits=[(sV, vt)])
        e12 = pr.op("scalar", lambda e: e.activation(out=e2t[:, :], in_=tmp4[:, :],
                                                     func=AF.Exp, scale=-1.0), inc=sS)
        NE = (NCH - 1) * H
        p3 = pr.op("vector", lambda e: e.tensor_tensor(
            out=tmp4[:, 0:NE], in0=totbc[:, 0:NE], in1=cumall[:, 0:NE],
            op=ALU.subtract), waits=[(sS, e12)], inc=sV)
        vt3 = pr.op("vector", lambda e: e.scalar_tensor_tensor(
            out=tmp4[:, 0:NE], in0=totbc[:, H:H + NE], scalar=0.5, in1=tmp4[:, 0:NE],
            op0=ALU.mult, op1=ALU.add), waits=[(sV, p3)], inc=sV)
        e3d = pr.op("scalar", lambda e: e.activation(
            out=e3t[:, :], in_=tmp4[:, 0:NE], func=AF.Exp), waits=[(sV, vt3)], inc=sS)
        vt4 = pr.op("vector", lambda e: e.tensor_tensor(
            out=tmp4[0:1, 0:NE], in0=totrow[0:1, 0:NE],
            in1=totrow[0:1, H:H + NE], op=ALU.add), waits=[(sS, e3d)], inc=sV)
        e4r = pr.op("scalar", lambda e: e.activation(
            out=tmp4[0:1, 0:NE], in_=tmp4[0:1, 0:NE], func=AF.Exp, scale=0.5),
            waits=[(sV, vt4)], inc=sS)
        for i in range((NE + P - 1) // P):
            w_ = min(P, NE - i * P)
            cp = i % 2
            td = pr.op("tensor", lambda e, i=i, w_=w_, cp=cp: e.matmul(
                ps_c[cp][:, 0:w_], ones_f[0:1, :], tmp4[0:1, i * P:i * P + w_],
                start=True, stop=True), waits=[(sS, e4r), (sV, pev[cp])], inc=sT)
            pev[cp] = pr.op("vector", lambda e, i=i, w_=w_, cp=cp: e.tensor_copy(
                out=e4bc[:, i * P:i * P + w_], in_=ps_c[cp][:, 0:w_]),
                waits=[(sT, td)], inc=sV)
        BARRIER()
        if debug:
            for nm, tens, wdt in (("ldtok", ldtok, NCH * H), ("cumall", cumall, NCH * H),
                                  ("e1t", e1t, NCH * H), ("e2t", e2t, NCH * H),
                                  ("e3t", e3t, NE), ("e4bc", e4bc, NE)):
                dbg[nm] = nc.dram_tensor("dbg_" + nm, [P, wdt], F32,
                                         kind="ExternalOutput").ap()
                dma("gpsimd", dbg[nm], tens[:, :], dOut)
            BARRIER()
        pr.flush()
    _sc3.close()

    _sc4 = ExitStack()
    wqbf0 = _sc4.enter_context(nc.sbuf_tensor("wqbf0", [P, D], BF16, side="right"))
    wkbf0 = _sc4.enter_context(nc.sbuf_tensor("wkbf0", [P, D], BF16, side="right"))
    wvbf0 = _sc4.enter_context(nc.sbuf_tensor("wvbf0", [P, D], BF16, side="right"))
    wqbf1 = _sc4.enter_context(nc.sbuf_tensor("wqbf1", [P, D], BF16, side="right"))
    wkbf1 = _sc4.enter_context(nc.sbuf_tensor("wkbf1", [P, D], BF16, side="right"))
    wvbf1 = _sc4.enter_context(nc.sbuf_tensor("wvbf1", [P, D], BF16, side="right"))
    qT0 = _sc4.enter_context(nc.sbuf_tensor("qT0", [P, D], BF16, side="right"))
    qT1 = _sc4.enter_context(nc.sbuf_tensor("qT1", [P, D], BF16, side="right"))
    kT0 = _sc4.enter_context(nc.sbuf_tensor("kT0", [P, D], BF16, side="right"))
    kT1 = _sc4.enter_context(nc.sbuf_tensor("kT1", [P, D], BF16, side="right"))
    vT = _sc4.enter_context(nc.sbuf_tensor("vT", [P, D], BF16, side="right"))
    ktok0 = _sc4.enter_context(nc.sbuf_tensor("ktok0", [P, D], BF16, side="right"))
    ktok1 = _sc4.enter_context(nc.sbuf_tensor("ktok1", [P, D], BF16, side="right"))
    vtok0 = _sc4.enter_context(nc.sbuf_tensor("vtok0", [P, D], BF16, side="right"))
    vtok1 = _sc4.enter_context(nc.sbuf_tensor("vtok1", [P, D], BF16, side="right"))
    s3b = [[_sc4.enter_context(nc.sbuf_tensor(f"s3b{i}{j}", [P, P], BF16, side="right"))
            for j in range(2)] for i in range(2)]
    obb = [[_sc4.enter_context(nc.sbuf_tensor(f"obb{i}{j}", [P, P], BF16, side="right"))
            for j in range(2)] for i in range(2)]
    kdb = [[_sc4.enter_context(nc.sbuf_tensor(f"kdb{i}{j}", [P, P], BF16, side="right"))
            for j in range(2)] for i in range(2)]
    junk5 = _sc4.enter_context(nc.sbuf_tensor("junk5", [P, P], BF16, side="right"))
    Sbf1 = _sc4.enter_context(nc.sbuf_tensor("Sbf1", [P, E], BF16, side="right"))
    ps_p0 = _sc4.enter_context(nc.psum_tensor("ps_p0", [P, 512], F32))
    ps_p1 = _sc4.enter_context(nc.psum_tensor("ps_p1", [P, 512], F32))
    ps_s0 = _sc4.enter_context(nc.psum_tensor("ps_s0", [P, 512], F32))
    ps_s1 = _sc4.enter_context(nc.psum_tensor("ps_s1", [P, 512], F32))
    ps_S0 = _sc4.enter_context(nc.psum_tensor("ps_S0", [P, 512], F32))
    ps_S1 = _sc4.enter_context(nc.psum_tensor("ps_S1", [P, 512], F32))
    ps_x0 = _sc4.enter_context(nc.psum_tensor("ps_x0", [P, 1024], BF16))
    ps_x1 = _sc4.enter_context(nc.psum_tensor("ps_x1", [P, 1024], BF16))
    if True:
        wsem = (dA, dB)
        qTs = (qT0, qT1)
        kTs = (kT0, kT1)
        ktoks = (ktok0, ktok1)
        vtoks = (vtok0, vtok1)
        Sbfs = (Sbf, Sbf1)
        ps_ss = (ps_s0, ps_s1)
        ps_Ss = (ps_S0, ps_S1)
        ps_os = (ps_p0, ps_p1)
        ps_xs = (ps_x0, ps_x1)
        wbfs2 = ({0: wqbf0, 1: wkbf0, 2: wvbf0}, {0: wqbf1, 1: wkbf1, 2: wvbf1})

        bank_free = [(sS, 0), (sS, 0)]
        trx_free = [(sV, 0), (sV, 0)]
        wsem_i = 0
        cast_done = {}
        mat_last_mm = [[0, 0, 0], [0, 0, 0]]

        def weight_chain(h):
            nonlocal wsem_i
            for m in range(3):
                sem = (dF, dG)[wsem_i % 2]
                wsem_i += 1
                cnt = dma("gpsimd",
                          wbfs2[h % 2][m][:, :].rearrange("p (dt e) -> p dt e", dt=DT),
                          wqkv_d[h][m].rearrange("dt p e -> p dt e"), sem,
                          waits=[(sT, mat_last_mm[h % 2][m]), (sem, sem.n)])
                cast_done[(h, m)] = (sem, cnt)

        weight_chain(0)
        weight_chain(1)

        supd = [0, 0]
        m2m3 = [0, 0]
        m4d = [0, 0]
        oev = [[(sS, 0), (sS, 0)], [(sS, 0), (sS, 0)]]
        s1ev = [0, 0]
        kdv = [[0, 0], [0, 0]]
        otr = [[(sV, 0), (sV, 0)], [(sV, 0), (sV, 0)]]
        sq = [0, 0]
        sq_add = 0
        for p in range(H // 2):
            ha, hb = 2 * p, 2 * p + 1

            def proj(h, m, dstT, func):
                lastmm = 0
                for ch in range(4):
                    ppar = ch % 2
                    pp = ps_os[ppar]
                    wbf = wbfs2[h % 2][m]
                    md = None
                    for dt_ in range(DT):
                        md = pr.op("tensor",
                                   lambda e, ch=ch, dt_=dt_, pp=pp, wbf=wbf:
                                   e.matmul(pp[:, :], wbf[:, dt_ * P:(dt_ + 1) * P],
                                            h1T[:, dt_ * D + ch * 512: dt_ * D + (ch + 1) * 512],
                                            start=(dt_ == 0), stop=(dt_ == DT - 1)),
                                   waits=([cast_done[(h, m)], bank_free[ppar]]
                                          if dt_ == 0 else []),
                                   inc=(sT if dt_ == DT - 1 else None))
                    lastmm = md
                    if (ch + m) % 2 == 0:
                        bank_free[ppar] = (sS, pr.op(
                            "scalar", lambda e, ch=ch, pp=pp, dstT=dstT, func=func:
                            e.activation(out=dstT[:, ch * 512:(ch + 1) * 512],
                                         in_=pp[:, :], func=func),
                            waits=[(sT, md)], inc=sS))
                    else:
                        if func == AF.Silu:
                            bank_free[ppar] = (sS, pr.op(
                                "scalar", lambda e, ch=ch, pp=pp, dstT=dstT:
                                e.activation(out=dstT[:, ch * 512:(ch + 1) * 512],
                                             in_=pp[:, :], func=AF.Silu),
                                waits=[(sT, md)], inc=sS))
                        else:
                            bank_free[ppar] = (sV, pr.op(
                                "vector", lambda e, ch=ch, pp=pp, dstT=dstT:
                                e.tensor_copy(out=dstT[:, ch * 512:(ch + 1) * 512],
                                              in_=pp[:, :]),
                                waits=[(sT, md)], inc=sV))
                mat_last_mm[h % 2][m] = lastmm
                return bank_free

            def transp_all(srcT, dst, hx):
                for g in range(NCH // 4):
                    xp = g % 2
                    pt = ps_xs[xp]
                    td = 0
                    for j in range(4):
                        c = g * 4 + j
                        td = pr.op("tensor",
                                   lambda e, c=c, j=j, pt=pt, srcT=srcT: e.transpose(
                                       pt[:, j * P:(j + 1) * P],
                                       srcT[:, c * P:(c + 1) * P], eye_bf[:, :]),
                                   waits=([bank_free[0], bank_free[1], trx_free[xp]]
                                          if j == 0 else []),
                                   inc=(sT if j == 3 else None))
                    trx_free[xp] = (sV, pr.op(
                        "vector", lambda e, g=g, pt=pt, dst=dst: e.tensor_copy(
                            out=dst[:, g * 512:(g + 1) * 512], in_=pt[:, 0:512]),
                        waits=[(sT, td)], inc=sV))

            for h, hi in ((ha, 0), (hb, 1)):
                proj(h, 0, qTs[hi], AF.Silu)
                proj(h, 1, kTs[hi], AF.Silu)
                proj(h, 2, vT, AF.Identity)
                transp_all(kTs[hi], ktoks[hi], hi)
                transp_all(vT, vtoks[hi], hi)

            if p + 1 < H // 2:
                weight_chain(2 * p + 2)
                weight_chain(2 * p + 3)

            kvdone = (trx_free[0], trx_free[1])
            pend_o = [None, None]

            def emit_otd(hi):
                c0, h0, obuf0, oevt = pend_o[hi]
                pt = ps_xs[hi]
                otd = pr.op("tensor", lambda e, obuf0=obuf0, pt=pt: e.transpose(
                    pt[:, 0:P], obuf0[:, :], eye_bf[:, :]),
                    waits=[oevt, trx_free[hi]], inc=sT)
                trx_free[hi] = (sS, pr.op(
                    "scalar", lambda e, c0=c0, h0=h0, pt=pt: e.activation(
                        out=oT[:, h0 * D + c0 * P: h0 * D + (c0 + 1) * P],
                        in_=pt[:, 0:P], func=AF.Identity),
                    waits=[(sT, otd)], inc=sS))
                otr[hi][c0 % 2] = trx_free[hi]
                pend_o[hi] = None

            def emit_m1(c, h, hi, first):
                qT = qTs[hi]
                kT = kTs[hi]
                pss = ps_ss[hi]
                s3buf = s3b[hi][c % 2]
                m1 = pr.op("tensor", lambda e, c=c, kT=kT, qT=qT, pss=pss:
                           e.matmul(pss[:, 0:P], kT[:, c * P:(c + 1) * P],
                                    qT[:, c * P:(c + 1) * P],
                                    start=True, stop=True),
                           waits=[(sV, s1ev[hi])] +
                                 ([kvdone[0], kvdone[1]] if first else []),
                           inc=sT)
                s1ev[hi] = pr.op("vector",
                                 lambda e, col=c * H + h, s3buf=s3buf, pss=pss:
                                 e.scalar_tensor_tensor(
                                     out=s3buf[:, :], in0=pss[:, 0:P],
                                     scalar=e1t[:, col:col + 1],
                                     in1=maskb[:, :],
                                     op0=ALU.mult, op1=ALU.mult),
                                 waits=[(sT, m1), (sT, m2m3[hi])], inc=sV)

            def emit_kdv(c, h, hi, first):
                kdbuf = kdb[hi][c % 2]
                ktok = ktoks[hi]
                kdv[hi][c % 2] = pr.op(
                    "scalar", lambda e, c=c, h=h, kdbuf=kdbuf, ktok=ktok:
                    e.activation(out=kdbuf[:, :],
                                 in_=ktok[:, c * P:(c + 1) * P],
                                 func=AF.Identity,
                                 scale=e3t[:, c * H + h:c * H + h + 1]),
                    waits=[(sT, m4d[hi])] +
                          ([kvdone[0], kvdone[1]] if first else []),
                    inc=sS)

            for hi in range(2):
                supd[hi] = pr.op("vector",
                                 lambda e, hi=hi: e.memset(Sbfs[hi][:, :], 0.0),
                                 waits=[(sT, m2m3[hi])], inc=sV)
            for h, hi in ((ha, 0), (hb, 1)):
                emit_m1(0, h, hi, True)
                emit_kdv(0, h, hi, True)
            for c in range(NCH):
                cp = c % 2
                for h, hi in ((ha, 0), (hb, 1)):
                    col = c * H + h
                    qT = qTs[hi]
                    vtok = vtoks[hi]
                    s3buf = s3b[hi][cp]
                    obuf = obb[hi][cp]
                    kdbuf = kdb[hi][cp]
                    po = ps_os[hi]
                    pS = ps_Ss[hi]

                    m2m3[hi] = pr.op("tensor",
                                     lambda e, c=c, po=po, s3buf=s3buf, vtok=vtok:
                                     e.matmul(po[:, 0:P], s3buf[:, :],
                                              vtok[:, c * P:(c + 1) * P],
                                              start=True, stop=(c == 0)),
                                     waits=[(sV, s1ev[hi]),
                                            (bank_free[hi] if c == 0
                                             else oev[hi][cp ^ 1])],
                                     inc=(sT if c == 0 else None))
                    if c > 0:
                        m2m3[hi] = pr.op("tensor",
                                         lambda e, c=c, po=po, qT=qT, hi=hi:
                                         e.matmul(po[:, 0:P],
                                                  qT[:, c * P:(c + 1) * P],
                                                  Sbfs[hi][:, :],
                                                  start=False, stop=True),
                                         waits=[(sV, supd[hi])], inc=sT)

                    if c < NCH - 1:
                        m4d[hi] = pr.op("tensor",
                                        lambda e, c=c, kdbuf=kdbuf, vtok=vtok, pS=pS:
                                        e.matmul(pS[:, 0:P], kdbuf[:, :],
                                                 vtok[:, c * P:(c + 1) * P],
                                                 start=True, stop=True),
                                        waits=[(sS, kdv[hi][cp]), (sV, supd[hi])],
                                        inc=sT)
                        supd[hi] = pr.op("vector",
                                         lambda e, c=c, h=h, hi=hi, pS=pS:
                                         e.scalar_tensor_tensor(
                                             out=Sbfs[hi][:, :], in0=Sbfs[hi][:, :],
                                             scalar=e4bc[:, c * H + h:c * H + h + 1],
                                             in1=pS[:, 0:P],
                                             op0=ALU.mult, op1=ALU.add),
                                         waits=[(sT, m4d[hi]), (sT, m2m3[hi]),
                                                (sV, supd[hi])], inc=sV)

                    if pend_o[hi] is not None:
                        emit_otd(hi)
                    oev[hi][cp] = (sS, pr.op(
                        "scalar", lambda e, col=col, po=po, obuf=obuf:
                        e.activation(out=obuf[:, :], in_=po[:, 0:P],
                                     func=AF.Identity,
                                     scale=e2t[:, col:col + 1]),
                        waits=[(sT, m2m3[hi]), otr[hi][cp]], inc=sS))
                    pend_o[hi] = (c, h, obuf, oev[hi][cp])
                if c + 1 < NCH:
                    for h, hi in ((ha, 0), (hb, 1)):
                        emit_m1(c + 1, h, hi, False)
                        if c + 1 < NCH - 1:
                            emit_kdv(c + 1, h, hi, False)
            for hi in range(2):
                emit_otd(hi)
            bank_free = [oev[0][(NCH - 1) % 2], oev[1][(NCH - 1) % 2]]
        BARRIER()
        if debug:
            for nm, tens in (("qT", qT1), ("kT", kT1), ("vtok", vtok1),
                             ("ktok", ktok1)):
                dbg[nm] = nc.dram_tensor("dbg_" + nm, [P, D], BF16,
                                         kind="ExternalOutput").ap()
                dma("gpsimd", dbg[nm], tens[:, :], dOut)
            dbg["oT"] = nc.dram_tensor("dbg_oT", [P, H * D], BF16,
                                       kind="ExternalOutput").ap()
            dma("gpsimd", dbg["oT"], oT[:, :], dOut)
            BARRIER()
        pr.flush()
    _sc4.close()

    wo_bf = h1T

    _sc5 = ExitStack()
    wob0 = _sc5.enter_context(nc.sbuf_tensor("wobuf0", [P, D], F32, side="right"))
    wob1 = _sc5.enter_context(nc.sbuf_tensor("wobuf1", [P, D], F32, side="right"))
    xrc0 = _sc5.enter_context(nc.sbuf_tensor("xrc0", [P, D], F32, side="right"))
    xrc1 = _sc5.enter_context(nc.sbuf_tensor("xrc1", [P, D], F32, side="right"))
    x1b0 = _sc5.enter_context(nc.sbuf_tensor("x1b0", [P, D], F32, side="right"))
    x1b1 = _sc5.enter_context(nc.sbuf_tensor("x1b1", [P, D], F32, side="right"))
    gaB = _sc5.enter_context(nc.sbuf_tensor("gaB", [P, D], BF16, side="right"))
    garow6 = _sc5.enter_context(nc.sbuf_tensor("garow6", [1, D], BF16, side="right"))
    junk6b = _sc5.enter_context(nc.sbuf_tensor("junk6b", [P, D], BF16, side="right"))
    rrow = _sc5.enter_context(nc.sbuf_tensor("rrow", [1, D], F32, side="right"))
    ps_a0 = _sc5.enter_context(nc.psum_tensor("ps_a0", [P, 512], F32))
    ps_a1 = _sc5.enter_context(nc.psum_tensor("ps_a1", [P, 512], F32))
    ps_a2 = _sc5.enter_context(nc.psum_tensor("ps_a2", [P, 512], F32))
    ps_a3 = _sc5.enter_context(nc.psum_tensor("ps_a3", [P, 512], F32))
    ps_gb = _sc5.enter_context(nc.psum_tensor("ps_gb", [P, 512], F32))
    ps_b0 = _sc5.enter_context(nc.psum_tensor("ps_b0", [P, 512], F32))
    ps_b1 = _sc5.enter_context(nc.psum_tensor("ps_b1", [P, 512], F32))
    ps_b2 = _sc5.enter_context(nc.psum_tensor("ps_b2", [P, 512], F32))
    if True:

        ps_a = (ps_a0, ps_a1, ps_a2, ps_a3)
        garc = dma("sync", garow6[0:1, :], g_hbm[0], dC)
        sqv = 0
        smm = 0
        for ft in range(DT):
            sqv = pr.op("vector", lambda e, ft=ft: e.tensor_tensor(
                out=junk6b[:, :], in0=oT[:, ft * D:(ft + 1) * D],
                in1=oT[:, ft * D:(ft + 1) * D], op=ALU.mult),
                waits=[(sT, smm)], inc=sV)
            for mc in range(4):
                smm = pr.op("tensor", lambda e, ft=ft, mc=mc: e.matmul(
                    ps_a[mc][0:1, :], onescol_b[:, 0:1],
                    junk6b[:, mc * 512:(mc + 1) * 512],
                    start=(ft == 0), stop=(ft == DT - 1)),
                    waits=[(sV, sqv)] if mc == 0 else [],
                    inc=(sT if mc == 3 else None))
        rrow_ev = 0
        rrowv = [0, 0, 0, 0]
        for mc in range(4):
            rrow_ev = pr.op("vector", lambda e, mc=mc: e.tensor_copy(
                out=rrow[0:1, mc * 512:(mc + 1) * 512], in_=ps_a[mc][0:1, :]),
                waits=[(sT, smm)], inc=sV)
            rrowv[mc] = rrow_ev
        rl = pr.op("scalar", lambda e: e.activation(
            out=rrow[0:1, :], in_=rrow[0:1, :], func=AF.Ln, bias=RMSEPS,
            scale=1.0 / D), waits=[(sV, rrow_ev)], inc=sS)
        rx = pr.op("scalar", lambda e: e.activation(
            out=rrow[0:1, :], in_=rrow[0:1, :], func=AF.Exp, scale=-0.5),
            waits=[(sS, rl)], inc=sS)
        rcm = 0
        for t in range(TT):
            rcm = pr.op("tensor", lambda e, t=t: e.matmul(
                ps_gb[:, t:t + 1], rrow[0:1, t * P:(t + 1) * P], ones_f[0:1, 0:1],
                start=True, stop=True),
                waits=[(sS, rx)], inc=(sT if t == TT - 1 else None))
        rms_done = pr.op("vector", lambda e: e.tensor_copy(
            out=rmsr[:, :], in_=ps_gb[:, 0:TT]), waits=[(sT, rcm)], inc=sV)
        rms_sem = sV

        gaB_done = 0
        gev = rms_done
        for i in range(4):
            td = pr.op("tensor", lambda e, i=i: e.matmul(
                ps_gb[:, :], ones_b[0:1, :], garow6[0:1, i * 512:(i + 1) * 512],
                start=True, stop=True), waits=[(sV, gev), (dC, garc)], inc=sT)
            gev = pr.op("vector", lambda e, i=i: e.tensor_copy(
                out=gaB[:, i * 512:(i + 1) * 512], in_=ps_gb[:, :]),
                waits=[(sT, td)], inc=sV)
        gaB_done = gev

        wsem = (dA, dB)
        wob = (wob0, wob1)
        wo_cast = [(sS, 0)] * DT
        ccount = [(sS, 0), (sS, 0)]
        for ft in range(DT):
            par = ft % 2
            w = [ccount[par]] if ft >= 2 else []
            cnt = dma("sync", wob[par][:, :], wo_d[ft], wsem[par], waits=w)
            if ft % 2 == 0:
                ccount[par] = (sS, pr.op("scalar", lambda e, ft=ft, par=par:
                                         e.activation(
                                             out=wo_bf[:, ft * D:(ft + 1) * D],
                                             in_=wob[par][:, :], func=AF.Identity,
                                             scale=nwcol[:, ft:ft + 1]),
                                         waits=[(wsem[par], cnt)], inc=sS))
            else:
                ccount[par] = (sV, pr.op("vector", lambda e, ft=ft, par=par:
                                         e.tensor_scalar(
                                             out=wo_bf[:, ft * D:(ft + 1) * D],
                                             in0=wob[par][:, :],
                                             scalar1=nwcol[:, ft:ft + 1],
                                             scalar2=None, op0=ALU.mult),
                                         waits=[(wsem[par], cnt)], inc=sV))
            wo_cast[ft] = ccount[par]

        xsem6 = (dC, dE)
        x1sem = (dF, dG)
        xcnt = [0, 0]
        x1w = [0, 0]
        vread = [0, 0]
        evs = [0, 0, 0, 0]
        ps_sets6 = ((ps_a0, ps_a1, ps_a2, ps_a3), (ps_b0, ps_b1, ps_b2, ps_gb))
        stt_done = [[0, 0, 0, 0], [0, 0, 0, gaB_done]]
        sst = 0
        for t in range(TT):
            par = t % 2
            pa = ps_sets6[par]
            xrc = xrc0 if par == 0 else xrc1
            x1buf = x1b0 if par == 0 else x1b1
            w = [(sV, vread[par])] if t >= 2 else [(dC, garc)]
            xcnt[par] = dma("sync", xrc[:, :], x_d[t], xsem6[par], waits=w)
            mmd = None
            for ft in range(DT):
                for mc in range(4):
                    mmd = pr.op("tensor", lambda e, t=t, mc=mc, ft=ft, pa=pa: e.matmul(
                        pa[mc][:, :], oT[:, ft * D + t * P:ft * D + (t + 1) * P],
                        wo_bf[:, ft * D + mc * 512: ft * D + (mc + 1) * 512],
                        start=(ft == 0), stop=(ft == DT - 1)),
                        waits=(([wo_cast[ft]] if t == 0 and mc == 0 else []) +
                               ([(sV, stt_done[par][mc])] if ft == 0 else []) +
                               ([(sV, rrowv[mc])] if t == 0 and ft == 0 else [])),
                        inc=(sT if (ft == DT - 1 and mc == 3) else None))
            for mc in range(4):
                sttd = pr.op("vector", lambda e, t=t, mc=mc, x1buf=x1buf, pa=pa:
                             e.scalar_tensor_tensor(
                                 out=x1buf[:, mc * 512:(mc + 1) * 512],
                                 in0=pa[mc][:, :],
                                 scalar=rmsr[:, t:t + 1],
                                 in1=gaB[:, mc * 512:(mc + 1) * 512],
                                 op0=ALU.mult, op1=ALU.mult),
                             waits=[(sT, mmd), (sV, rms_done), (sV, gaB_done),
                                    (x1sem[par], x1w[par])], inc=sV)
                stt_done[par][mc] = sttd
                evs[mc] = pr.op("vector", lambda e, mc=mc, x1buf=x1buf, xrc=xrc:
                                e.tensor_tensor(
                                    out=x1buf[:, mc * 512:(mc + 1) * 512],
                                    in0=x1buf[:, mc * 512:(mc + 1) * 512],
                                    in1=xrc[:, mc * 512:(mc + 1) * 512],
                                    op=ALU.add),
                                waits=[(xsem6[par], xcnt[par]), (sV, sttd)], inc=sV)
            vread[par] = evs[3]
            vst = pr.op("vector", lambda e, t=t, x1buf=x1buf: e.tensor_reduce(
                out=msum[:, t:t + 1], in_=x1buf[:, :], axis=AX.X, op=ALU.add),
                waits=[(sV, evs[3])], inc=sV)
            sst = pr.op("scalar", lambda e, t=t, x1buf=x1buf: e.activation(
                out=junk6b[:, :], in_=x1buf[:, :], func=AF.Square,
                accum_out=sqsum[:, t:t + 1]),
                waits=[(sV, evs[3]), (sS, sst if t else 0)], inc=sS)
            x1w[par] = dma("gpsimd", x1_hbm[t], x1buf[:, :], x1sem[par],
                           waits=[(sV, vst), (sS, sst)])

        mean2_done = pr.op("vector", lambda e: e.tensor_scalar(
            out=mean2[:, :], in0=msum[:, :], scalar1=1.0 / D, scalar2=None,
            op0=ALU.mult), waits=[(sV, sV.n), (sS, sS.n)], inc=sV)
        c1b = pr.op("vector", lambda e: e.tensor_scalar(
            out=tmp1[:, 0:TT], in0=sqsum[:, :], scalar1=1.0 / D, scalar2=None,
            op0=ALU.mult), waits=[(sV, mean2_done)], inc=sV)
        c2b = pr.op("vector", lambda e: e.tensor_tensor(
            out=tmp1[:, 16:32], in0=mean2[:, :], in1=mean2[:, :], op=ALU.mult),
            waits=[(sV, c1b)], inc=sV)
        vst2 = pr.op("vector", lambda e: e.tensor_tensor(
            out=tmp1[:, 0:TT], in0=tmp1[:, 0:TT], in1=tmp1[:, 16:32],
            op=ALU.subtract), waits=[(sV, c2b)], inc=sV)
        ln2d = pr.op("scalar", lambda e: e.activation(
            out=tmp1[:, 0:TT], in_=tmp1[:, 0:TT], func=AF.Ln, bias=LNEPS, scale=1.0),
            waits=[(sV, vst2)], inc=sS)
        rstd2_done = pr.op("scalar", lambda e: e.activation(
            out=rstd2[:, :], in_=tmp1[:, 0:TT], func=AF.Exp, scale=-0.5),
            waits=[(sS, ln2d)], inc=sS)
        BARRIER()
        if debug:
            dbg["x1"] = nc.dram_tensor("dbg_x1", [TT, P, D], F32,
                                       kind="ExternalOutput").ap()
            for t in range(TT):
                dma("gpsimd", dbg["x1"][t], x1_hbm[t], dOut)
            BARRIER()
        pr.flush()
    _sc5.close()

    big1.close()

    big2 = ExitStack()
    h2T = big2.enter_context(nc.sbuf_tensor("h2T8", [P, DT * D], F8, side="left"))
    act2 = big2.enter_context(nc.sbuf_tensor("act2", [P, 2 * NS * QW], F8, side="left"))
    wl2 = big2.enter_context(nc.sbuf_tensor("wl2", [P, 2 * NS * QW], F8, side="left"))
    gbB = big2.enter_context(nc.sbuf_tensor("gbB", [P, D], BF16, side="left"))
    gbrow9 = big2.enter_context(nc.sbuf_tensor("gbrow9", [1, D], BF16, side="left"))

    _sc6 = ExitStack()
    x1r0 = _sc6.enter_context(nc.sbuf_tensor("x1r0", [P, D], F32, side="right"))
    x1r1 = _sc6.enter_context(nc.sbuf_tensor("x1r1", [P, D], F32, side="right"))
    lnxb20 = _sc6.enter_context(nc.sbuf_tensor("lnxb20", [P, D], BF16, side="right"))
    lnxb21 = _sc6.enter_context(nc.sbuf_tensor("lnxb21", [P, D], BF16, side="right"))
    ps_u0 = _sc6.enter_context(nc.psum_tensor("ps_u0", [P, 1024], BF16))
    ps_u1 = _sc6.enter_context(nc.psum_tensor("ps_u1", [P, 1024], BF16))
    ps_u2 = _sc6.enter_context(nc.psum_tensor("ps_u2", [P, 1024], BF16))
    ps_u3 = _sc6.enter_context(nc.psum_tensor("ps_u3", [P, 1024], BF16))
    if True:

        x1r = (x1r0, x1r1)
        xsem7 = (dA, dB)
        psux = (ps_u0, ps_u1, ps_u2, ps_u3)
        vdone = [0, 0]
        tile_tr_done = [0, 0]
        ev_done = [(sS, 0)] * 4
        for t in range(TT):
            par = t % 2
            lnxb2 = lnxb20 if par == 0 else lnxb21
            w = [(sV, vdone[par])] if t >= 2 else []
            cnt = dma("sync", x1r[par][:, :], x1_hbm[t], xsem7[par], waits=w)
            vh = [0, 0]
            for half in range(2):
                vh[half] = pr.op("vector", lambda e, par=par, t=t, lnxb2=lnxb2,
                                 half=half: e.tensor_scalar(
                                     out=lnxb2[:, half * (D // 2):
                                               (half + 1) * (D // 2)],
                                     in0=x1r[par][:, half * (D // 2):
                                                  (half + 1) * (D // 2)],
                                     scalar1=mean2[:, t:t + 1],
                                     scalar2=rstd2[:, t:t + 1],
                                     op0=ALU.subtract, op1=ALU.mult),
                                 waits=[(xsem7[par], cnt), (sS, rstd2_done),
                                        (sV, mean2_done),
                                        (sT, tile_tr_done[par])], inc=sV)
            vdone[par] = vh[1]
            for dt_ in range(DT):
                ppar = dt_ % 4
                ptr = psux[ppar]
                td = pr.op("tensor", lambda e, dt_=dt_, ptr=ptr, lnxb2=lnxb2: e.transpose(
                    ptr[:, 0:P], lnxb2[:, dt_ * P:(dt_ + 1) * P], eye_bf[:, :]),
                    waits=[(sV, vh[0] if dt_ < DT // 2 else vh[1]),
                           ev_done[ppar]], inc=sT)
                if dt_ % 2 == 0:
                    ev_done[ppar] = (sS, pr.op(
                        "scalar", lambda e, dt_=dt_, t=t, ptr=ptr: e.activation(
                            out=h2T[:, dt_ * D + t * P: dt_ * D + (t + 1) * P],
                            in_=ptr[:, 0:P], func=AF.Identity,
                            scale=cbcol[:, dt_:dt_ + 1],
                            bias=sbcol[:, dt_:dt_ + 1]),
                        waits=[(sT, td)], inc=sS))
                else:
                    ev_done[ppar] = (sV, pr.op(
                        "vector", lambda e, dt_=dt_, t=t, ptr=ptr: e.tensor_scalar(
                            out=h2T[:, dt_ * D + t * P: dt_ * D + (t + 1) * P],
                            in0=ptr[:, 0:P],
                            scalar1=cbcol[:, dt_:dt_ + 1],
                            scalar2=sbcol[:, dt_:dt_ + 1],
                            op0=ALU.mult, op1=ALU.add),
                        waits=[(sT, td)], inc=sV))
            tile_tr_done[par] = sT.n
        BARRIER()
        if debug:
            dbg["h2T"] = nc.dram_tensor("dbg_h2T", [P, DT * D], BF16,
                                        kind="ExternalOutput").ap()
            dma("gpsimd", dbg["h2T"], h2T[:, :], dOut)
            BARRIER()
        pr.flush()
    _sc6.close()

    _sc7 = ExitStack()
    wg0f8 = [_sc7.enter_context(nc.sbuf_tensor(f"wg0f8{i}", [P, DT * E], F8, side="right"))
             for i in range(2)]
    wg1f8 = [_sc7.enter_context(nc.sbuf_tensor(f"wg1f8{i}", [P, DT * E], F8, side="right"))
             for i in range(2)]
    gelbs = [_sc7.enter_context(nc.sbuf_tensor(f"gelb{i}", [P, 512], BF16, side="right"))
             for i in range(4)]
    actbs = [_sc7.enter_context(nc.sbuf_tensor(f"actb{i}", [P, 2 * QW], F8, side="right"))
             for i in range(4)]
    ps_gs = [_sc7.enter_context(nc.psum_tensor(f"ps_g{i}", [P, 512], F32))
             for i in range(4)]
    ps_fs = [_sc7.enter_context(nc.psum_tensor(f"ps_f{i}", [P, 512], F32))
             for i in range(4)]
    if True:
        wcsem = (dJ, dK)
        actsem = (dF, dG, dH, dI)
        gmm = 0
        fmm = 0
        gel_done = [0, 0, 0, 0]
        act_done = [0, 0, 0, 0]
        actw = [0, 0, 0, 0]
        g0cnt = [0, 0]
        g1cnt = [0, 0]
        NDP = DT // 2
        wlcast_q = [[None] * 8 for _ in range(NQ)]
        preload_acnts = [[None] * 4 for _ in range(2)]
        ACT_CSEM = (dC, dE, dN, dO)
        ACT_CQ = ("sync", "sync", "scalar", "scalar")

        def act_chunk_dma(bp, ats, ck, extra_waits):
            sem = ACT_CSEM[ck]
            cnt = dma(ACT_CQ[ck],
                      act2[:, bp * NS * QW + ck * 8 * 2 * QW:
                           bp * NS * QW + (ck + 1) * 8 * 2 * QW].rearrange(
                          "p (s c) -> p s c", s=8),
                      act_hbm[ats][ck * 8:(ck + 1) * 8].rearrange("s p c -> p s c"),
                      sem, waits=[(sem, sem.n)] + list(extra_waits))
            return (sem, cnt)

        def wl_group_dma(q, g, extra_waits):
            sem = dP if g % 2 == 0 else dQ
            cnt = dma("gpsimd",
                      wl2[:, (q % 2) * NS * QW + g * 8 * QW:
                           (q % 2) * NS * QW + (g + 1) * 8 * QW].rearrange(
                          "p (s c) -> p s c", s=8),
                      wl_d[q][g * 8:(g + 1) * 8].rearrange("s p c -> p s c"),
                      sem, waits=[(sem, sem.n)] + list(extra_waits))
            wlcast_q[q][g] = (sem, cnt)

        def wl_half_dma(q, g, half, extra_waits):
            sem = dP if g % 2 == 0 else dQ
            s0 = g * 8 + half * 4
            cnt = dma("gpsimd",
                      wl2[:, (q % 2) * NS * QW + s0 * QW:
                           (q % 2) * NS * QW + (s0 + 4) * QW].rearrange(
                          "p (s c) -> p s c", s=4),
                      wl_d[q][s0:s0 + 4].rearrange("s p c -> p s c"),
                      sem, waits=[(sem, sem.n)] + list(extra_waits))
            wlcast_q[q][g] = (sem, cnt)

        def h2pair(dtp, tc):
            return h2T[:, :].rearrange("p (dt c) -> p dt c", dt=DT)[
                :, 2 * dtp:2 * dtp + 2, tc * 512:(tc + 1) * 512]

        def wgpair(buf, dtp):
            return buf[:, :].rearrange("p (dt e) -> p dt e", dt=DT)[
                :, 2 * dtp:2 * dtp + 2, :]

        gbrc = dma("sync", gbrow9[0:1, :], g_hbm[1], dC, waits=[(dC, dC.n)])
        gbB_done = 0
        for i in range(4):
            td = pr.op("tensor", lambda e, i=i: e.matmul(
                ps_gs[i][:, :], ones_b[0:1, :],
                gbrow9[0:1, i * 512:(i + 1) * 512],
                start=True, stop=True), waits=[(dC, gbrc)], inc=sT)
            gbB_done = pr.op("vector", lambda e, i=i: e.tensor_copy(
                out=gbB[:, i * 512:(i + 1) * 512], in_=ps_gs[i][:, :]),
                waits=[(sT, td)], inc=sV)

        g0cnt[0] = dma("gpsimd", wg0f8[0][:, :].rearrange("p (dt e) -> p dt e", dt=DT),
                       wg_d[0][0].rearrange("dt p e -> p dt e"), dJ)
        g1cnt[0] = dma("gpsimd", wg1f8[0][:, :].rearrange("p (dt e) -> p dt e", dt=DT),
                       wg_d[0][1].rearrange("dt p e -> p dt e"), dK,
                       waits=[(dJ, g0cnt[0])])
        for s in range(NS):
            par = s % 2
            if s % 2 == 1:
                idx = (s - 1) // 2
                wl_half_dma(idx // 16, (idx % 16) // 2, idx % 2, [])
            if s + 1 < NS:
                g0cnt[1 - par] = dma("gpsimd",
                                     wg0f8[1 - par][:, :].rearrange(
                                         "p (dt e) -> p dt e", dt=DT),
                                     wg_d[s + 1][0].rearrange("dt p e -> p dt e"),
                                     dJ, waits=[(dJ, dJ.n), (sT, gmm)])
            for dtp in range(NDP):
                for tc in range(4):
                    gmm = pr.op("tensor", lambda e, tc=tc, dtp=dtp, par=par:
                                e.matmul(ps_gs[tc][:, :], wgpair(wg0f8[par], dtp),
                                         h2pair(dtp, tc),
                                         start=(dtp == 0), stop=(dtp == NDP - 1),
                                         perf_mode=mybir.MatmulPerfMode.DoubleRow),
                                waits=(([(wcsem[0], g0cnt[par])] +
                                        [(sS, g) for g in gel_done] +
                                        ([(sV, gbB_done)] if s == 0 else []))
                                       if (dtp == 0 and tc == 0) else []),
                                inc=(sT if (dtp == NDP - 1 and tc == 3) else None))
            for tc in range(4):
                gel_done[tc] = pr.op("scalar", lambda e, tc=tc: e.activation(
                    out=gelbs[tc][:, :], in_=ps_gs[tc][:, :],
                    func=AF.Gelu_apprx_tanh),
                    waits=[(sT, gmm), (sV, act_done[tc])], inc=sS)
            if s + 1 < NS:
                g1cnt[1 - par] = dma("gpsimd",
                                     wg1f8[1 - par][:, :].rearrange(
                                         "p (dt e) -> p dt e", dt=DT),
                                     wg_d[s + 1][1].rearrange("dt p e -> p dt e"),
                                     dK, waits=[(dK, dK.n), (sT, fmm)])
            for dtp in range(NDP):
                for tc in range(4):
                    fmm = pr.op("tensor", lambda e, tc=tc, dtp=dtp, par=par:
                                e.matmul(ps_fs[tc][:, :], wgpair(wg1f8[par], dtp),
                                         h2pair(dtp, tc),
                                         start=(dtp == 0), stop=(dtp == NDP - 1),
                                         perf_mode=mybir.MatmulPerfMode.DoubleRow),
                                waits=(([(wcsem[1], g1cnt[par])] +
                                        [(sV, a) for a in act_done])
                                       if (dtp == 0 and tc == 0) else []),
                                inc=(sT if (dtp == NDP - 1 and tc == 3) else None))
            half = s % 2
            for tc in range(4):
                act_done[tc] = pr.op("vector", lambda e, tc=tc, half=half:
                                     e.tensor_tensor(
                                         out=actbs[tc][:, :].rearrange(
                                             "p (tt two c) -> p tt two c",
                                             tt=4, two=2)[:, :, half, :],
                                         in0=gelbs[tc][:, :].rearrange(
                                             "p (tt c) -> p tt c", tt=4),
                                         in1=ps_fs[tc][:, :].rearrange(
                                             "p (tt c) -> p tt c", tt=4),
                                         op=ALU.mult),
                                     waits=[(sS, gel_done[tc]), (sT, fmm)] +
                                           ([(actsem[tc], actw[tc])] if half == 0
                                            else []), inc=sV)
                if half == 1:
                    actw[tc] = dma("gpsimd", act_hbm[tc][s // 2], actbs[tc][:, :],
                                   actsem[tc], waits=[(sV, act_done[tc])])
            if half == 1 and (s // 2 + 1) % 8 == 0:
                ck = (s // 2 + 1) // 8 - 1
                for ts_ in (0, 1):
                    preload_acnts[ts_][ck] = act_chunk_dma(
                        ts_, ts_, ck, [(actsem[ts_], actw[ts_])])
        BARRIER()
        pr.flush()
    _sc7.close()

    _sc8 = ExitStack()
    x1q0 = _sc8.enter_context(nc.sbuf_tensor("x1q0", [P, QW], F32, side="right"))
    x1q1 = _sc8.enter_context(nc.sbuf_tensor("x1q1", [P, QW], F32, side="right"))
    outq0 = _sc8.enter_context(nc.sbuf_tensor("outq0", [P, QW], F32, side="right"))
    outq1 = _sc8.enter_context(nc.sbuf_tensor("outq1", [P, QW], F32, side="right"))
    ps_y0 = _sc8.enter_context(nc.psum_tensor("ps_y0", [P, 512], F32))
    ps_y1 = _sc8.enter_context(nc.psum_tensor("ps_y1", [P, 512], F32))
    ps_y2 = _sc8.enter_context(nc.psum_tensor("ps_y2", [P, 512], F32))
    ps_y3 = _sc8.enter_context(nc.psum_tensor("ps_y3", [P, 512], F32))
    ps_y4 = _sc8.enter_context(nc.psum_tensor("ps_y4", [P, 512], F32))
    ps_y5 = _sc8.enter_context(nc.psum_tensor("ps_y5", [P, 512], F32))
    ps_y6 = _sc8.enter_context(nc.psum_tensor("ps_y6", [P, 512], F32))
    ps_y7 = _sc8.enter_context(nc.psum_tensor("ps_y7", [P, 512], F32))
    if True:
        ps_y_sets = ((ps_y0, ps_y1, ps_y2, ps_y3), (ps_y4, ps_y5, ps_y6, ps_y7))
        x1qsem = (dL, dM)
        outsem = (dOut, dOut2)
        evq = [0, 0]
        vq_done = [[0, 0, 0, 0], [0, 0, 0, 0]]
        x1cnt = [0, 0]
        outw = [0, 0]
        gts = 0
        SPC = NS // 4
        NSP = NS // 2
        NGTS = NQ * TSUP
        mm_end_of = [0] * NGTS
        q_end_mm = [0] * NQ
        buf_acnts = [preload_acnts[0], preload_acnts[1]]

        def acpair(bp, sp, tt_):
            base = bp * NS * QW + sp * 2 * QW + tt_ * 256
            return act2[:, base:base + 256].rearrange("p (two c) -> p two c", two=2)

        def wlpair(qp, sp):
            return wl2[:, qp * NS * QW:(qp + 1) * NS * QW].rearrange(
                "p (s c) -> p s c", s=NS)[:, 2 * sp:2 * sp + 2, :]

        for q in range(NQ):
            if 1 <= q < NQ - 1:
                for g in range(8):
                    wl_group_dma(q + 1, g, [(sT, q_end_mm[q - 1])])
            for ts in range(TSUP):
                bp = gts % 2
                ps_y = ps_y_sets[bp]
                if gts >= 1 and gts + 1 < NGTS:
                    nb = (gts + 1) % 2
                    for ck in range(4):
                        buf_acnts[nb][ck] = act_chunk_dma(
                            nb, (gts + 1) % TSUP, ck, [(sT, mm_end_of[gts - 1])])
                for tt_ in range(2):
                    t = ts * 4 + tt_
                    tp = tt_ % 2
                    w = [(sV, evq[tp])]
                    if q == 0 and ts == 0:
                        w += [(dF, dF.n), (dG, dG.n)]
                    x1cnt[tp] = dma("sync",
                                    (x1q0 if tp == 0 else x1q1)[:, :],
                                    x1_hbm[t][:, q * QW:(q + 1) * QW],
                                    x1qsem[tp], waits=w)
                mmd = 0
                for sp in range(NSP):
                    ck = (2 * sp) // SPC
                    for tt_ in range(4):
                        w = []
                        if (2 * sp) % SPC == 0 and tt_ == 0:
                            w += [buf_acnts[bp][ck]]
                        if (2 * sp) % 8 == 0 and tt_ == 0 and ts == 0:
                            w += [wlcast_q[q][(2 * sp) // 8]]
                        if sp == 0:
                            w += [(sV, vq_done[bp][tt_])]
                        do_inc = (tt_ == 3 and (sp % 4 == 3))
                        mmd = pr.op("tensor",
                                    lambda e, bp=bp, qp=q % 2, sp=sp, tt_=tt_,
                                    py=ps_y[tt_]:
                                    e.matmul(py[:, :], acpair(bp, sp, tt_),
                                             wlpair(qp, sp),
                                             start=(sp == 0), stop=(sp == NSP - 1),
                                             perf_mode=mybir.MatmulPerfMode.DoubleRow),
                                    waits=w,
                                    inc=(sT if do_inc else None))
                mm_end_of[gts] = mmd
                for tt_ in range(4):
                    t = ts * 4 + tt_
                    tp = tt_ % 2
                    x1q = x1q0 if tp == 0 else x1q1
                    outq = outq0 if tp == 0 else outq1
                    vq = pr.op("vector", lambda e, q=q, outq=outq, py=ps_y[tt_]:
                               e.tensor_tensor(
                                   out=outq[:, :], in0=py[:, :],
                                   in1=gbB[:, q * QW:(q + 1) * QW], op=ALU.mult),
                               waits=[(sT, mmd), (sV, gbB_done),
                                      (x1qsem[tp], x1cnt[tp]),
                                      (outsem[tp], outw[tp])], inc=sV)
                    vq_done[bp][tt_] = vq
                    evq[tp] = pr.op("vector", lambda e, outq=outq, x1q=x1q:
                                    e.tensor_tensor(out=outq[:, :], in0=outq[:, :],
                                                    in1=x1q[:, :], op=ALU.add),
                                    waits=[(sV, vq)], inc=sV)
                    outw[tp] = dma("scalar", y_d[t][:, q * QW:(q + 1) * QW],
                                   outq[:, :], outsem[tp], waits=[(sV, evq[tp])])
                    if tt_ == 1:
                        for tt2 in (2, 3):
                            t2 = ts * 4 + tt2
                            tp2 = tt2 % 2
                            x1cnt[tp2] = dma("sync",
                                             (x1q0 if tp2 == 0 else x1q1)[:, :],
                                             x1_hbm[t2][:, q * QW:(q + 1) * QW],
                                             x1qsem[tp2], waits=[(sV, evq[tp2])])
                gts += 1
            q_end_mm[q] = mmd
        pr.op("gpsimd", lambda e: e.engine_nop(),
              waits=[(dOut, dOut.n), (dOut2, dOut2.n)])
        pr.flush()
    _sc8.close()

    big2.close()
    es.close()
    return nc, dbg


_BUILT = None


def _host_tiles(inputs):
    f32 = np.float32
    Wq = np.asarray(inputs["L0_attn_Wq"], f32)
    Wk = np.asarray(inputs["L0_attn_Wk"], f32)
    Wv = np.asarray(inputs["L0_attn_Wv"], f32)
    wqkv = np.ascontiguousarray(np.stack([
        Wq.reshape(DT, P, H, E).transpose(2, 0, 1, 3),
        Wk.reshape(DT, P, H, E).transpose(2, 0, 1, 3),
        Wv.reshape(DT, P, H, E).transpose(2, 0, 1, 3),
    ], axis=1))
    wmod = np.ascontiguousarray(
        np.asarray(inputs["L0_wmod"], f32).reshape(DT, P, NMS, MSW).transpose(2, 0, 1, 3))
    ccol = np.ascontiguousarray(
        np.asarray(inputs["L0_c"], f32).reshape(8, DT, P).transpose(0, 2, 1))
    wf = np.ascontiguousarray(np.asarray(inputs["L0_attn_Wf"], f32).reshape(DT, P, H))
    wo = np.ascontiguousarray(np.asarray(inputs["L0_attn_Wo"], f32).reshape(DT, P, D))
    nwcol_ = np.ascontiguousarray(
        np.asarray(inputs["L0_attn_norm_w"], f32).reshape(DT, P).T)
    wg_in = np.asarray(inputs["L0_ffn_w_gating"], f32)
    wg = np.ascontiguousarray(
        np.stack([wg_in[0].reshape(DT, P, NS, E).transpose(2, 0, 1, 3),
                  wg_in[1].reshape(DT, P, NS, E).transpose(2, 0, 1, 3)],
                 axis=1))
    wl = np.ascontiguousarray(
        np.asarray(inputs["L0_ffn_w_linear"], f32).reshape(NS, P, NQ, QW).transpose(2, 0, 1, 3))
    return wqkv, wmod, ccol, wf, wo, nwcol_, wg, wl


def kernel(**inputs):
    global _BUILT
    if _BUILT is None:
        _BUILT = build(debug=False)
    nc, _ = _BUILT
    wqkv, wmod, ccol, wf, wo, nwcol_, wg, wl = _host_tiles(inputs)
    x = np.ascontiguousarray(np.asarray(inputs["L0_attn_x"], np.float32))
    in_maps = []
    for b in range(8):
        in_maps.append({
            "x": x[b].reshape(TT, P, D),
            "wmod": wmod, "ccol": np.ascontiguousarray(ccol[b]),
            "wqkv": wqkv, "wf": wf, "wo": wo, "nwcol": nwcol_,
            "wg": wg, "wl": wl,
        })
    res = run_bass_kernel_spmd(nc, in_maps, core_ids=list(range(8)))
    out = np.stack([r["y"].reshape(D, D) for r in res.results])
    return out
